# version 20
# baseline (speedup 1.0000x reference)
import math
from contextlib import ExitStack

import numpy as np
import ml_dtypes

import concourse.bass as bass
import concourse.mybir as mybir
from concourse.bass import IndirectOffsetOnAxis
from concourse.bass_utils import run_bass_kernel_spmd

F32 = mybir.dt.float32
BF16 = mybir.dt.bfloat16
U32 = mybir.dt.uint32
AF = mybir.ActivationFunctionType
ALU = mybir.AluOpType
AX = mybir.AxisListType

LAMBDA_INIT = 0.8 - 0.6 * math.exp(-0.3 * 1)


class Cfg:
    def __init__(self, cores=8, T=4096, HID=4096, NB=4, CAP=2.0):
        self.cores = cores
        self.T = T
        self.D = 128 * cores
        self.HID = HID
        self.NB = NB
        self.NE = cores
        self.K = int(CAP * T / self.NE)
        D = self.D
        self.DC = D // 128
        self.TC = T // 128
        self.QB = 256
        self.NQB = T // self.QB
        self.KCN = T // 128
        self.HC = HID // 128
        self.KC = self.K // 128
        self.SB = min(128, self.K)
        self.NSB = self.K // self.SB
        self.OUTR = T // cores
        self.TB = 256
        self.NTB = T // self.TB
        self.PO_F = min(512, D)
        self.POT = D // self.PO_F


FULL = Cfg()


class Eng:

    def __init__(self, nc, ctx, name, dma=False, serialize=True):
        self.sem = ctx.enter_context(nc.semaphore(name))
        self.count = 0
        self.inc = 16 if dma else 1
        self.serialize = serialize

    def mark(self):
        return self.count

    def __call__(self, eng, emit, waits=(), inc=True, noser=False):
        merged = {}

        def add(other, cnt):
            if cnt and cnt > 0:
                k = id(other.sem)
                prev = merged.get(k)
                if prev is None or cnt > prev[1]:
                    merged[k] = (other, cnt)

        for item in waits:
            if isinstance(item, list):
                for (o, c) in item:
                    add(o, c)
            else:
                add(*item)
        if self.serialize and not noser and self.count:
            add(self, self.count)
        for (o, c) in merged.values():
            eng.wait_ge(o.sem, c)
        inst = emit()
        if inc:
            inst.then_inc(self.sem, self.inc)
            self.count += self.inc
        return self.count

    def wait(self, eng, count=None):
        eng.wait_ge(self.sem, self.count if count is None else count)


class DmaPool:

    def __init__(self, nc, ctx, name, nlanes=8):
        self.lanes = [Eng(nc, ctx, f"{name}{i}", dma=True, serialize=True)
                      for i in range(nlanes)]
        self.i = 0

    def __call__(self, eng, emit, waits=()):
        lane = self.lanes[self.i]
        self.i = (self.i + 1) % len(self.lanes)
        lane(eng, emit, waits)
        return [(lane, lane.count)]

    def mark(self):
        return [(l, l.count) for l in self.lanes if l.count]

    def wait(self, eng, marks=None):
        for (l, c) in (self.mark() if marks is None else marks):
            if c:
                eng.wait_ge(l.sem, c)


PHASES = ["qkv", "vtr", "attn", "ar", "ln1", "x1bf", "gate", "sm", "bis",
          "mask", "sel", "ex", "mg", "ffn", "rs", "ln2", "all"]


def build_nc(cfg: Cfg, debug=False, stop_after="all", dbg_tensors=(),
             attn_sub="full"):
    T, D, NB, NE = cfg.T, cfg.D, cfg.NB, cfg.NE
    DC, TC, QB, NQB, KCN = cfg.DC, cfg.TC, cfg.QB, cfg.NQB, cfg.KCN
    HID, HC, K, KC, SB, NSB = cfg.HID, cfg.HC, cfg.K, cfg.KC, cfg.SB, cfg.NSB
    OUTR, TB, NTB = cfg.OUTR, cfg.TB, cfg.NTB
    PO_F, POT = cfg.PO_F, cfg.POT
    groups = [list(range(cfg.cores))]
    NITER = 36

    SL = PHASES.index(stop_after)

    def ph(name):
        return SL >= PHASES.index(name)

    nc = bass.Bass("TRN2", target_bir_lowering=False, debug=debug)

    x_d = nc.dram_tensor("x", [NB * T, D], F32, kind="ExternalInput").ap()
    xts_d = nc.dram_tensor("xts", [128, NB * T * (D // 128)], F32,
                           kind="ExternalInput").ap()
    wqkv_d = nc.dram_tensor("wqkv", [D, 384], F32, kind="ExternalInput").ap()
    wproj_d = nc.dram_tensor("wproj", [128, D], F32, kind="ExternalInput").ap()
    gatew_d = nc.dram_tensor("gatew", [D, NE], F32, kind="ExternalInput").ap()
    w1_d = nc.dram_tensor("w1", [128, HC * DC * 128], F32,
                          kind="ExternalInput").ap()
    w2_d = nc.dram_tensor("w2", [HID, D], F32, kind="ExternalInput").ap()
    lam_d = nc.dram_tensor("lamv", [1, 128], F32, kind="ExternalInput").ap()
    rms_d = nc.dram_tensor("rmsrow", [1, 128], F32, kind="ExternalInput").ap()
    out_d = nc.dram_tensor("out", [NB * OUTR, D], F32, kind="ExternalOutput").ap()

    xbf_d = nc.dram_tensor("xbf", [NB * T, D], BF16).ap()
    xtsbf_d = nc.dram_tensor("xtsbf", [128, NB * T * (D // 128)], BF16).ap()
    w1bf_d = nc.dram_tensor("w1bf", [128, HC * DC * 128], BF16).ap()
    ppart_d = nc.dram_tensor("ppart", [NB * T, D], F32).ap()
    psum_d = nc.dram_tensor("psumd", [NB * T, D], F32,
                            addr_space="Shared" if cfg.cores > 4 else "Local").ap()
    x1_ds = [nc.dram_tensor(f"x1tok{b}", [T, D], F32).ap() for b in range(NB)]
    x1bf_d = nc.dram_tensor("x1bf", [NB * T, D], BF16).ap()
    opart_ds = [nc.dram_tensor(f"opart{b}", [T, D], F32).ap() for b in range(NB)]
    ored_d = nc.dram_tensor("ored", [NB * OUTR, D], F32).ap()

    with ExitStack() as ctx:
        sb = lambda n, s, d: ctx.enter_context(nc.sbuf_tensor(n, s, d))
        ps_ = lambda n, s, d: ctx.enter_context(nc.psum_tensor(n, s, d))

        ident_b = sb("ident_b", [128, 128], BF16)
        ident_f = sb("ident_f", [128, 128], F32)
        iota_col = sb("iota_col", [128, 1], F32)
        iota_row = None

        ones8_f = sb("ones8_f", [NE, 128], F32)
        onesrow_f = sb("onesrow_f", [1, 128], F32)
        onescol_f = sb("onescol_f", [128, 1], F32)
        hT_sb = sb("hT_sb", [128, HC * 512], BF16)
        hT_f32 = hT_sb[:, :].bitcast(F32)
        iota_j = hT_f32[:, T:T + K]
        iota_colb = sb("iota_colb", [128, 1], BF16)
        chunkv_b = sb("chunkv_b", [128, TC], BF16)
        iota_e = sb("iota_e", [NE, TC * NE], F32)
        ind_em = sb("ind_em", [NE, TC * NE], F32)
        biasc = sb("biasc", [128, 6], F32)
        scaletile = sb("scaletile", [128, 64], F32)
        neglam = sb("neglam", [128, 1], F32)
        lam_sb = sb("lam_sb", [1, 128], F32)
        lamp = sb("lamp", [1, 64], F32)
        lamsc = sb("lamsc", [1, 8], F32)
        rmsrow_sb = sb("rmsrow_sb", [1, 128], F32)

        wqkv_sb = sb("wqkv_sb", [128, DC * 384], BF16)
        wproj_sb = sb("wproj_sb", [128, D], BF16)
        gatew_sb = sb("gatew_sb", [128, DC * NE], BF16)
        w2_sb = sb("w2_sb", [128, HC * D], BF16)
        w1_buf = sb("w1_buf", [128, 2 * D], BF16)

        xfm_buf = sb("xfm_buf", [128, 2 * DC * TB], BF16)
        qT_sb = sb("qT_sb", [128, T], BF16)
        kT_sb = sb("kT_sb", [128, T], BF16)

        vaug_sb = sb("vaug_sb", [128, TC * 130], BF16)
        qblk_sb = sb("qblk_sb", [128, 4 * QB], BF16)
        probs_buf = sb("probs_buf", [128, 2 * 4 * QB], BF16)
        zcatT_sb = sb("zcatT_sb", [128, QB], BF16)
        zct_sb = sb("zct_sb", [128, 256], BF16)
        fm_ep = sb("fm_ep", [128, 1024], F32)
        ep_f = sb("ep_f", [128, 4 * QB], F32)
        iota_row = ep_f[:, 128:256]
        assert D <= 4 * QB
        x18buf = ep_f[:, 0:D]
        ybuf = sb("ybuf", [128, 2 * D], F32)
        x1buf = sb("x1buf", [128, 2 * D], F32)
        x1fm_buf = sb("x1fm_buf", [128, max(T, 4 * D)], BF16)
        vT_sb = x1fm_buf[:, 0:T]
        hi_b = x1fm_buf[0:8, 0:T]
        tie_b = ybuf[:, :].bitcast(BF16)[0:8, 0:T]
        rank_f = hT_f32[0:8, 0:T]
        gacc = rank_f[0:NE, :]
        xq_sb = [hT_f32[:, T + K:T + K + 1024],
                 hT_f32[:, T + K + 1024:T + K + 2048]]
        gtok = sb("gtok", [128, TC * NE], F32)
        probs_tok = gtok
        ssum = sb("ssum", [128, TC], F32)
        tau_lo = sb("tau_lo", [NE, 1], F32)
        tau_hi = sb("tau_hi", [NE, 1], F32)
        tau_mid = sb("tau_mid", [NE, 1], F32)
        tau_m8 = sb("tau_m8", [NE, TC * NE], F32)
        cmp_tok = sb("cmp_tok", [128, TC * NE], F32)
        part8 = sb("part8", [128, NE], F32)
        cnt8 = sb("cnt8", [NE, 1], F32)
        cond8 = sb("cond8", [NE, 1], F32)
        bs_sc = sb("bs_sc", [NE, 8], F32)
        hi_tok = cmp_tok
        tie_tok = sb("tie_tok", [128, TC * NE], F32)

        fill8 = sb("fill8", [NE, 1], F32)

        gs_tok = sb("gs_tok", [128, 2 * TC], F32)
        pc_b = ep_f[:, 0:K // 2].bitcast(BF16)
        payload = sb("payload", [128, 3], BF16)
        mg_f = sb("mg_f", [3, K], BF16)
        mg3 = sb("mg3", [128, 4], F32)
        mcols = sb("mcols", [128, KC], F32)
        midx = sb("midx", [128, KC], U32)
        gcol = sb("gcol", [128, KC], F32)
        xin_sb = sb("xin_sb", [128, 2 * D], BF16)
        xinT_sb = sb("xinT_sb", [128, DC * 512], BF16)
        fw_sb = probs_buf[:, 0:D]
        xinT_f32 = xinT_sb[:, :].bitcast(F32)
        pstage2 = [xinT_f32[:, 0:1024], xinT_f32[:, 1024:2048]]
        pstage = pstage2[0]
        lnslots = [ybuf[:, 0:D], ybuf[:, D:2 * D]]
        stat_f = sb("stat_f", [128, 16], F32)

        sc0 = ps_("sc0", [128, 4 * QB], F32)
        sc1 = ps_("sc1", [128, 4 * QB], F32)
        accv = ps_("accv", [128, 1024], F32)
        chain = ps_("chain", [128, 1024], F32)
        chA = chain[:, 0:512]
        chB = chain[:, 512:1024]
        chain_bf = chain[:, :].bitcast(BF16)

        def chAb(i):
            return chain_bf[:, i * 128:(i + 1) * 128]

        def chBb(i):
            return chain_bf[:, 1024 + i * 128:1024 + (i + 1) * 128]

        dmas = DmaPool(nc, ctx, "dmas")
        dmag = DmaPool(nc, ctx, "dmag")
        pe = Eng(nc, ctx, "pe", serialize=False)
        act = Eng(nc, ctx, "act")
        dve = Eng(nc, ctx, "dve")
        pool = Eng(nc, ctx, "pool")
        ccx = Eng(nc, ctx, "ccx", serialize=False)
        ln2st = Eng(nc, ctx, "ln2st", dma=True, serialize=False)

        block = ctx.enter_context(nc.Block())

        chfree = {"A": (dve, 0), "B": (dve, 0)}

        def chw(which):
            e, c = chfree[which]
            return [(e, c)] if c > 0 else []

        xconv_m = []
        @block.gpsimd
        def _(g):
            for dc in range(DC):
                dmag(g, lambda dc=dc: g.dma_start(
                    out=wqkv_sb[:, dc * 384:(dc + 1) * 384],
                    in_=wqkv_d[dc * 128:(dc + 1) * 128, :]))
            dmag(g, lambda: g.dma_start(out=wproj_sb[:, :], in_=wproj_d))
            for dc in range(DC):
                dmag(g, lambda dc=dc: g.dma_start(
                    out=gatew_sb[:, dc * NE:(dc + 1) * NE],
                    in_=gatew_d[dc * 128:(dc + 1) * 128, :]))
            for hc in range(HC):
                dmag(g, lambda hc=hc: g.dma_start(
                    out=w2_sb[:, hc * D:(hc + 1) * D],
                    in_=w2_d[hc * 128:(hc + 1) * 128, :]))
            for ci in range(8):
                cw = NB * T * DC // 8
                dmag(g, lambda ci=ci, cw=cw: g.dma_start(
                    out=xtsbf_d[:, ci * cw:(ci + 1) * cw],
                    in_=xts_d[:, ci * cw:(ci + 1) * cw]))
                xconv_m.append(dmag.mark())
            for ci in range(4):
                cw = HC * DC * 128 // 4
                dmag(g, lambda ci=ci, cw=cw: g.dma_start(
                    out=w1bf_d[:, ci * cw:(ci + 1) * cw],
                    in_=w1_d[:, ci * cw:(ci + 1) * cw]))
            pool(g, lambda: g.iota(iota_col[:, :], pattern=[[0, 1]], base=0,
                                   channel_multiplier=1,
                                   allow_small_or_imprecise_dtypes=True))
            pool(g, lambda: g.iota(iota_row[:, :], pattern=[[1, 128]], base=0,
                                   channel_multiplier=0,
                                   allow_small_or_imprecise_dtypes=True))
            pool(g, lambda: g.iota(iota_j[:, :], pattern=[[1, K]], base=0,
                                   channel_multiplier=0,
                                   allow_small_or_imprecise_dtypes=True))
            pool(g, lambda: g.iota(iota_colb[:, :], pattern=[[0, 1]], base=0,
                                   channel_multiplier=1,
                                   allow_small_or_imprecise_dtypes=True))
            pool(g, lambda: g.iota(chunkv_b[:, :], pattern=[[1, TC]], base=0,
                                   channel_multiplier=0,
                                   allow_small_or_imprecise_dtypes=True))
            pool(g, lambda: g.iota(iota_e[:, :], pattern=[[0, TC], [1, NE]],
                                   base=0, channel_multiplier=0,
                                   allow_small_or_imprecise_dtypes=True))


        @block.sync
        def _(s):
            dmas(s, lambda: s.dma_start(out=lam_sb[:, :], in_=lam_d))
            dmas(s, lambda: s.dma_start(out=rmsrow_sb[:, :], in_=rms_d))

        @block.vector
        def _(v):
            pool.wait(v)
            dmas.wait(v)
            dve(v, lambda: v.memset(ones8_f[:, :], 1.0))
            dve(v, lambda: v.memset(onesrow_f[:, :], 1.0))
            dve(v, lambda: v.memset(onescol_f[:, :], 1.0))
            dve(v, lambda: v.memset(biasc[:, :], 0.0))
            dve(v, lambda: v.memset(biasc[:, 0:1], -8.0))
            dve(v, lambda: v.memset(biasc[:, 1:2], -0.5))
            dve(v, lambda: v.memset(biasc[:, 2:3], 1.0 / math.sqrt(32.0)))
            dve(v, lambda: v.memset(biasc[:, 4:5], 1e-6))
            dve(v, lambda: v.memset(biasc[:, 5:6], 1.0 / 64))
            dve(v, lambda: v.memset(qblk_sb[:, :], 0.0))
            va3 = vaug_sb[:, :].rearrange("p (t c) -> p t c", c=130)
            dve(v, lambda: v.memset(va3[:, :, 64:65], 1.0))
            dve(v, lambda: v.memset(va3[:, :, 129:130], 1.0))
            dve(v, lambda: v.tensor_scalar(
                out=ep_f[0:1, 0:64], in0=rmsrow_sb[0:1, 0:64],
                scalar1=(1.0 - LAMBDA_INIT), scalar2=None, op0=ALU.mult))
            dve(v, lambda: v.tensor_scalar(
                out=ident_b[:, :], in0=iota_row[:, :], scalar1=iota_col[:, :1],
                scalar2=None, op0=ALU.is_equal))
            dve(v, lambda: v.tensor_scalar(
                out=ident_f[:, :], in0=iota_row[:, :], scalar1=iota_col[:, :1],
                scalar2=None, op0=ALU.is_equal))
            dve(v, lambda: v.tensor_scalar(
                out=ind_em[:, :], in0=iota_e[:, :], scalar1=iota_col[:NE, :1],
                scalar2=None, op0=ALU.is_equal))
            dve(v, lambda: v.tensor_mul(lamp[:, :], lam_sb[:, 0:64],
                                        lam_sb[:, 64:128]))
            dve(v, lambda: v.reduce_sum(lamsc[0:1, 0:1], lamp[:, 0:32],
                                        axis=AX.X))
            dve(v, lambda: v.reduce_sum(lamsc[0:1, 1:2], lamp[:, 32:64],
                                        axis=AX.X))

        @block.scalar
        def _(s):
            dve.wait(s)
            act(s, lambda: s.activation(out=lamsc[0:1, 2:4], in_=lamsc[0:1, 0:2],
                                        func=AF.Exp))
        lam_act = act.mark()

        @block.vector
        def _(v):
            act.wait(v)
            dve(v, lambda: v.tensor_sub(lamsc[0:1, 4:5], lamsc[0:1, 3:4],
                                        lamsc[0:1, 2:3]))
        lam_dve2 = dve.mark()

        @block.tensor
        def _(t):
            pe(t, lambda: t.matmul(chA[:, 0:1], onesrow_f[0:1, :],
                                   lamsc[0:1, 4:5], start=True, stop=True),
               waits=[(dve, lam_dve2)])
            pe(t, lambda: t.matmul(chB[:, 0:64], onesrow_f[0:1, :],
                                   ep_f[0:1, 0:64], start=True, stop=True))
        bc_mm = pe.mark()

        @block.vector
        def _(v):
            dve(v, lambda: v.tensor_scalar(
                out=neglam[:, :], in0=chA[:, 0:1], scalar1=-LAMBDA_INIT,
                scalar2=None, op0=ALU.add), waits=[(pe, bc_mm)])
            dve(v, lambda: v.tensor_copy(out=scaletile[:, :], in_=chB[:, 0:64]))
        chfree["A"] = (dve, dve.mark())
        chfree["B"] = (dve, dve.mark())
        setup_dmag = dmag.mark()

        def ln2_body(beta, rs_done):
            nln = (OUTR + 127) // 128
            l2stats = [0] * nln
            l2_base = act.count

            def ln2_scale(oc):
                par = oc % 2
                slot = lnslots[par]
                so = par * 8
                r0 = oc * 128

                @block.vector
                def _(v, slot=slot, so=so, oc=oc):
                    dve(v, lambda: v.reciprocal(
                        stat_f[:, so + 5:so + 6], stat_f[:, so + 5:so + 6]),
                        waits=[(act, l2_base + oc + 1)])
                    dve(v, lambda: v.tensor_scalar(
                        out=slot[:, :], in0=slot[:, :],
                        scalar1=stat_f[:, so + 2:so + 3],
                        scalar2=stat_f[:, so + 5:so + 6],
                        op0=ALU.subtract, op1=ALU.mult))
                l2o = dve.mark()

                @block.gpsimd
                def _(g, r0=r0, slot=slot, l2o=l2o):
                    ln2st(g, lambda: g.dma_start(
                        out=out_d[beta * OUTR + r0:beta * OUTR + r0 + 128, :],
                        in_=slot[:, :]), waits=[(dve, l2o)])

            for oc in range(nln):
                par = oc % 2
                slot = lnslots[par]
                so = par * 8

                @block.gpsimd
                def _(g, oc=oc, slot=slot, rs_done=rs_done):
                    w = [(ccx, rs_done)]
                    if oc >= 2:
                        w.append((ln2st, ln2st.count))
                    dmag(g, lambda oc=oc: g.dma_start(
                        out=slot[:, :],
                        in_=ored_d[beta * OUTR + oc * 128:
                                   beta * OUTR + (oc + 1) * 128, :]),
                        waits=w)
                lnl = dmag.mark()

                @block.vector
                def _(v, slot=slot, so=so, lnl=lnl):
                    Yl = slot[:, :]
                    dve(v, lambda: v.tensor_scalar(
                        out=hT_f32[:, 0:1024], in0=Yl, scalar1=1.0,
                        scalar2=None, op0=ALU.mult, op1=ALU.add,
                        accum_out=stat_f[:, so + 0:so + 1]),
                        waits=[lnl])
                    dve(v, lambda: v.scalar_tensor_tensor(
                        out=hT_f32[:, 0:1024], in0=Yl, scalar=1.0, in1=Yl,
                        op0=ALU.mult, op1=ALU.mult,
                        accum_out=stat_f[:, so + 1:so + 2]))
                    dve(v, lambda: v.tensor_scalar(
                        out=stat_f[:, so + 2:so + 3],
                        in0=stat_f[:, so + 0:so + 1],
                        scalar1=1.0 / D, scalar2=None, op0=ALU.mult))
                    dve(v, lambda: v.tensor_mul(
                        stat_f[:, so + 3:so + 4], stat_f[:, so + 2:so + 3],
                        stat_f[:, so + 2:so + 3]))
                    dve(v, lambda: v.scalar_tensor_tensor(
                        out=stat_f[:, so + 4:so + 5],
                        in0=stat_f[:, so + 1:so + 2], scalar=1.0 / D,
                        in1=stat_f[:, so + 3:so + 4],
                        op0=ALU.mult, op1=ALU.subtract))
                    dve(v, lambda: v.tensor_scalar(
                        out=stat_f[:, so + 4:so + 5],
                        in0=stat_f[:, so + 4:so + 5],
                        scalar1=1e-5, scalar2=None, op0=ALU.add))
                l2stats[oc] = dve.mark()
                if oc >= 1:
                    ln2_scale(oc - 1)
            ln2_scale(nln - 1)

            @block.scalar
            def _(s):
                for oc in range(nln):
                    so = (oc % 2) * 8
                    act(s, lambda so=so, oc=oc: s.activation(
                        out=stat_f[:, so + 5:so + 6],
                        in_=stat_f[:, so + 4:so + 5], func=AF.Sqrt),
                        waits=[(dve, l2stats[oc])], noser=True)
            assert act.count == l2_base + nln
            out_st_h[0] = [(ln2st, ln2st.count)]


        out_st_h = [[]]
        pending_ln2 = [None]
        prev_ffn_dma = []
        prev_gate_pe = 0
        prev_x1_dma = []
        at_exp_war = {0: 0, 1: 0, 2: 0, 3: 0}
        at_chain_war = [[]]
        at_trans_m = [0]
        at_evac_prev = [0]
        at_qblk_war = [0]
        prev_ffn_pe = [0]
        prev_gelu_m = [0]
        ffn_acc_last = [0]
        for beta in range(NB):
            xr = beta * T

            qk_evac = dve.mark()
            xfm_war = {0: 0, 1: 0}
            for tb in range(NTB):
                par = tb % 2
                xo = par * DC * TB

                @block.sync
                def _(s, tb=tb, par=par, xo=xo):
                    c0 = (beta * NTB + tb) * DC * TB
                    w = [xconv_m[c0 // (NB * T * DC // 8)]]
                    if xfm_war[par]:
                        w.append((pe, xfm_war[par]))
                    dmas(s, lambda c0=c0: s.dma_start(
                        out=xfm_buf[:, xo:xo + DC * TB],
                        in_=xtsbf_d[:, c0:c0 + DC * TB]), waits=w)
                xfm_ld = dmas.mark()

                @block.tensor
                def _(t, tb=tb, xo=xo):
                    outs = [sc0[:, 0:TB], sc0[:, TB:2 * TB], sc1[:, 0:TB]]
                    w = [xfm_ld,
                         (dve, qk_evac if tb > 0 else ffn_acc_last[0])]
                    if beta == 0 and tb == 0:
                        w.append(setup_dmag)
                    first = True
                    for ci in range(3):
                        for dc in range(DC):
                            pe(t, lambda ci=ci, dc=dc: t.matmul(
                                outs[ci][:, 0:TB],
                                wqkv_sb[:, dc * 384 + ci * 128:
                                        dc * 384 + (ci + 1) * 128],
                                xfm_buf[:, xo + dc * TB:xo + (dc + 1) * TB],
                                start=(dc == 0), stop=(dc == DC - 1)),
                               waits=w if first else ())
                            first = False
                xfm_war[par] = pe.mark()
                qkv_mm = pe.mark()

                @block.vector
                def _(v, tb=tb):
                    dve(v, lambda: v.tensor_copy(
                        out=qT_sb[:, tb * TB:(tb + 1) * TB], in_=sc0[:, 0:TB]),
                        waits=[(pe, qkv_mm)])
                    dve(v, lambda: v.tensor_copy(
                        out=kT_sb[:, tb * TB:(tb + 1) * TB], in_=sc0[:, TB:2 * TB]))
                    wv = []
                    if prev_gate_pe:
                        wv.append((pe, prev_gate_pe))
                    dve(v, lambda: v.tensor_copy(
                        out=vT_sb[:, tb * TB:(tb + 1) * TB], in_=sc1[:, 0:TB]),
                        waits=wv)
                qk_evac = dve.mark()

            if not ph("vtr"):
                continue
            for tg in range(0, TC, 8):
                n = min(8, TC - tg)
                bk = "A" if (tg // 8) % 2 == 0 else "B"
                slot = chAb if bk == "A" else chBb

                @block.tensor
                def _(t, tg=tg, n=n, slot=slot, bk=bk):
                    w = chw(bk) + [(dve, qk_evac)]
                    for i in range(n):
                        pe(t, lambda i=i: t.transpose(
                            slot(i), vT_sb[:, (tg + i) * 128:(tg + i + 1) * 128],
                            ident_b[:, :]), waits=w if i == 0 else ())
                vtp = pe.mark()

                @block.vector
                def _(v, tg=tg, n=n, slot=slot):
                    for i in range(n):
                        tc = tg + i
                        dve(v, lambda i=i, tc=tc: v.tensor_copy(
                            out=vaug_sb[:, tc * 130:tc * 130 + 64],
                            in_=slot(i)[:, 0:64]),
                            waits=[(pe, vtp)] if i == 0 else ())
                        dve(v, lambda i=i, tc=tc: v.tensor_copy(
                            out=vaug_sb[:, tc * 130 + 65:tc * 130 + 129],
                            in_=slot(i)[:, 64:128]))
                chfree[bk] = (dve, dve.mark())
            vtok_ready = dve.mark()
            at_chain_war[0] = chw("A") + chw("B")

            if not ph("attn"):
                continue
            accA = accv[:, 0:512]
            accB = accv[:, 512:1024]
            E = ep_f
            A0 = 16
            Z0 = A0 + 512
            SQ0 = Z0 + 256
            pstage_war = {0: [], 1: []}
            xq_war = {0: [], 1: []}
            xq_lanes = {0: [], 1: []}
            p_dma = dmas.mark()
            p_dma_q = [0, 0, 0, 0]
            epi = None
            NEARLY = 6
            for qb in range(NQB + 1):
                last = qb == NQB
                if not last:
                    q0 = qb * QB

                    @block.vector
                    def _(v, q0=q0, qb=qb):
                        w = []
                        if at_qblk_war[0]:
                            w.append((pe, at_qblk_war[0]))
                        if qb == 0:
                            w += list(prev_x1_dma)
                        for X in range(4):
                            dve(v, lambda X=X: v.tensor_copy(
                                out=qblk_sb[32 * X:32 * (X + 1),
                                            X * QB:(X + 1) * QB],
                                in_=qT_sb[32 * X:32 * (X + 1), q0:q0 + QB]),
                                waits=w if X == 0 else ())
                    qblk_done = dve.mark()

                if epi is not None:
                    @block.sync
                    def _(s, epi=epi):
                        for qt in range(2):
                            w = list(xq_war[qt])
                            if epi["q0"] == 0:
                                if prev_gelu_m[0]:
                                    w.append((act, prev_gelu_m[0]))
                                if prev_ffn_pe[0]:
                                    w.append((pe, prev_ffn_pe[0]))
                            r = xr + epi["q0"] + qt * 128
                            h = dmas(s, lambda r=r, qt=qt: s.dma_start(
                                out=xq_sb[qt][:, :], in_=x_d[r:r + 128, :]),
                                waits=w)
                            xq_lanes[qt] = h

                    @block.tensor
                    def _(t, epi=epi):
                        w = [(dve, epi["evac"])] + list(at_chain_war[0])
                        for s_ in range(8):
                            X, qt = s_ % 4, s_ // 4
                            co = X * 256 + qt * 128
                            cb = s_ * 65 if s_ < 4 else 512 + (s_ - 4) * 65
                            pe(t, lambda co=co, cb=cb: t.transpose(
                                chain[:, cb:cb + 65], fm_ep[0:65, co:co + 128],
                                ident_f[0:65, 0:65]),
                               waits=w if s_ == 0 else ())
                    at_trans_m[0] = pe.mark()

                sc_done = {}
                if not last:
                    act_base = act.count
                    nmid = 5 if epi is not None else 0

                    def _ep(kc):
                        return act_base + kc + 1 + (nmid if kc >= NEARLY
                                                    else 0)
                    exp_pred = [_ep(kc) for kc in range(KCN)]
                    sqrt_pred = (act_base + NEARLY + 5) if epi is not None \
                        else 0

                    @block.tensor
                    def _(t, qb=qb, qblk_done=qblk_done):
                        for kc in range(KCN):
                            par = kc % 2
                            scs = sc0 if par == 0 else sc1
                            po = par * 4 * QB
                            w = []
                            if at_exp_war[par]:
                                w.append((act, at_exp_war[par]))
                            if kc == 0:
                                w.append((dve, qblk_done))
                                if qb == 0:
                                    w.append((dve, vtok_ready))
                            for half in range(2):
                                pe(t, lambda kc=kc, scs=scs, half=half:
                                   t.matmul(
                                    scs[:, half * 512:(half + 1) * 512],
                                    kT_sb[:, kc * 128:(kc + 1) * 128],
                                    qblk_sb[:, half * 512:(half + 1) * 512],
                                    start=True, stop=True),
                                   waits=w if half == 0 else ())
                            at_exp_war[par] = exp_pred[kc]
                            sc_done[kc] = pe.mark()
                            wv = [(act, exp_pred[kc])]
                            if kc == 0:
                                if at_evac_prev[0]:
                                    wv.append((dve, at_evac_prev[0]))
                                if qb == 0:
                                    wv.append((dve, vtok_ready))
                            pe(t, lambda kc=kc, po=po: t.matmul(
                                accA[0:65, :],
                                vaug_sb[:, kc * 130:kc * 130 + 65],
                                probs_buf[:, po:po + 512],
                                start=(kc == 0), stop=(kc == KCN - 1)),
                               waits=wv)
                            pe(t, lambda kc=kc, po=po: t.matmul(
                                accB[0:65, :],
                                vaug_sb[:, kc * 130 + 65:kc * 130 + 130],
                                probs_buf[:, po + 512:po + 1024],
                                start=(kc == 0), stop=(kc == KCN - 1)))
                    pv_all = pe.mark()

                if epi is not None:
                    zm = [0] * 4

                    @block.vector
                    def _(v, epi=epi, zm=zm):
                        w = [(pe, at_trans_m[0])]
                        ch3a = chain[:, 0:260].rearrange(
                            "p (s c) -> p s c", c=65)
                        ch3b = chain[:, 512:772].rearrange(
                            "p (s c) -> p s c", c=65)
                        dve(v, lambda: v.reciprocal(
                            E[:, 0:4].rearrange("p (s c) -> p s c", c=1),
                            ch3a[:, :, 64:65]), waits=w)
                        dve(v, lambda: v.reciprocal(
                            E[:, 4:8].rearrange("p (s c) -> p s c", c=1),
                            ch3b[:, :, 64:65]))
                        eodd = E[:, 0:8].rearrange(
                            "p (s c) -> p s c", c=2)[:, :, 1:2]
                        dve(v, lambda: v.tensor_scalar(
                            out=eodd, in0=eodd, scalar1=neglam[:, :1],
                            scalar2=None, op0=ALU.mult))
                        for qt in range(2):
                            for head in range(2):
                                s1 = qt * 4 + 2 * head
                                s2 = s1 + 1
                                cb1 = s1 * 65 if s1 < 4 else 512 + (s1 - 4) * 65
                                cb2 = s2 * 65 if s2 < 4 else 512 + (s2 - 4) * 65
                                zi = qt * 2 + head
                                z = E[:, Z0 + zi * 64:Z0 + (zi + 1) * 64]
                                dve(v, lambda cb1=cb1, s1=s1, z=z:
                                    v.tensor_scalar(
                                        out=z, in0=chain[:, cb1:cb1 + 64],
                                        scalar1=E[:, s1:s1 + 1], scalar2=None,
                                        op0=ALU.mult))
                                dve(v, lambda cb2=cb2, s2=s2, z=z:
                                    v.scalar_tensor_tensor(
                                        out=z, in0=chain[:, cb2:cb2 + 64],
                                        scalar=E[:, s2:s2 + 1], in1=z,
                                        op0=ALU.mult, op1=ALU.add))
                                zm[zi] = dve.mark()
                    epi["zm"] = zm
                    epi["rsq"] = dve.mark()

                @block.scalar
                def _(s, epi=epi, sc_done=sc_done, last=last):
                    def emit_exp(kc):
                        par = kc % 2
                        scs = sc0 if par == 0 else sc1
                        po = par * 4 * QB
                        act(s, lambda scs=scs, po=po: s.activation(
                            out=probs_buf[:, po:po + 4 * QB],
                            in_=scs[:, 0:4 * QB], func=AF.Exp,
                            bias=biasc[:, 0:1], scale=biasc[:, 2:3]),
                            waits=[(pe, sc_done[kc])], noser=True)

                    if not last:
                        for kc in range(NEARLY):
                            emit_exp(kc)
                    if epi is not None:
                        for zi in range(4):
                            act(s, lambda zi=zi: s.activation(
                                out=E[:, A0 + zi * 64:A0 + (zi + 1) * 64],
                                in_=E[:, Z0 + zi * 64:Z0 + (zi + 1) * 64],
                                func=AF.Square,
                                accum_out=E[:, 8 + zi:9 + zi]),
                                waits=[(dve, epi["zm"][zi])], noser=True)
                        act(s, lambda: s.activation(
                            out=E[:, 12:16], in_=E[:, 8:12], func=AF.Sqrt,
                            bias=biasc[:, 4:5], scale=biasc[:, 5:6]))
                    if not last:
                        for kc in range(NEARLY, KCN):
                            emit_exp(kc)
                if not last:
                    assert act.count == act_base + KCN + nmid
                    if epi is not None:
                        assert act.count >= sqrt_pred
                else:
                    sqrt_pred = act.count

                if qb == 2 and pending_ln2[0] is not None:
                    pending_ln2[0]()
                    pending_ln2[0] = None

                if epi is not None:
                    @block.vector
                    def _(v, sqrt_pred=sqrt_pred):
                        dve(v, lambda: v.reciprocal(E[:, 12:16], E[:, 12:16]),
                            waits=[(act, sqrt_pred)])
                        for qt in range(2):
                            for head in range(2):
                                zi = qt * 2 + head
                                dve(v, lambda zi=zi, qt=qt, head=head:
                                    v.scalar_tensor_tensor(
                                        out=zct_sb[:, qt * 128 + head * 64:
                                                   qt * 128 + (head + 1) * 64],
                                        in0=E[:, Z0 + zi * 64:
                                              Z0 + (zi + 1) * 64],
                                        scalar=E[:, 12 + zi:13 + zi],
                                        in1=scaletile[:, :],
                                        op0=ALU.mult, op1=ALU.mult))
                    zct_done = dve.mark()

                    @block.tensor
                    def _(t, zct_done=zct_done):
                        for qt in range(2):
                            pe(t, lambda qt=qt: t.transpose(
                                chain_bf[:, qt * 128:(qt + 1) * 128],
                                zct_sb[:, qt * 128:(qt + 1) * 128],
                                ident_b[:, :]),
                               waits=[(dve, zct_done)] if qt == 0 else ())
                    zT_mm = pe.mark()

                    @block.vector
                    def _(v, zT_mm=zT_mm):
                        for qt in range(2):
                            dve(v, lambda qt=qt: v.tensor_copy(
                                out=zcatT_sb[:, qt * 128:(qt + 1) * 128],
                                in_=chain_bf[:, qt * 128:(qt + 1) * 128]),
                                waits=[(pe, zT_mm)] if qt == 0 else ())
                    zcat_done = dve.mark()

                    pstage_evac = 0
                    for qt in range(2):
                        @block.tensor
                        def _(t, qt=qt, zcat_done=zcat_done,
                              pstage_evac=pstage_evac, epi=epi):
                            w = []
                            if qt == 0:
                                w.append((dve, zcat_done))
                                w.append((dve, epi["rsq"]))
                            else:
                                w.append((dve, pstage_evac))
                            pe(t, lambda qt=qt: t.matmul(
                                chA[:, 0:512],
                                zcatT_sb[:, qt * 128:(qt + 1) * 128],
                                wproj_sb[:, 0:512], start=True, stop=True),
                               waits=w)
                            pe(t, lambda qt=qt: t.matmul(
                                chB[:, 0:512],
                                zcatT_sb[:, qt * 128:(qt + 1) * 128],
                                wproj_sb[:, 512:1024], start=True, stop=True))
                        pmm = pe.mark()

                        @block.vector
                        def _(v, qt=qt, pmm=pmm):
                            w = ([(pe, pmm)] + list(pstage_war[qt])
                                 + list(xq_lanes[qt]))
                            dve(v, lambda qt=qt: v.scalar_tensor_tensor(
                                out=pstage2[qt][:, 0:512],
                                in0=xq_sb[qt][:, 0:512],
                                scalar=1.0 / cfg.cores,
                                in1=chA[:, 0:512],
                                op0=ALU.mult, op1=ALU.add), waits=w)
                            dve(v, lambda qt=qt: v.scalar_tensor_tensor(
                                out=pstage2[qt][:, 512:1024],
                                in0=xq_sb[qt][:, 512:1024],
                                scalar=1.0 / cfg.cores,
                                in1=chB[:, 0:512],
                                op0=ALU.mult, op1=ALU.add))
                        pstage_evac = dve.mark()
                        xq_war[qt] = [(dve, pstage_evac)]

                        @block.sync
                        def _(s, qt=qt, epi=epi, pstage_evac=pstage_evac):
                            r = xr + epi["q0"] + qt * 128
                            h = dmas(s, lambda r=r, qt=qt: s.dma_start(
                                out=ppart_d[r:r + 128, :],
                                in_=pstage2[qt][:, :]),
                                waits=[(dve, pstage_evac)])
                            pstage_war[qt] = h
                        p_dma = dmas.mark()
                    pq = epi["q0"] // QB
                    if pq % 4 == 3:
                        p_dma_q[pq // 4] = p_dma
                    at_chain_war[0] = [(dve, pstage_evac)]
                    chfree["A"] = (dve, pstage_evac)
                    chfree["B"] = (dve, pstage_evac)

                if not last:
                    @block.vector
                    def _(v, pv_all=pv_all):
                        w = [(pe, pv_all)]
                        if at_trans_m[0]:
                            w.append((pe, at_trans_m[0]))
                        dve(v, lambda: v.tensor_copy(
                            out=fm_ep[0:65, 0:512], in_=accA[0:65, :]),
                            waits=w)
                        dve(v, lambda: v.tensor_copy(
                            out=fm_ep[0:65, 512:1024], in_=accB[0:65, :]))
                    at_evac_prev[0] = dve.mark()
                    epi = {"evac": at_evac_prev[0], "q0": q0}
                    at_qblk_war[0] = pe.mark()
                else:
                    epi = None

            if not ph("ar"):
                continue
            arq = []

            @block.gpsimd
            def _(g):
                QR = T // 4
                for q in range(4):
                    dmas.wait(g, p_dma_q[q])
                    ccx(g, lambda q=q, QR=QR: g.collective_compute(
                        "AllReduce", ALU.add, replica_groups=groups,
                        ins=[ppart_d[xr + q * QR:xr + (q + 1) * QR, :]],
                        outs=[psum_d[xr + q * QR:xr + (q + 1) * QR, :]]))
                    arq.append(ccx.mark())

            if not ph("ln1"):
                continue
            scale_war = {0: 0, 1: 0}
            x1war = {0: [], 1: []}
            stats_m = [0] * TC
            ln_base = act.count
            x18slot = [ep_f[:, 0:D], fm_ep[:, 0:D]]

            def ln1_scale(tc):
                par = tc % 2
                so = par * 8
                Y = ybuf[:, par * D:(par + 1) * D]

                @block.vector
                def _(v, tc=tc, par=par, so=so, Y=Y):
                    dve(v, lambda: v.reciprocal(
                        stat_f[:, so + 5:so + 6], stat_f[:, so + 5:so + 6]),
                        waits=[(act, ln_base + tc + 1),
                               (pe, at_trans_m[0])])
                    dve(v, lambda: v.tensor_scalar(
                        out=x1buf[:, par * D:(par + 1) * D], in0=Y,
                        scalar1=stat_f[:, so + 2:so + 3],
                        scalar2=stat_f[:, so + 5:so + 6],
                        op0=ALU.subtract, op1=ALU.mult))
                    dve(v, lambda: v.tensor_scalar(
                        out=x18slot[par], in0=x1buf[:, par * D:(par + 1) * D],
                        scalar1=1.0 / cfg.cores, scalar2=None, op0=ALU.mult))
                scale_war[par] = dve.mark()

                @block.sync
                def _(s, tc=tc, par=par):
                    r = tc * 128
                    h1 = dmas(s, lambda r=r: s.dma_start(
                        out=x1_ds[beta][r:r + 128, :],
                        in_=x1buf[:, par * D:(par + 1) * D]),
                        waits=[(dve, scale_war[par])])
                    h2 = dmas(s, lambda r=r, par=par: s.dma_start(
                        out=opart_ds[beta][r:r + 128, :], in_=x18slot[par]))
                    x1war[par] = h1 + h2

            for tc in range(TC):
                par = tc % 2
                yo = par * D
                so = par * 8

                @block.sync
                def _(s, tc=tc, yo=yo, par=par):
                    w = [(ccx, arq[tc // 8])]
                    if scale_war[par]:
                        w.append((dve, scale_war[par]))
                    if tc < 2:
                        w += list(out_st_h[0])
                    dmas(s, lambda: s.dma_start(
                        out=ybuf[:, yo:yo + D],
                        in_=psum_d[xr + tc * 128:xr + (tc + 1) * 128, :]),
                        waits=w)
                xl = dmas.mark()

                @block.vector
                def _(v, yo=yo, so=so, par=par, xl=xl):
                    Y = ybuf[:, yo:yo + D]
                    xb = x1buf[:, par * D:(par + 1) * D]
                    dve(v, lambda: v.tensor_scalar(
                        out=xb, in0=Y, scalar1=1.0, scalar2=None,
                        op0=ALU.mult, op1=ALU.add,
                        accum_out=stat_f[:, so + 0:so + 1]),
                        waits=[xl] + list(x1war[par]))
                    dve(v, lambda: v.scalar_tensor_tensor(
                        out=xb, in0=Y, scalar=1.0, in1=Y,
                        op0=ALU.mult, op1=ALU.mult,
                        accum_out=stat_f[:, so + 1:so + 2]))
                    dve(v, lambda: v.tensor_scalar(
                        out=stat_f[:, so + 2:so + 3],
                        in0=stat_f[:, so + 0:so + 1],
                        scalar1=1.0 / D, scalar2=None, op0=ALU.mult))
                    dve(v, lambda: v.tensor_mul(
                        stat_f[:, so + 3:so + 4], stat_f[:, so + 2:so + 3],
                        stat_f[:, so + 2:so + 3]))
                    dve(v, lambda: v.scalar_tensor_tensor(
                        out=stat_f[:, so + 4:so + 5],
                        in0=stat_f[:, so + 1:so + 2], scalar=1.0 / D,
                        in1=stat_f[:, so + 3:so + 4],
                        op0=ALU.mult, op1=ALU.subtract))
                    dve(v, lambda: v.tensor_scalar(
                        out=stat_f[:, so + 4:so + 5],
                        in0=stat_f[:, so + 4:so + 5],
                        scalar1=1e-5, scalar2=None, op0=ALU.add))
                stats_m[tc] = dve.mark()
                if tc >= 1:
                    ln1_scale(tc - 1)
            ln1_scale(TC - 1)

            @block.scalar
            def _(s):
                for tc in range(TC):
                    so = (tc % 2) * 8
                    act(s, lambda so=so, tc=tc: s.activation(
                        out=stat_f[:, so + 5:so + 6],
                        in_=stat_f[:, so + 4:so + 5], func=AF.Sqrt),
                        waits=[(dve, stats_m[tc])], noser=True)
            assert act.count == ln_base + TC
            x1_dma = dmas.mark()
            prev_x1_dma = x1_dma

            if not ph("x1bf"):
                continue

            @block.gpsimd
            def _(g):
                dmag(g, lambda: g.dma_start(
                    out=x1bf_d[xr:xr + T, :], in_=x1_ds[beta][:, :]),
                    waits=[x1_dma])
            x1bf_done = dmag.mark()

            if not ph("gate"):
                continue
            x1fm_war = {0: 0}
            for dc in range(DC):
                par = 0
                fo = 0

                @block.sync
                def _(s, dc=dc, fo=fo, par=par):
                    w = [x1bf_done]
                    if x1fm_war[par]:
                        w.append((pe, x1fm_war[par]))
                    dmas(s, lambda: s.dma_start(
                        out=x1fm_buf[:, fo:fo + T],
                        in_=x1bf_d[xr:xr + T, dc * 128:(dc + 1) * 128],
                        transpose=True), waits=w)
                x1fm_ld = dmas.mark()

                for ft in range(T // 512):
                    bk = "A" if ft % 2 == 0 else "B"
                    dst = chA if ft % 2 == 0 else chB

                    @block.tensor
                    def _(t, dc=dc, ft=ft, fo=fo, dst=dst, bk=bk,
                          x1fm_ld=x1fm_ld):
                        w = [x1fm_ld] + chw(bk)
                        pe(t, lambda: t.matmul(
                            dst[0:NE, 0:512],
                            gatew_sb[:, dc * NE:(dc + 1) * NE],
                            x1fm_buf[:, fo + ft * 512:fo + (ft + 1) * 512],
                            start=True, stop=True), waits=w)
                    gmm = pe.mark()

                    @block.vector
                    def _(v, dc=dc, ft=ft, dst=dst, gmm=gmm):
                        if dc == 0:
                            dve(v, lambda: v.tensor_copy(
                                out=gacc[:, ft * 512:(ft + 1) * 512],
                                in_=dst[0:NE, 0:512]), waits=[(pe, gmm)])
                        else:
                            dve(v, lambda: v.tensor_add(
                                gacc[:, ft * 512:(ft + 1) * 512],
                                gacc[:, ft * 512:(ft + 1) * 512],
                                dst[0:NE, 0:512]), waits=[(pe, gmm)])
                    chfree[bk] = (dve, dve.mark())
                x1fm_war[par] = pe.mark()
            prev_gate_pe = pe.mark()
            gate_done = dve.mark()

            for tg in range(0, TC, 8):
                n = min(8, TC - tg)
                bk = "A" if (tg // 8) % 2 == 0 else "B"
                base = 0 if bk == "A" else 512

                @block.tensor
                def _(t, tg=tg, n=n, base=base, bk=bk):
                    w = [(dve, gate_done)] + chw(bk)
                    for i in range(n):
                        pe(t, lambda i=i: t.transpose(
                            chain[:, base + i * 8:base + i * 8 + NE],
                            gacc[:, (tg + i) * 128:(tg + i + 1) * 128],
                            ident_f[0:NE, 0:NE]), waits=w if i == 0 else ())
                gt = pe.mark()

                @block.vector
                def _(v, tg=tg, n=n, base=base, gt=gt):
                    for i in range(n):
                        dve(v, lambda i=i: v.tensor_copy(
                            out=gtok[:, (tg + i) * NE:(tg + i + 1) * NE],
                            in_=chain[:, base + i * 8:base + i * 8 + NE]),
                            waits=[(pe, gt)] if i == 0 else ())
                chfree[bk] = (dve, dve.mark())
            gtok_done = dve.mark()

            if not ph("sm"):
                continue
            @block.scalar
            def _(s):
                act(s, lambda: s.activation(out=probs_tok[:, :], in_=gtok[:, :],
                                            func=AF.Exp),
                    waits=[(dve, gtok_done)], noser=True)
            sm_exp = act.mark()

            @block.vector
            def _(v):
                pt3 = probs_tok[:, :].rearrange("p (tc e) -> p tc e", e=NE)
                dve(v, lambda: v.reduce_sum(ssum[:, :], pt3, axis=AX.X),
                    waits=[(act, sm_exp)])
                dve(v, lambda: v.reciprocal(ssum[:, :], ssum[:, :]))
                for e in range(NE):
                    dve(v, lambda e=e: v.tensor_mul(
                        pt3[:, :, e], pt3[:, :, e], ssum[:, :]))
                dve(v, lambda: v.memset(tau_lo[:, :], 0.0))
                dve(v, lambda: v.memset(tau_hi[:, :], 1.0))
            probs_done = dve.mark()

            def count_pass(tau_src, out_mask):
                @block.vector
                def _(v):
                    dve(v, lambda: v.tensor_scalar(
                        out=tau_m8[:, :], in0=ind_em[:, :],
                        scalar1=tau_src[0:NE, 0:1], scalar2=None, op0=ALU.mult))
                tm = dve.mark()

                @block.tensor
                def _(t, tm=tm):
                    w = [(dve, tm)] + chw("A")
                    pe(t, lambda: t.matmul(chA[:, 0:TC * NE], ones8_f[:, :],
                                           tau_m8[:, :], start=True, stop=True),
                       waits=w)
                taumm = pe.mark()

                @block.vector
                def _(v, taumm=taumm):
                    dve(v, lambda: v.tensor_tensor(
                        out=out_mask[:, :], in0=probs_tok[:, :],
                        in1=chA[:, 0:TC * NE], op=ALU.is_gt),
                        waits=[(pe, taumm)])
                    om3 = out_mask[:, :].rearrange("p (tc e) -> p e tc", e=NE)
                    dve(v, lambda: v.reduce_sum(part8[:, :], om3, axis=AX.X))
                chfree["A"] = (dve, dve.mark())
                pm = dve.mark()

                @block.tensor
                def _(t, pm=pm):
                    w = [(dve, pm)] + chw("B")
                    pe(t, lambda: t.matmul(chB[0:NE, 0:1], part8[:, :],
                                           onescol_f[:, 0:1],
                                           start=True, stop=True), waits=w)
                cmm = pe.mark()

                @block.vector
                def _(v, cmm=cmm):
                    dve(v, lambda: v.tensor_copy(out=cnt8[:, :],
                                                 in_=chB[0:NE, 0:1]),
                        waits=[(pe, cmm)])
                chfree["B"] = (dve, dve.mark())

            if not ph("bis"):
                continue
            TN = TC * NE
            cmp_x = x1buf[:, 0:3 * TN]
            tau_m8x = x1buf[0:NE, 4 * TN:7 * TN]
            part8x = x1buf[:, 7 * TN:7 * TN + 3 * NE]
            delta = tau_hi
            for it in range(18):
                @block.vector
                def _(v):
                    dve(v, lambda: v.tensor_scalar(
                        out=delta[:, :], in0=delta[:, :], scalar1=0.25,
                        scalar2=None, op0=ALU.mult))
                    dve(v, lambda: v.tensor_add(bs_sc[:, 0:1], tau_lo[:, :],
                                                delta[:, :]))
                    dve(v, lambda: v.tensor_add(bs_sc[:, 1:2], bs_sc[:, 0:1],
                                                delta[:, :]))
                    dve(v, lambda: v.tensor_add(bs_sc[:, 2:3], bs_sc[:, 1:2],
                                                delta[:, :]))
                    for j in range(3):
                        dve(v, lambda j=j: v.tensor_scalar(
                            out=tau_m8x[:, j * TN:(j + 1) * TN],
                            in0=ind_em[:, :], scalar1=bs_sc[:, j:j + 1],
                            scalar2=None, op0=ALU.mult))
                tm = dve.mark()

                @block.tensor
                def _(t, tm=tm):
                    w = [(dve, tm)] + chw("A") + chw("B")
                    pe(t, lambda: t.matmul(chA[:, 0:512], ones8_f[:, :],
                                           tau_m8x[:, 0:512],
                                           start=True, stop=True), waits=w)
                    pe(t, lambda: t.matmul(chB[:, 0:256], ones8_f[:, :],
                                           tau_m8x[:, 512:768],
                                           start=True, stop=True))
                taumm = pe.mark()

                @block.vector
                def _(v, taumm=taumm):
                    for j in range(3):
                        src = (chA[:, j * TN:(j + 1) * TN] if j < 2
                               else chB[:, 0:TN])
                        dve(v, lambda j=j, src=src: v.tensor_tensor(
                            out=cmp_x[:, j * TN:(j + 1) * TN],
                            in0=probs_tok[:, :], in1=src, op=ALU.is_gt),
                            waits=[(pe, taumm)] if j == 0 else ())
                    cm4 = cmp_x.rearrange("p (j tc e) -> p j e tc", e=NE,
                                          tc=TC)
                    dve(v, lambda: v.reduce_sum(part8x[:, :], cm4, axis=AX.X))
                pm = dve.mark()

                @block.tensor
                def _(t, pm=pm):
                    for j in range(3):
                        pe(t, lambda j=j: t.matmul(
                            chB[0:NE, 300 + j:301 + j],
                            part8x[:, j * NE:(j + 1) * NE],
                            onescol_f[:, 0:1], start=(j == 0), stop=True),
                           waits=[(dve, pm)] if j == 0 else ())
                cmm = pe.mark()

                @block.vector
                def _(v, cmm=cmm):
                    dve(v, lambda: v.tensor_scalar(
                        out=bs_sc[:, 4:7], in0=chB[0:NE, 300:303],
                        scalar1=float(K), scalar2=None, op0=ALU.is_ge),
                        waits=[(pe, cmm)])
                    dve(v, lambda: v.reduce_sum(
                        bs_sc[:, 7:8], bs_sc[:, 4:7], axis=AX.X))
                    dve(v, lambda: v.scalar_tensor_tensor(
                        out=tau_lo[:, :], in0=bs_sc[:, 7:8],
                        scalar=delta[:, 0:1], in1=tau_lo[:, :],
                        op0=ALU.mult, op1=ALU.add))
            @block.vector
            def _(v):
                dve(v, lambda: v.tensor_add(tau_hi[:, :], tau_lo[:, :],
                                            delta[:, :]))

            if not ph("mask"):
                continue
            count_pass(tau_hi, hi_tok)

            @block.vector
            def _(v):
                dve(v, lambda: v.tensor_scalar(
                    out=fill8[:, :], in0=cnt8[:, :], scalar1=-1.0,
                    scalar2=float(K), op0=ALU.mult, op1=ALU.add))
            count_pass(tau_lo, tie_tok)

            @block.vector
            def _(v):
                dve(v, lambda: v.tensor_sub(tie_tok[:, :], tie_tok[:, :],
                                            hi_tok[:, :]))
            masks_done = dve.mark()

            for tg in range(0, TC, 2):
                n = min(2, TC - tg)
                bk = "A" if (tg // 2) % 2 == 0 else "B"
                base = 0 if bk == "A" else 512

                @block.tensor
                def _(t, tg=tg, n=n, base=base, bk=bk):
                    w = [(dve, masks_done)] + chw(bk)
                    for i in range(n):
                        pe(t, lambda i=i: t.transpose(
                            chain[0:NE, base + (2 * i) * 128:
                                  base + (2 * i) * 128 + 128],
                            hi_tok[:, (tg + i) * NE:(tg + i + 1) * NE],
                            ident_f[:, :]), waits=w if i == 0 else ())
                        pe(t, lambda i=i: t.transpose(
                            chain[0:NE, base + (2 * i + 1) * 128:
                                  base + (2 * i + 1) * 128 + 128],
                            tie_tok[:, (tg + i) * NE:(tg + i + 1) * NE],
                            ident_f[:, :]))
                ht = pe.mark()

                @block.vector
                def _(v, tg=tg, n=n, base=base, ht=ht):
                    for i in range(n):
                        w = [(pe, ht)] + list(out_st_h[0])
                        dve(v, lambda i=i: v.tensor_copy(
                            out=hi_b[0:NE, (tg + i) * 128:(tg + i + 1) * 128],
                            in_=chain[0:NE, base + (2 * i) * 128:
                                      base + (2 * i) * 128 + 128]),
                            waits=w if i == 0 else ())
                        dve(v, lambda i=i: v.tensor_copy(
                            out=tie_b[0:NE, (tg + i) * 128:(tg + i + 1) * 128],
                            in_=chain[0:NE, base + (2 * i + 1) * 128:
                                      base + (2 * i + 1) * 128 + 128]))
                chfree[bk] = (dve, dve.mark())
            em_done = dve.mark()

            if not ph("sel"):
                continue
            @block.vector
            def _(v):
                HIe = hi_b[0:NE, :]
                TIe = tie_b[0:NE, :]
                RNK = rank_f[0:NE, :]
                dve(v, lambda: v.tensor_tensor_scan(
                    out=RNK, data0=TIe, data1=TIe, initial=0.0,
                    op0=ALU.add, op1=ALU.bypass), waits=[(dve, em_done)])
                dve(v, lambda: v.tensor_scalar(
                    out=RNK, in0=RNK,
                    scalar1=fill8[0:NE, 0:1], scalar2=None, op0=ALU.is_le))
                dve(v, lambda: v.tensor_mul(TIe, TIe, RNK))
                dve(v, lambda: v.tensor_add(TIe, TIe, HIe))
                dve(v, lambda: v.tensor_tensor_scan(
                    out=RNK, data0=TIe, data1=TIe, initial=0.0,
                    op0=ALU.add, op1=ALU.bypass))
                dve(v, lambda: v.tensor_scalar(
                    out=RNK, in0=RNK, scalar1=-1.0,
                    scalar2=None, op0=ALU.add))
            gs_done = dve.mark()

            for tg in range(0, TC, 8):
                n = min(8, TC - tg)
                bk = "A" if (tg // 8) % 2 == 0 else "B"
                base = 0 if bk == "A" else 512

                @block.tensor
                def _(t, tg=tg, n=n, base=base, bk=bk):
                    w = [(dve, gs_done)] + chw(bk)
                    for i in range(n):
                        pe(t, lambda i=i: t.transpose(
                            chain[:, base + i * 2:base + i * 2 + 1],
                            rank_f[0:1, (tg + i) * 128:(tg + i + 1) * 128],
                            ident_f[0:1, 0:1]), waits=w if i == 0 else ())
                        pe(t, lambda i=i: t.transpose(
                            chain_bf[:, 2 * (base + i * 2 + 1):
                                     2 * (base + i * 2 + 1) + 1],
                            tie_b[0:1, (tg + i) * 128:(tg + i + 1) * 128],
                            ident_b[0:1, 0:1]))
                gst = pe.mark()

                @block.vector
                def _(v, tg=tg, n=n, base=base, gst=gst):
                    for i in range(n):
                        dve(v, lambda i=i: v.tensor_copy(
                            out=gs_tok[:, (tg + i) * 2:(tg + i) * 2 + 1],
                            in_=chain[:, base + i * 2:base + i * 2 + 1]),
                            waits=[(pe, gst)] if i == 0 else ())
                        dve(v, lambda i=i: v.tensor_copy(
                            out=gs_tok[:, (tg + i) * 2 + 1:(tg + i) * 2 + 2],
                            in_=chain_bf[:, 2 * (base + i * 2 + 1):
                                         2 * (base + i * 2 + 1) + 1]))
                chfree[bk] = (dve, dve.mark())
            gstok_done = dve.mark()

            if not ph("ex"):
                continue
            if beta > 0:
                @block.gpsimd
                def _(g):
                    pool(g, lambda: g.iota(
                        iota_j[:, :], pattern=[[1, K]], base=0,
                        channel_multiplier=0,
                        allow_small_or_imprecise_dtypes=True),
                        waits=[(pe, prev_ffn_pe[0])])
            iota_m = pool.mark()
            ex_prev = pe.mark()
            for tc in range(TC):
                @block.vector
                def _(v, tc=tc, ex_prev=ex_prev):
                    w = [(pe, ex_prev)]
                    if tc == 0:
                        w.append((dve, gstok_done))
                        w.append((pool, iota_m))
                    dve(v, lambda: v.tensor_copy(out=payload[:, 0:1],
                                                 in_=iota_colb[:, :]), waits=w)
                    dve(v, lambda: v.tensor_copy(out=payload[:, 1:2],
                                                 in_=chunkv_b[:, tc:tc + 1]))
                    dve(v, lambda: v.tensor_copy(
                        out=payload[:, 2:3],
                        in_=probs_tok[:, tc * NE:tc * NE + 1]))
                    dve(v, lambda: v.tensor_scalar(
                        out=pc_b[:, :], in0=iota_j[:, :],
                        scalar1=gs_tok[:, 2 * tc:2 * tc + 1],
                        scalar2=gs_tok[:, 2 * tc + 1:2 * tc + 2],
                        op0=ALU.is_equal, op1=ALU.mult))
                pcm = dve.mark()

                @block.tensor
                def _(t, tc=tc, pcm=pcm):
                    w = [(dve, pcm)]
                    if tc == 0:
                        w += chw("A") + chw("B")
                    for kf in range((K + 511) // 512):
                        f0, f1 = kf * 512, min(K, kf * 512 + 512)
                        pe(t, lambda f0=f0, f1=f1: t.matmul(
                            chain[0:3, f0:f1], payload[:, :], pc_b[:, f0:f1],
                            start=(tc == 0), stop=(tc == TC - 1)),
                           waits=w if kf == 0 else ())
                ex_prev = pe.mark()
            ex_mm = pe.mark()

            @block.vector
            def _(v):
                dve(v, lambda: v.tensor_copy(out=mg_f[:, :],
                                             in_=chain[0:3, 0:K]),
                    waits=[(pe, ex_mm)])
            chfree["A"] = (dve, dve.mark())
            chfree["B"] = (dve, dve.mark())
            mg_done = dve.mark()

            if not ph("mg"):
                continue
            prev_ffn = prev_ffn_dma if beta > 0 else []
            for ks in range(KC):
                bk = "A" if ks % 2 == 0 else "B"
                base = 0 if bk == "A" else 512

                @block.tensor
                def _(t, ks=ks, base=base, bk=bk):
                    w = [(dve, mg_done)] + chw(bk)
                    pe(t, lambda: t.transpose(
                        chain_bf[:, 2 * base:2 * base + 3],
                        mg_f[0:3, ks * 128:(ks + 1) * 128],
                        ident_b[0:3, 0:3]), waits=w)
                mt = pe.mark()

                @block.vector
                def _(v, ks=ks, base=base, mt=mt):
                    dve(v, lambda: v.tensor_copy(
                        out=mg3[:, 0:3], in_=chain_bf[:, 2 * base:2 * base + 3]),
                        waits=[(pe, mt)] + list(prev_ffn))
                    dve(v, lambda: v.scalar_tensor_tensor(
                        out=mcols[:, ks:ks + 1], in0=mg3[:, 1:2],
                        scalar=128.0, in1=mg3[:, 0:1],
                        op0=ALU.mult, op1=ALU.add))
                    dve(v, lambda: v.tensor_copy(out=gcol[:, ks:ks + 1],
                                                 in_=mg3[:, 2:3]))
                chfree[bk] = (dve, dve.mark())

            @block.vector
            def _(v):
                dve(v, lambda: v.tensor_copy(out=midx[:, :], in_=mcols[:, :]))
            midx_done = dve.mark()

            if not ph("ffn"):
                continue
            NG = K // 128
            NGH = NG // 2
            accpair = [sc1, accv]
            acc_war = {0: 0, 1: 0}
            scat_h = []
            scat_mark = dmag.mark()
            xin_war = [0]
            mm2_last = 0
            for hf in range(2):
                xinT_done = 0
                for g in range(NGH):
                    par = g % 2
                    col = hf * NGH + g

                    @block.gpsimd
                    def _(gg, g=g, par=par, col=col):
                        w = [(dve, midx_done), x1_dma]
                        if xin_war[0]:
                            w.append((pe, xin_war[0]))
                        dmag(gg, lambda: gg.indirect_dma_start(
                            out=xin_sb[:, par * D:(par + 1) * D],
                            out_offset=None, in_=x1_ds[beta],
                            in_offset=IndirectOffsetOnAxis(
                                ap=midx[:, col:col + 1], axis=0)), waits=w)
                    gth = dmag.mark()

                    bk = "A" if par == 0 else "B"
                    slot = chAb if par == 0 else chBb

                    @block.tensor
                    def _(t, par=par, slot=slot, bk=bk, gth=gth):
                        w = [gth] + chw(bk)
                        if mm2_last:
                            w.append((pe, mm2_last))
                        for dc in range(DC):
                            pe(t, lambda dc=dc, par=par: t.transpose(
                                slot(dc),
                                xin_sb[:, par * D + dc * 128:
                                       par * D + (dc + 1) * 128],
                                ident_b[:, :]), waits=w if dc == 0 else ())
                    xt = pe.mark()
                    if g >= 1:
                        xin_war[0] = xt

                    @block.vector
                    def _(v, g=g, slot=slot, xt=xt):
                        for dc in range(DC):
                            dve(v, lambda dc=dc: v.tensor_copy(
                                out=xinT_sb[:, dc * 512 + g * 128:
                                            dc * 512 + (g + 1) * 128],
                                in_=slot(dc)),
                                waits=[(pe, xt)] if dc == 0 else ())
                    chfree[bk] = (dve, dve.mark())
                xinT_done = dve.mark()
                xin_war[0] = pe.mark()

                pe_base = pe.count
                w1_lds = []
                for hc in range(HC):
                    @block.gpsimd
                    def _(gg, hc=hc):
                        w = [setup_dmag] if (beta == 0 and hf == 0) else []
                        if hc >= 2:
                            w.append((pe, pe_base + (hc - 1) * DC))
                        dmag(gg, lambda hc=hc: gg.dma_start(
                            out=w1_buf[:, (hc % 2) * D:(hc % 2) * D + D],
                            in_=w1bf_d[:, hc * D:(hc + 1) * D]), waits=w)
                    w1_lds.append(dmag.mark())

                gelu_base = act.count
                mm1_m = []
                @block.tensor
                def _(t, xinT_done=xinT_done, gelu_base=gelu_base,
                      w1_lds=w1_lds):
                    for hc in range(HC):
                        par = hc % 2
                        w = [w1_lds[hc]]
                        if hc == 0:
                            w.append((dve, xinT_done))
                        if hc >= 2:
                            w.append((act, gelu_base + hc - 1))
                        for dc in range(DC):
                            pe(t, lambda hc=hc, par=par, dc=dc: t.matmul(
                                sc0[:, par * 512:(par + 1) * 512],
                                w1_buf[:, (hc % 2) * D + dc * 128:
                                       (hc % 2) * D + (dc + 1) * 128],
                                xinT_sb[:, dc * 512:(dc + 1) * 512],
                                start=(dc == 0), stop=(dc == DC - 1)),
                               waits=w if dc == 0 else ())
                        mm1_m.append(pe.mark())
                assert pe.count == pe_base + HC * DC

                @block.scalar
                def _(s, mm1_m=mm1_m):
                    for hc in range(HC):
                        par = hc % 2
                        act(s, lambda hc=hc, par=par: s.activation(
                            out=hT_sb[:, hc * 512:(hc + 1) * 512],
                            in_=sc0[:, par * 512:(par + 1) * 512],
                            func=AF.Gelu_apprx_tanh),
                            waits=[(pe, mm1_m[hc])], noser=True)
                assert act.count == gelu_base + HC
                gelu_mk = [gelu_base + hc + 1 for hc in range(HC)]

                for g in range(NGH):
                    col = hf * NGH + g
                    par = g % 2
                    acc = accpair[par]

                    @block.tensor
                    def _(t, g=g, par=par, acc=acc, gelu_mk=gelu_mk):
                        for hc in range(HC):
                            w = [(act, gelu_mk[hc])]
                            if hc == 0 and acc_war[par]:
                                w.append((dve, acc_war[par]))
                            if hc == 0 and hf == 0 and g < 2:
                                w.append((dve, at_evac_prev[0]))
                            for dh in range(2):
                                pe(t, lambda hc=hc, g=g, dh=dh, acc=acc:
                                   t.matmul(
                                    acc[:, dh * 512:(dh + 1) * 512],
                                    hT_sb[:, hc * 512 + g * 128:
                                          hc * 512 + (g + 1) * 128],
                                    w2_sb[:, hc * D + dh * 512:
                                          hc * D + dh * 512 + 512],
                                    start=(hc == 0), stop=(hc == HC - 1)),
                                   waits=w if (hc == 0 and dh == 0) else ())
                    fmm = pe.mark()
                    mm2_last = fmm

                    @block.vector
                    def _(v, col=col, acc=acc, fmm=fmm):
                        dve(v, lambda: v.tensor_scalar(
                            out=fw_sb[:, :], in0=acc[:, :],
                            scalar1=gcol[:, col:col + 1],
                            scalar2=None, op0=ALU.mult),
                            waits=[(pe, fmm)] + list(scat_h))
                    acc_war[par] = dve.mark()
                    fw_done = dve.mark()

                    @block.gpsimd
                    def _(gg, col=col, fw_done=fw_done, scat_mark=scat_mark):
                        h = dmag(gg, lambda: gg.indirect_dma_start(
                            out=opart_ds[beta],
                            out_offset=IndirectOffsetOnAxis(
                                ap=midx[:, col:col + 1], axis=0),
                            in_=fw_sb[:, :], in_offset=None,
                            compute_op=ALU.add),
                            waits=[(dve, fw_done), x1_dma, scat_mark])
                        scat_h.clear()
                        scat_h.extend(h)
                    scat_mark = dmag.mark()

            at_evac_prev[0] = acc_war[1]
            ffn_acc_last[0] = max(acc_war[0], acc_war[1])
            prev_gelu_m[0] = act.count
            prev_ffn_pe[0] = pe.mark()
            prev_ffn_dma = dmag.mark()
            if not ph("rs"):
                continue

            @block.gpsimd
            def _(g, scat_mark=scat_mark):
                dmag.wait(g, scat_mark)
                ccx(g, lambda: g.collective_compute(
                    "ReduceScatter", ALU.add, replica_groups=groups,
                    ins=[opart_ds[beta][:, :]],
                    outs=[ored_d[beta * OUTR:(beta + 1) * OUTR, :]]))
            rs_done = ccx.mark()

            if not ph("ln2"):
                continue

            def emit_ln2(beta=beta, rs_done=rs_done):
                ln2_body(beta, rs_done)

            if beta == NB - 1:
                emit_ln2()
            else:
                pending_ln2[0] = emit_ln2

        if dbg_tensors:
            dbg_map = {"ppart": ppart_d, "psumd": psum_d, "xbf": xbf_d,
                       "x1bf": x1bf_d, "ored": ored_d}
            for b in range(NB):
                dbg_map[f"x1tok{b}"] = x1_ds[b]
                dbg_map[f"opart{b}"] = opart_ds[b]

            @block.gpsimd
            def _(g):
                dmas.wait(g)
                dmag.wait(g)
                ccx.wait(g)
                for nm in dbg_tensors:
                    src = dbg_map[nm]
                    dst = nc.dram_tensor(
                        "dbg_" + nm, list(src.shape),
                        src.dtype, kind="ExternalOutput").ap()
                    dmag(g, lambda src=src, dst=dst: g.dma_start(
                        out=dst, in_=src))
                dmag.wait(g)

        @block.sync
        def _(s):
            dmas.wait(s)

        @block.gpsimd
        def _(g):
            dmag.wait(g)
            ccx.wait(g)
            ln2st.wait(g)

    return nc


def make_inmaps(cfg: Cfg, x, Wqkv, Wproj, gate_w, w1, w2, lq1, lk1, lq2, lk2,
                rms_scale):
    T, D, NB, NE, HC = cfg.T, cfg.D, cfg.NB, cfg.NE, cfg.HC
    xf = np.ascontiguousarray(np.asarray(x).reshape(NB * T, D), dtype=np.float32)
    xts = np.ascontiguousarray(
        xf.reshape(NB * T // cfg.TB, cfg.TB, cfg.DC, 128)
        .transpose(3, 0, 2, 1).reshape(128, NB * T * cfg.DC),
        dtype=np.float32)
    lamv = np.ascontiguousarray(
        np.concatenate([lq1, lq2, lk1, lk2])[None, :], dtype=np.float32)
    rmsrow = np.ascontiguousarray(
        np.concatenate([rms_scale, rms_scale])[None, :], dtype=np.float32)
    in_maps = []
    for c in range(cfg.cores):
        A, B = 2 * c, 2 * c + 1
        qcols = []
        for hh in (A, B):
            qcols += list(range(32 * hh, 32 * hh + 32))
            qcols += list(range(D // 2 + 32 * hh, D // 2 + 32 * hh + 32))
        kcols = [D + cc for cc in qcols]
        vcols = []
        for hh in (A, B):
            vcols += list(range(2 * D + 64 * hh, 2 * D + 64 * hh + 64))
        wq = np.ascontiguousarray(Wqkv[:, qcols + kcols + vcols], dtype=np.float32)
        wp = np.ascontiguousarray(Wproj[128 * c:128 * (c + 1), :], dtype=np.float32)
        perm = [c] + [e for e in range(NE) if e != c]
        gw = np.ascontiguousarray(gate_w[:, perm], dtype=np.float32)
        w1c = np.ascontiguousarray(
            np.asarray(w1)[c].reshape(cfg.DC, 128, HC, 128)
            .transpose(1, 2, 0, 3).reshape(128, HC * cfg.DC * 128),
            dtype=np.float32)
        w2c = np.ascontiguousarray(np.asarray(w2)[c], dtype=np.float32)
        in_maps.append({
            "x": xf, "xts": xts, "wqkv": wq, "wproj": wp, "gatew": gw,
            "w1": w1c, "w2": w2c, "lamv": lamv, "rmsrow": rmsrow,
        })
    return in_maps


def assemble(cfg: Cfg, results):
    out = np.empty((cfg.NB, cfg.T, cfg.D), dtype=np.float32)
    for c in range(cfg.cores):
        r = np.asarray(results[c]["out"]).reshape(cfg.NB, cfg.OUTR, cfg.D)
        out[:, c * cfg.OUTR:(c + 1) * cfg.OUTR, :] = r
    return out


def kernel(L0_attn_x, L0_attn_Wqkv, L0_attn_Wproj, L0_attn_lambda_q1,
           L0_attn_lambda_k1, L0_attn_lambda_q2, L0_attn_lambda_k2,
           L0_attn_rms_scale, L0_ffn_gate_w, L0_ffn_w1, L0_ffn_w2):
    cfg = FULL
    in_maps = make_inmaps(
        cfg, np.asarray(L0_attn_x), np.asarray(L0_attn_Wqkv),
        np.asarray(L0_attn_Wproj), np.asarray(L0_ffn_gate_w),
        np.asarray(L0_ffn_w1), np.asarray(L0_ffn_w2),
        np.asarray(L0_attn_lambda_q1), np.asarray(L0_attn_lambda_k1),
        np.asarray(L0_attn_lambda_q2), np.asarray(L0_attn_lambda_k2),
        np.asarray(L0_attn_rms_scale))
    nc = build_nc(cfg)
    res = run_bass_kernel_spmd(nc, in_maps, core_ids=list(range(cfg.cores)))
    return assemble(cfg, res.results)


# version 21
# speedup vs baseline: 1.0007x; 1.0007x over previous
import math
from contextlib import ExitStack

import numpy as np
import ml_dtypes

import concourse.bass as bass
import concourse.mybir as mybir
from concourse.bass import IndirectOffsetOnAxis
from concourse.bass_utils import run_bass_kernel_spmd

F32 = mybir.dt.float32
BF16 = mybir.dt.bfloat16
U32 = mybir.dt.uint32
AF = mybir.ActivationFunctionType
ALU = mybir.AluOpType
AX = mybir.AxisListType

LAMBDA_INIT = 0.8 - 0.6 * math.exp(-0.3 * 1)


class Cfg:
    def __init__(self, cores=8, T=4096, HID=4096, NB=4, CAP=2.0):
        self.cores = cores
        self.T = T
        self.D = 128 * cores
        self.HID = HID
        self.NB = NB
        self.NE = cores
        self.K = int(CAP * T / self.NE)
        D = self.D
        self.DC = D // 128
        self.TC = T // 128
        self.QB = 256
        self.NQB = T // self.QB
        self.KCN = T // 128
        self.HC = HID // 128
        self.KC = self.K // 128
        self.SB = min(128, self.K)
        self.NSB = self.K // self.SB
        self.OUTR = T // cores
        self.TB = 256
        self.NTB = T // self.TB
        self.PO_F = min(512, D)
        self.POT = D // self.PO_F


FULL = Cfg()


class Eng:

    def __init__(self, nc, ctx, name, dma=False, serialize=True):
        self.sem = ctx.enter_context(nc.semaphore(name))
        self.count = 0
        self.inc = 16 if dma else 1
        self.serialize = serialize

    def mark(self):
        return self.count

    def __call__(self, eng, emit, waits=(), inc=True, noser=False):
        merged = {}

        def add(other, cnt):
            if cnt and cnt > 0:
                k = id(other.sem)
                prev = merged.get(k)
                if prev is None or cnt > prev[1]:
                    merged[k] = (other, cnt)

        for item in waits:
            if isinstance(item, list):
                for (o, c) in item:
                    add(o, c)
            else:
                add(*item)
        if self.serialize and not noser and self.count:
            add(self, self.count)
        for (o, c) in merged.values():
            eng.wait_ge(o.sem, c)
        inst = emit()
        if inc:
            inst.then_inc(self.sem, self.inc)
            self.count += self.inc
        return self.count

    def wait(self, eng, count=None):
        eng.wait_ge(self.sem, self.count if count is None else count)


class DmaPool:

    def __init__(self, nc, ctx, name, nlanes=8):
        self.lanes = [Eng(nc, ctx, f"{name}{i}", dma=True, serialize=True)
                      for i in range(nlanes)]
        self.i = 0

    def __call__(self, eng, emit, waits=()):
        lane = self.lanes[self.i]
        self.i = (self.i + 1) % len(self.lanes)
        lane(eng, emit, waits)
        return [(lane, lane.count)]

    def mark(self):
        return [(l, l.count) for l in self.lanes if l.count]

    def wait(self, eng, marks=None):
        for (l, c) in (self.mark() if marks is None else marks):
            if c:
                eng.wait_ge(l.sem, c)


PHASES = ["qkv", "vtr", "attn", "ar", "ln1", "x1bf", "gate", "sm", "bis",
          "mask", "sel", "ex", "mg", "ffn", "rs", "ln2", "all"]


def build_nc(cfg: Cfg, debug=False, stop_after="all", dbg_tensors=(),
             attn_sub="full"):
    T, D, NB, NE = cfg.T, cfg.D, cfg.NB, cfg.NE
    DC, TC, QB, NQB, KCN = cfg.DC, cfg.TC, cfg.QB, cfg.NQB, cfg.KCN
    HID, HC, K, KC, SB, NSB = cfg.HID, cfg.HC, cfg.K, cfg.KC, cfg.SB, cfg.NSB
    OUTR, TB, NTB = cfg.OUTR, cfg.TB, cfg.NTB
    PO_F, POT = cfg.PO_F, cfg.POT
    groups = [list(range(cfg.cores))]
    NITER = 36

    SL = PHASES.index(stop_after)

    def ph(name):
        return SL >= PHASES.index(name)

    nc = bass.Bass("TRN2", target_bir_lowering=False, debug=debug)

    x_d = nc.dram_tensor("x", [NB * T, D], F32, kind="ExternalInput").ap()
    xts_d = nc.dram_tensor("xts", [128, NB * T * (D // 128)], F32,
                           kind="ExternalInput").ap()
    wqkv_d = nc.dram_tensor("wqkv", [D, 384], F32, kind="ExternalInput").ap()
    wproj_d = nc.dram_tensor("wproj", [128, D], F32, kind="ExternalInput").ap()
    gatew_d = nc.dram_tensor("gatew", [D, NE], F32, kind="ExternalInput").ap()
    w1_d = nc.dram_tensor("w1", [128, HC * DC * 128], F32,
                          kind="ExternalInput").ap()
    w2_d = nc.dram_tensor("w2", [HID, D], F32, kind="ExternalInput").ap()
    lam_d = nc.dram_tensor("lamv", [1, 128], F32, kind="ExternalInput").ap()
    rms_d = nc.dram_tensor("rmsrow", [1, 128], F32, kind="ExternalInput").ap()
    out_d = nc.dram_tensor("out", [NB * OUTR, D], F32, kind="ExternalOutput").ap()

    xbf_d = nc.dram_tensor("xbf", [NB * T, D], BF16).ap()
    xtsbf_d = nc.dram_tensor("xtsbf", [128, NB * T * (D // 128)], BF16).ap()
    w1bf_d = nc.dram_tensor("w1bf", [128, HC * DC * 128], BF16).ap()
    ppart_d = nc.dram_tensor("ppart", [NB * T, D], F32).ap()
    psum_d = nc.dram_tensor("psumd", [NB * T, D], F32,
                            addr_space="Shared" if cfg.cores > 4 else "Local").ap()
    x1_ds = [nc.dram_tensor(f"x1tok{b}", [T, D], F32).ap() for b in range(NB)]
    x1bf_d = nc.dram_tensor("x1bf", [NB * T, D], BF16).ap()
    opart_ds = [nc.dram_tensor(f"opart{b}", [T, D], F32).ap() for b in range(NB)]
    ored_d = nc.dram_tensor("ored", [NB * OUTR, D], F32).ap()

    with ExitStack() as ctx:
        sb = lambda n, s, d: ctx.enter_context(nc.sbuf_tensor(n, s, d))
        ps_ = lambda n, s, d: ctx.enter_context(nc.psum_tensor(n, s, d))

        ident_b = sb("ident_b", [128, 128], BF16)
        ident_f = sb("ident_f", [128, 128], F32)
        iota_col = sb("iota_col", [128, 1], F32)
        iota_row = None

        ones8_f = sb("ones8_f", [NE, 128], F32)
        onesrow_f = sb("onesrow_f", [1, 128], F32)
        onescol_f = sb("onescol_f", [128, 1], F32)
        hT_sb = sb("hT_sb", [128, HC * 512], BF16)
        hT_f32 = hT_sb[:, :].bitcast(F32)
        iota_j = hT_f32[:, T:T + K]
        iota_colb = sb("iota_colb", [128, 1], BF16)
        chunkv_b = sb("chunkv_b", [128, TC], BF16)
        iota_e = sb("iota_e", [NE, TC * NE], F32)
        ind_em = sb("ind_em", [NE, TC * NE], F32)
        biasc = sb("biasc", [128, 6], F32)
        scaletile = sb("scaletile", [128, 64], F32)
        neglam = sb("neglam", [128, 1], F32)
        lam_sb = sb("lam_sb", [1, 128], F32)
        lamp = sb("lamp", [1, 64], F32)
        lamsc = sb("lamsc", [1, 8], F32)
        rmsrow_sb = sb("rmsrow_sb", [1, 128], F32)

        wqkv_sb = sb("wqkv_sb", [128, DC * 384], BF16)
        wproj_sb = sb("wproj_sb", [128, D], BF16)
        gatew_sb = sb("gatew_sb", [128, DC * NE], BF16)
        w2_sb = sb("w2_sb", [128, HC * D], BF16)
        w1_buf = sb("w1_buf", [128, 2 * D], BF16)

        xfm_buf = sb("xfm_buf", [128, 2 * DC * TB], BF16)
        qT_sb = sb("qT_sb", [128, T], BF16)
        kT_sb = sb("kT_sb", [128, T], BF16)

        vaug_sb = sb("vaug_sb", [128, TC * 130], BF16)
        qblk_sb = sb("qblk_sb", [128, 4 * QB], BF16)
        probs_buf = sb("probs_buf", [128, 2 * 4 * QB], BF16)
        zcatT_sb = sb("zcatT_sb", [128, QB], BF16)
        zct_sb = sb("zct_sb", [128, 256], BF16)
        fm_ep = sb("fm_ep", [128, 1024], F32)
        ep_f = sb("ep_f", [128, 4 * QB], F32)
        iota_row = ep_f[:, 128:256]
        assert D <= 4 * QB
        x18buf = ep_f[:, 0:D]
        ybuf = sb("ybuf", [128, 2 * D], F32)
        x1buf = sb("x1buf", [128, 2 * D], F32)
        x1fm_buf = sb("x1fm_buf", [128, max(T, 4 * D)], BF16)
        vT_sb = x1fm_buf[:, 0:T]
        hi_b = x1fm_buf[0:8, 0:T]
        tie_b = ybuf[:, :].bitcast(BF16)[0:8, 0:T]
        rank_f = hT_f32[0:8, 0:T]
        gacc = rank_f[0:NE, :]
        xq_sb = [hT_f32[:, T + K:T + K + 1024],
                 hT_f32[:, T + K + 1024:T + K + 2048]]
        gtok = sb("gtok", [128, TC * NE], F32)
        probs_tok = gtok
        ssum = sb("ssum", [128, TC], F32)
        tau_lo = sb("tau_lo", [NE, 1], F32)
        tau_hi = sb("tau_hi", [NE, 1], F32)
        tau_mid = sb("tau_mid", [NE, 1], F32)
        tau_m8 = sb("tau_m8", [NE, TC * NE], F32)
        cmp_tok = sb("cmp_tok", [128, TC * NE], F32)
        part8 = sb("part8", [128, NE], F32)
        cnt8 = sb("cnt8", [NE, 1], F32)
        cond8 = sb("cond8", [NE, 1], F32)
        bs_sc = sb("bs_sc", [NE, 8], F32)
        hi_tok = cmp_tok
        tie_tok = sb("tie_tok", [128, TC * NE], F32)

        fill8 = sb("fill8", [NE, 1], F32)

        gs_tok = sb("gs_tok", [128, 2 * TC], F32)
        pc_b = ep_f[:, 0:K // 2].bitcast(BF16)
        payload = sb("payload", [128, 3], BF16)
        mg_f = sb("mg_f", [3, K], BF16)
        mg3 = sb("mg3", [128, 4], F32)
        mcols = sb("mcols", [128, KC], F32)
        midx = sb("midx", [128, KC], U32)
        gcol = sb("gcol", [128, KC], F32)
        xin_sb = sb("xin_sb", [128, 2 * D], BF16)
        xinT_sb = sb("xinT_sb", [128, DC * 512], BF16)
        fw_sb = probs_buf[:, 0:D]
        xinT_f32 = xinT_sb[:, :].bitcast(F32)
        pstage2 = [xinT_f32[:, 0:1024], xinT_f32[:, 1024:2048]]
        pstage = pstage2[0]
        lnslots = [ybuf[:, 0:D], ybuf[:, D:2 * D]]
        stat_f = sb("stat_f", [128, 16], F32)

        sc0 = ps_("sc0", [128, 4 * QB], F32)
        sc1 = ps_("sc1", [128, 4 * QB], F32)
        accv = ps_("accv", [128, 1024], F32)
        chain = ps_("chain", [128, 1024], F32)
        chA = chain[:, 0:512]
        chB = chain[:, 512:1024]
        chain_bf = chain[:, :].bitcast(BF16)

        def chAb(i):
            return chain_bf[:, i * 128:(i + 1) * 128]

        def chBb(i):
            return chain_bf[:, 1024 + i * 128:1024 + (i + 1) * 128]

        dmas = DmaPool(nc, ctx, "dmas")
        dmag = DmaPool(nc, ctx, "dmag")
        pe = Eng(nc, ctx, "pe", serialize=False)
        act = Eng(nc, ctx, "act")
        dve = Eng(nc, ctx, "dve")
        pool = Eng(nc, ctx, "pool")
        ccx = Eng(nc, ctx, "ccx", serialize=False)
        ln2st = Eng(nc, ctx, "ln2st", dma=True, serialize=False)

        block = ctx.enter_context(nc.Block())

        chfree = {"A": (dve, 0), "B": (dve, 0)}

        def chw(which):
            e, c = chfree[which]
            return [(e, c)] if c > 0 else []

        xconv_m = []
        @block.gpsimd
        def _(g):
            for dc in range(DC):
                dmag(g, lambda dc=dc: g.dma_start(
                    out=wqkv_sb[:, dc * 384:(dc + 1) * 384],
                    in_=wqkv_d[dc * 128:(dc + 1) * 128, :]))
            dmag(g, lambda: g.dma_start(out=wproj_sb[:, :], in_=wproj_d))
            for dc in range(DC):
                dmag(g, lambda dc=dc: g.dma_start(
                    out=gatew_sb[:, dc * NE:(dc + 1) * NE],
                    in_=gatew_d[dc * 128:(dc + 1) * 128, :]))
            for hc in range(HC):
                dmag(g, lambda hc=hc: g.dma_start(
                    out=w2_sb[:, hc * D:(hc + 1) * D],
                    in_=w2_d[hc * 128:(hc + 1) * 128, :]))
            for ci in range(8):
                cw = NB * T * DC // 8
                dmag(g, lambda ci=ci, cw=cw: g.dma_start(
                    out=xtsbf_d[:, ci * cw:(ci + 1) * cw],
                    in_=xts_d[:, ci * cw:(ci + 1) * cw]))
                xconv_m.append(dmag.mark())
            for ci in range(4):
                cw = HC * DC * 128 // 4
                dmag(g, lambda ci=ci, cw=cw: g.dma_start(
                    out=w1bf_d[:, ci * cw:(ci + 1) * cw],
                    in_=w1_d[:, ci * cw:(ci + 1) * cw]))
            pool(g, lambda: g.iota(iota_col[:, :], pattern=[[0, 1]], base=0,
                                   channel_multiplier=1,
                                   allow_small_or_imprecise_dtypes=True))
            pool(g, lambda: g.iota(iota_row[:, :], pattern=[[1, 128]], base=0,
                                   channel_multiplier=0,
                                   allow_small_or_imprecise_dtypes=True))
            pool(g, lambda: g.iota(iota_j[:, :], pattern=[[1, K]], base=0,
                                   channel_multiplier=0,
                                   allow_small_or_imprecise_dtypes=True))
            pool(g, lambda: g.iota(iota_colb[:, :], pattern=[[0, 1]], base=0,
                                   channel_multiplier=1,
                                   allow_small_or_imprecise_dtypes=True))
            pool(g, lambda: g.iota(chunkv_b[:, :], pattern=[[1, TC]], base=0,
                                   channel_multiplier=0,
                                   allow_small_or_imprecise_dtypes=True))
            pool(g, lambda: g.iota(iota_e[:, :], pattern=[[0, TC], [1, NE]],
                                   base=0, channel_multiplier=0,
                                   allow_small_or_imprecise_dtypes=True))


        @block.sync
        def _(s):
            dmas(s, lambda: s.dma_start(out=lam_sb[:, :], in_=lam_d))
            dmas(s, lambda: s.dma_start(out=rmsrow_sb[:, :], in_=rms_d))

        @block.vector
        def _(v):
            pool.wait(v)
            dmas.wait(v)
            dve(v, lambda: v.memset(ones8_f[:, :], 1.0))
            dve(v, lambda: v.memset(onesrow_f[:, :], 1.0))
            dve(v, lambda: v.memset(onescol_f[:, :], 1.0))
            dve(v, lambda: v.memset(biasc[:, :], 0.0))
            dve(v, lambda: v.memset(biasc[:, 0:1], -8.0))
            dve(v, lambda: v.memset(biasc[:, 1:2], -0.5))
            dve(v, lambda: v.memset(biasc[:, 2:3], 1.0 / math.sqrt(32.0)))
            dve(v, lambda: v.memset(biasc[:, 4:5], 1e-6))
            dve(v, lambda: v.memset(biasc[:, 5:6], 1.0 / 64))
            dve(v, lambda: v.memset(qblk_sb[:, :], 0.0))
            va3 = vaug_sb[:, :].rearrange("p (t c) -> p t c", c=130)
            dve(v, lambda: v.memset(va3[:, :, 64:65], 1.0))
            dve(v, lambda: v.memset(va3[:, :, 129:130], 1.0))
            dve(v, lambda: v.tensor_scalar(
                out=ep_f[0:1, 0:64], in0=rmsrow_sb[0:1, 0:64],
                scalar1=(1.0 - LAMBDA_INIT), scalar2=None, op0=ALU.mult))
            dve(v, lambda: v.tensor_scalar(
                out=ident_b[:, :], in0=iota_row[:, :], scalar1=iota_col[:, :1],
                scalar2=None, op0=ALU.is_equal))
            dve(v, lambda: v.tensor_scalar(
                out=ident_f[:, :], in0=iota_row[:, :], scalar1=iota_col[:, :1],
                scalar2=None, op0=ALU.is_equal))
            dve(v, lambda: v.tensor_scalar(
                out=ind_em[:, :], in0=iota_e[:, :], scalar1=iota_col[:NE, :1],
                scalar2=None, op0=ALU.is_equal))
            dve(v, lambda: v.tensor_mul(lamp[:, :], lam_sb[:, 0:64],
                                        lam_sb[:, 64:128]))
            dve(v, lambda: v.reduce_sum(lamsc[0:1, 0:1], lamp[:, 0:32],
                                        axis=AX.X))
            dve(v, lambda: v.reduce_sum(lamsc[0:1, 1:2], lamp[:, 32:64],
                                        axis=AX.X))

        @block.scalar
        def _(s):
            dve.wait(s)
            act(s, lambda: s.activation(out=lamsc[0:1, 2:4], in_=lamsc[0:1, 0:2],
                                        func=AF.Exp))
        lam_act = act.mark()

        @block.vector
        def _(v):
            act.wait(v)
            dve(v, lambda: v.tensor_sub(lamsc[0:1, 4:5], lamsc[0:1, 3:4],
                                        lamsc[0:1, 2:3]))
        lam_dve2 = dve.mark()

        @block.tensor
        def _(t):
            pe(t, lambda: t.matmul(chA[:, 0:1], onesrow_f[0:1, :],
                                   lamsc[0:1, 4:5], start=True, stop=True),
               waits=[(dve, lam_dve2)])
            pe(t, lambda: t.matmul(chB[:, 0:64], onesrow_f[0:1, :],
                                   ep_f[0:1, 0:64], start=True, stop=True))
        bc_mm = pe.mark()

        @block.vector
        def _(v):
            dve(v, lambda: v.tensor_scalar(
                out=neglam[:, :], in0=chA[:, 0:1], scalar1=-LAMBDA_INIT,
                scalar2=None, op0=ALU.add), waits=[(pe, bc_mm)])
            dve(v, lambda: v.tensor_copy(out=scaletile[:, :], in_=chB[:, 0:64]))
        chfree["A"] = (dve, dve.mark())
        chfree["B"] = (dve, dve.mark())
        setup_dmag = dmag.mark()

        def ln2_body(beta, rs_done):
            nln = (OUTR + 127) // 128
            l2stats = [0] * nln
            l2_base = act.count

            def ln2_scale(oc):
                par = oc % 2
                slot = lnslots[par]
                so = par * 8
                r0 = oc * 128

                @block.vector
                def _(v, slot=slot, so=so, oc=oc):
                    dve(v, lambda: v.reciprocal(
                        stat_f[:, so + 5:so + 6], stat_f[:, so + 5:so + 6]),
                        waits=[(act, l2_base + oc + 1)])
                    dve(v, lambda: v.tensor_scalar(
                        out=slot[:, :], in0=slot[:, :],
                        scalar1=stat_f[:, so + 2:so + 3],
                        scalar2=stat_f[:, so + 5:so + 6],
                        op0=ALU.subtract, op1=ALU.mult))
                l2o = dve.mark()

                @block.gpsimd
                def _(g, r0=r0, slot=slot, l2o=l2o):
                    ln2st(g, lambda: g.dma_start(
                        out=out_d[beta * OUTR + r0:beta * OUTR + r0 + 128, :],
                        in_=slot[:, :]), waits=[(dve, l2o)])

            for oc in range(nln):
                par = oc % 2
                slot = lnslots[par]
                so = par * 8

                @block.gpsimd
                def _(g, oc=oc, slot=slot, rs_done=rs_done):
                    w = [(ccx, rs_done)]
                    if oc >= 2:
                        w.append((ln2st, ln2st.count))
                    dmag(g, lambda oc=oc: g.dma_start(
                        out=slot[:, :],
                        in_=ored_d[beta * OUTR + oc * 128:
                                   beta * OUTR + (oc + 1) * 128, :]),
                        waits=w)
                lnl = dmag.mark()

                @block.vector
                def _(v, slot=slot, so=so, lnl=lnl):
                    Yl = slot[:, :]
                    dve(v, lambda: v.tensor_scalar(
                        out=hT_f32[:, 0:1024], in0=Yl, scalar1=1.0,
                        scalar2=None, op0=ALU.mult, op1=ALU.add,
                        accum_out=stat_f[:, so + 0:so + 1]),
                        waits=[lnl])
                    dve(v, lambda: v.scalar_tensor_tensor(
                        out=hT_f32[:, 0:1024], in0=Yl, scalar=1.0, in1=Yl,
                        op0=ALU.mult, op1=ALU.mult,
                        accum_out=stat_f[:, so + 1:so + 2]))
                    dve(v, lambda: v.tensor_scalar(
                        out=stat_f[:, so + 2:so + 3],
                        in0=stat_f[:, so + 0:so + 1],
                        scalar1=1.0 / D, scalar2=None, op0=ALU.mult))
                    dve(v, lambda: v.tensor_mul(
                        stat_f[:, so + 3:so + 4], stat_f[:, so + 2:so + 3],
                        stat_f[:, so + 2:so + 3]))
                    dve(v, lambda: v.scalar_tensor_tensor(
                        out=stat_f[:, so + 4:so + 5],
                        in0=stat_f[:, so + 1:so + 2], scalar=1.0 / D,
                        in1=stat_f[:, so + 3:so + 4],
                        op0=ALU.mult, op1=ALU.subtract))
                    dve(v, lambda: v.tensor_scalar(
                        out=stat_f[:, so + 4:so + 5],
                        in0=stat_f[:, so + 4:so + 5],
                        scalar1=1e-5, scalar2=None, op0=ALU.add))
                l2stats[oc] = dve.mark()
                if oc >= 1:
                    ln2_scale(oc - 1)
            ln2_scale(nln - 1)

            @block.scalar
            def _(s):
                for oc in range(nln):
                    so = (oc % 2) * 8
                    act(s, lambda so=so, oc=oc: s.activation(
                        out=stat_f[:, so + 5:so + 6],
                        in_=stat_f[:, so + 4:so + 5], func=AF.Sqrt),
                        waits=[(dve, l2stats[oc])], noser=True)
            assert act.count == l2_base + nln
            out_st_h[0] = [(ln2st, ln2st.count)]


        out_st_h = [[]]
        pending_ln2 = [None]
        prev_ffn_dma = []
        prev_gate_pe = 0
        prev_x1_dma = []
        at_exp_war = {0: 0, 1: 0, 2: 0, 3: 0}
        at_chain_war = [[]]
        at_trans_m = [0]
        at_evac_prev = [0]
        at_qblk_war = [0]
        prev_ffn_pe = [0]
        prev_gelu_m = [0]
        ffn_acc_last = [0]
        for beta in range(NB):
            xr = beta * T

            qk_evac = dve.mark()
            xfm_war = {0: 0, 1: 0}
            for tb in range(NTB):
                par = tb % 2
                xo = par * DC * TB

                @block.sync
                def _(s, tb=tb, par=par, xo=xo):
                    c0 = (beta * NTB + tb) * DC * TB
                    w = [xconv_m[c0 // (NB * T * DC // 8)]]
                    if xfm_war[par]:
                        w.append((pe, xfm_war[par]))
                    dmas(s, lambda c0=c0: s.dma_start(
                        out=xfm_buf[:, xo:xo + DC * TB],
                        in_=xtsbf_d[:, c0:c0 + DC * TB]), waits=w)
                xfm_ld = dmas.mark()

                @block.tensor
                def _(t, tb=tb, xo=xo):
                    outs = [sc0[:, 0:TB], sc0[:, TB:2 * TB], sc1[:, 0:TB]]
                    w = [xfm_ld,
                         (dve, qk_evac if tb > 0 else ffn_acc_last[0])]
                    if beta == 0 and tb == 0:
                        w.append(setup_dmag)
                    first = True
                    for ci in range(3):
                        for dc in range(DC):
                            pe(t, lambda ci=ci, dc=dc: t.matmul(
                                outs[ci][:, 0:TB],
                                wqkv_sb[:, dc * 384 + ci * 128:
                                        dc * 384 + (ci + 1) * 128],
                                xfm_buf[:, xo + dc * TB:xo + (dc + 1) * TB],
                                start=(dc == 0), stop=(dc == DC - 1)),
                               waits=w if first else ())
                            first = False
                xfm_war[par] = pe.mark()
                qkv_mm = pe.mark()

                @block.vector
                def _(v, tb=tb):
                    dve(v, lambda: v.tensor_copy(
                        out=qT_sb[:, tb * TB:(tb + 1) * TB], in_=sc0[:, 0:TB]),
                        waits=[(pe, qkv_mm)])
                    dve(v, lambda: v.tensor_copy(
                        out=kT_sb[:, tb * TB:(tb + 1) * TB], in_=sc0[:, TB:2 * TB]))
                    wv = []
                    if prev_gate_pe:
                        wv.append((pe, prev_gate_pe))
                    dve(v, lambda: v.tensor_copy(
                        out=vT_sb[:, tb * TB:(tb + 1) * TB], in_=sc1[:, 0:TB]),
                        waits=wv)
                qk_evac = dve.mark()

            if not ph("vtr"):
                continue
            for tg in range(0, TC, 8):
                n = min(8, TC - tg)
                bk = "A" if (tg // 8) % 2 == 0 else "B"
                slot = chAb if bk == "A" else chBb

                @block.tensor
                def _(t, tg=tg, n=n, slot=slot, bk=bk):
                    w = chw(bk) + [(dve, qk_evac)]
                    for i in range(n):
                        pe(t, lambda i=i: t.transpose(
                            slot(i), vT_sb[:, (tg + i) * 128:(tg + i + 1) * 128],
                            ident_b[:, :]), waits=w if i == 0 else ())
                vtp = pe.mark()

                @block.vector
                def _(v, tg=tg, n=n, slot=slot):
                    for i in range(n):
                        tc = tg + i
                        dve(v, lambda i=i, tc=tc: v.tensor_copy(
                            out=vaug_sb[:, tc * 130:tc * 130 + 64],
                            in_=slot(i)[:, 0:64]),
                            waits=[(pe, vtp)] if i == 0 else ())
                        dve(v, lambda i=i, tc=tc: v.tensor_copy(
                            out=vaug_sb[:, tc * 130 + 65:tc * 130 + 129],
                            in_=slot(i)[:, 64:128]))
                chfree[bk] = (dve, dve.mark())
            vtok_ready = dve.mark()
            at_chain_war[0] = chw("A") + chw("B")

            if not ph("attn"):
                continue
            accA = accv[:, 0:512]
            accB = accv[:, 512:1024]
            E = ep_f
            A0 = 16
            Z0 = A0 + 512
            SQ0 = Z0 + 256
            pstage_war = {0: [], 1: []}
            xq_war = {0: [], 1: []}
            xq_lanes = {0: [], 1: []}
            p_dma = dmas.mark()
            p_dma_q = [0, 0, 0, 0]
            epi = None
            NEARLY = 6
            for qb in range(NQB + 1):
                last = qb == NQB
                if not last:
                    q0 = qb * QB

                    @block.vector
                    def _(v, q0=q0, qb=qb):
                        w = []
                        if at_qblk_war[0]:
                            w.append((pe, at_qblk_war[0]))
                        if qb == 0:
                            w += list(prev_x1_dma)
                        for X in range(4):
                            dve(v, lambda X=X: v.tensor_copy(
                                out=qblk_sb[32 * X:32 * (X + 1),
                                            X * QB:(X + 1) * QB],
                                in_=qT_sb[32 * X:32 * (X + 1), q0:q0 + QB]),
                                waits=w if X == 0 else ())
                    qblk_done = dve.mark()

                if epi is not None:
                    @block.sync
                    def _(s, epi=epi):
                        for qt in range(2):
                            w = list(xq_war[qt])
                            if epi["q0"] == 0:
                                if prev_gelu_m[0]:
                                    w.append((act, prev_gelu_m[0]))
                                if prev_ffn_pe[0]:
                                    w.append((pe, prev_ffn_pe[0]))
                            r = xr + epi["q0"] + qt * 128
                            h = dmas(s, lambda r=r, qt=qt: s.dma_start(
                                out=xq_sb[qt][:, :], in_=x_d[r:r + 128, :]),
                                waits=w)
                            xq_lanes[qt] = h

                    @block.tensor
                    def _(t, epi=epi):
                        w = [(dve, epi["evac"])] + list(at_chain_war[0])
                        for s_ in range(8):
                            X, qt = s_ % 4, s_ // 4
                            co = X * 256 + qt * 128
                            cb = s_ * 65 if s_ < 4 else 512 + (s_ - 4) * 65
                            pe(t, lambda co=co, cb=cb: t.transpose(
                                chain[:, cb:cb + 65], fm_ep[0:65, co:co + 128],
                                ident_f[0:65, 0:65]),
                               waits=w if s_ == 0 else ())
                    at_trans_m[0] = pe.mark()

                sc_done = {}
                if not last:
                    act_base = act.count
                    nmid = 5 if epi is not None else 0

                    def _ep(kc):
                        return act_base + kc + 1 + (nmid if kc >= NEARLY
                                                    else 0)
                    exp_pred = [_ep(kc) for kc in range(KCN)]
                    sqrt_pred = (act_base + NEARLY + 5) if epi is not None \
                        else 0

                    @block.tensor
                    def _(t, qb=qb, qblk_done=qblk_done):
                        for kc in range(KCN):
                            par = kc % 2
                            scs = sc0 if par == 0 else sc1
                            po = par * 4 * QB
                            w = []
                            if at_exp_war[par]:
                                w.append((act, at_exp_war[par]))
                            if kc == 0:
                                w.append((dve, qblk_done))
                                if qb == 0:
                                    w.append((dve, vtok_ready))
                            for half in range(2):
                                pe(t, lambda kc=kc, scs=scs, half=half:
                                   t.matmul(
                                    scs[:, half * 512:(half + 1) * 512],
                                    kT_sb[:, kc * 128:(kc + 1) * 128],
                                    qblk_sb[:, half * 512:(half + 1) * 512],
                                    start=True, stop=True),
                                   waits=w if half == 0 else ())
                            at_exp_war[par] = exp_pred[kc]
                            sc_done[kc] = pe.mark()
                            wv = [(act, exp_pred[kc])]
                            if kc == 0:
                                if at_evac_prev[0]:
                                    wv.append((dve, at_evac_prev[0]))
                                if qb == 0:
                                    wv.append((dve, vtok_ready))
                            pe(t, lambda kc=kc, po=po: t.matmul(
                                accA[0:65, :],
                                vaug_sb[:, kc * 130:kc * 130 + 65],
                                probs_buf[:, po:po + 512],
                                start=(kc == 0), stop=(kc == KCN - 1)),
                               waits=wv)
                            pe(t, lambda kc=kc, po=po: t.matmul(
                                accB[0:65, :],
                                vaug_sb[:, kc * 130 + 65:kc * 130 + 130],
                                probs_buf[:, po + 512:po + 1024],
                                start=(kc == 0), stop=(kc == KCN - 1)))
                    pv_all = pe.mark()

                if epi is not None:
                    zm = [0] * 4

                    @block.vector
                    def _(v, epi=epi, zm=zm):
                        w = [(pe, at_trans_m[0])]
                        ch3a = chain[:, 0:260].rearrange(
                            "p (s c) -> p s c", c=65)
                        ch3b = chain[:, 512:772].rearrange(
                            "p (s c) -> p s c", c=65)
                        dve(v, lambda: v.reciprocal(
                            E[:, 0:4].rearrange("p (s c) -> p s c", c=1),
                            ch3a[:, :, 64:65]), waits=w)
                        dve(v, lambda: v.reciprocal(
                            E[:, 4:8].rearrange("p (s c) -> p s c", c=1),
                            ch3b[:, :, 64:65]))
                        eodd = E[:, 0:8].rearrange(
                            "p (s c) -> p s c", c=2)[:, :, 1:2]
                        dve(v, lambda: v.tensor_scalar(
                            out=eodd, in0=eodd, scalar1=neglam[:, :1],
                            scalar2=None, op0=ALU.mult))
                        for qt in range(2):
                            for head in range(2):
                                s1 = qt * 4 + 2 * head
                                s2 = s1 + 1
                                cb1 = s1 * 65 if s1 < 4 else 512 + (s1 - 4) * 65
                                cb2 = s2 * 65 if s2 < 4 else 512 + (s2 - 4) * 65
                                zi = qt * 2 + head
                                z = E[:, Z0 + zi * 64:Z0 + (zi + 1) * 64]
                                dve(v, lambda cb1=cb1, s1=s1, z=z:
                                    v.tensor_scalar(
                                        out=z, in0=chain[:, cb1:cb1 + 64],
                                        scalar1=E[:, s1:s1 + 1], scalar2=None,
                                        op0=ALU.mult))
                                dve(v, lambda cb2=cb2, s2=s2, z=z:
                                    v.scalar_tensor_tensor(
                                        out=z, in0=chain[:, cb2:cb2 + 64],
                                        scalar=E[:, s2:s2 + 1], in1=z,
                                        op0=ALU.mult, op1=ALU.add))
                                zm[zi] = dve.mark()
                    epi["zm"] = zm
                    epi["rsq"] = dve.mark()

                @block.scalar
                def _(s, epi=epi, sc_done=sc_done, last=last):
                    def emit_exp(kc):
                        par = kc % 2
                        scs = sc0 if par == 0 else sc1
                        po = par * 4 * QB
                        act(s, lambda scs=scs, po=po: s.activation(
                            out=probs_buf[:, po:po + 4 * QB],
                            in_=scs[:, 0:4 * QB], func=AF.Exp,
                            bias=biasc[:, 0:1], scale=biasc[:, 2:3]),
                            waits=[(pe, sc_done[kc])], noser=True)

                    if not last:
                        for kc in range(NEARLY):
                            emit_exp(kc)
                    if epi is not None:
                        for zi in range(4):
                            act(s, lambda zi=zi: s.activation(
                                out=E[:, A0 + zi * 64:A0 + (zi + 1) * 64],
                                in_=E[:, Z0 + zi * 64:Z0 + (zi + 1) * 64],
                                func=AF.Square,
                                accum_out=E[:, 8 + zi:9 + zi]),
                                waits=[(dve, epi["zm"][zi])], noser=True)
                        act(s, lambda: s.activation(
                            out=E[:, 12:16], in_=E[:, 8:12], func=AF.Sqrt,
                            bias=biasc[:, 4:5], scale=biasc[:, 5:6]))
                    if not last:
                        for kc in range(NEARLY, KCN):
                            emit_exp(kc)
                if not last:
                    assert act.count == act_base + KCN + nmid
                    if epi is not None:
                        assert act.count >= sqrt_pred
                else:
                    sqrt_pred = act.count

                if qb == 2 and pending_ln2[0] is not None:
                    pending_ln2[0]()
                    pending_ln2[0] = None

                if epi is not None:
                    @block.vector
                    def _(v, sqrt_pred=sqrt_pred):
                        dve(v, lambda: v.reciprocal(E[:, 12:16], E[:, 12:16]),
                            waits=[(act, sqrt_pred)])
                        for qt in range(2):
                            for head in range(2):
                                zi = qt * 2 + head
                                dve(v, lambda zi=zi, qt=qt, head=head:
                                    v.scalar_tensor_tensor(
                                        out=zct_sb[:, qt * 128 + head * 64:
                                                   qt * 128 + (head + 1) * 64],
                                        in0=E[:, Z0 + zi * 64:
                                              Z0 + (zi + 1) * 64],
                                        scalar=E[:, 12 + zi:13 + zi],
                                        in1=scaletile[:, :],
                                        op0=ALU.mult, op1=ALU.mult))
                    zct_done = dve.mark()

                    @block.tensor
                    def _(t, zct_done=zct_done):
                        for qt in range(2):
                            pe(t, lambda qt=qt: t.transpose(
                                chain_bf[:, qt * 128:(qt + 1) * 128],
                                zct_sb[:, qt * 128:(qt + 1) * 128],
                                ident_b[:, :]),
                               waits=[(dve, zct_done)] if qt == 0 else ())
                    zT_mm = pe.mark()

                    @block.vector
                    def _(v, zT_mm=zT_mm):
                        for qt in range(2):
                            dve(v, lambda qt=qt: v.tensor_copy(
                                out=zcatT_sb[:, qt * 128:(qt + 1) * 128],
                                in_=chain_bf[:, qt * 128:(qt + 1) * 128]),
                                waits=[(pe, zT_mm)] if qt == 0 else ())
                    zcat_done = dve.mark()

                    pstage_evac = 0
                    for qt in range(2):
                        @block.tensor
                        def _(t, qt=qt, zcat_done=zcat_done,
                              pstage_evac=pstage_evac, epi=epi):
                            w = []
                            if qt == 0:
                                w.append((dve, zcat_done))
                                w.append((dve, epi["rsq"]))
                            else:
                                w.append((dve, pstage_evac))
                            pe(t, lambda qt=qt: t.matmul(
                                chA[:, 0:512],
                                zcatT_sb[:, qt * 128:(qt + 1) * 128],
                                wproj_sb[:, 0:512], start=True, stop=True),
                               waits=w)
                            pe(t, lambda qt=qt: t.matmul(
                                chB[:, 0:512],
                                zcatT_sb[:, qt * 128:(qt + 1) * 128],
                                wproj_sb[:, 512:1024], start=True, stop=True))
                        pmm = pe.mark()

                        @block.vector
                        def _(v, qt=qt, pmm=pmm):
                            w = ([(pe, pmm)] + list(pstage_war[qt])
                                 + list(xq_lanes[qt]))
                            dve(v, lambda qt=qt: v.scalar_tensor_tensor(
                                out=pstage2[qt][:, 0:512],
                                in0=xq_sb[qt][:, 0:512],
                                scalar=1.0 / cfg.cores,
                                in1=chA[:, 0:512],
                                op0=ALU.mult, op1=ALU.add), waits=w)
                            dve(v, lambda qt=qt: v.scalar_tensor_tensor(
                                out=pstage2[qt][:, 512:1024],
                                in0=xq_sb[qt][:, 512:1024],
                                scalar=1.0 / cfg.cores,
                                in1=chB[:, 0:512],
                                op0=ALU.mult, op1=ALU.add))
                        pstage_evac = dve.mark()
                        xq_war[qt] = [(dve, pstage_evac)]

                        @block.sync
                        def _(s, qt=qt, epi=epi, pstage_evac=pstage_evac):
                            r = xr + epi["q0"] + qt * 128
                            h = dmas(s, lambda r=r, qt=qt: s.dma_start(
                                out=ppart_d[r:r + 128, :],
                                in_=pstage2[qt][:, :]),
                                waits=[(dve, pstage_evac)])
                            pstage_war[qt] = h
                        p_dma = dmas.mark()
                    pq = epi["q0"] // QB
                    if pq % 4 == 3:
                        p_dma_q[pq // 4] = p_dma
                    at_chain_war[0] = [(dve, pstage_evac)]
                    chfree["A"] = (dve, pstage_evac)
                    chfree["B"] = (dve, pstage_evac)

                if not last:
                    @block.vector
                    def _(v, pv_all=pv_all):
                        w = [(pe, pv_all)]
                        if at_trans_m[0]:
                            w.append((pe, at_trans_m[0]))
                        dve(v, lambda: v.tensor_copy(
                            out=fm_ep[0:65, 0:512], in_=accA[0:65, :]),
                            waits=w)
                        dve(v, lambda: v.tensor_copy(
                            out=fm_ep[0:65, 512:1024], in_=accB[0:65, :]))
                    at_evac_prev[0] = dve.mark()
                    epi = {"evac": at_evac_prev[0], "q0": q0}
                    at_qblk_war[0] = pe.mark()
                else:
                    epi = None

            if not ph("ar"):
                continue
            arq = []

            @block.gpsimd
            def _(g):
                QR = T // 4
                for q in range(4):
                    dmas.wait(g, p_dma_q[q])
                    ccx(g, lambda q=q, QR=QR: g.collective_compute(
                        "AllReduce", ALU.add, replica_groups=groups,
                        ins=[ppart_d[xr + q * QR:xr + (q + 1) * QR, :]],
                        outs=[psum_d[xr + q * QR:xr + (q + 1) * QR, :]]))
                    arq.append(ccx.mark())

            if not ph("ln1"):
                continue
            scale_war = {0: 0, 1: 0}
            x1war = {0: [], 1: []}
            stats_m = [0] * TC
            ln_base = act.count
            x18slot = [ep_f[:, 0:D], fm_ep[:, 0:D]]

            def ln1_scale(tc):
                par = tc % 2
                so = par * 8
                Y = ybuf[:, par * D:(par + 1) * D]

                @block.vector
                def _(v, tc=tc, par=par, so=so, Y=Y):
                    dve(v, lambda: v.reciprocal(
                        stat_f[:, so + 5:so + 6], stat_f[:, so + 5:so + 6]),
                        waits=[(act, ln_base + tc + 1),
                               (pe, at_trans_m[0])])
                    dve(v, lambda: v.tensor_scalar(
                        out=x1buf[:, par * D:(par + 1) * D], in0=Y,
                        scalar1=stat_f[:, so + 2:so + 3],
                        scalar2=stat_f[:, so + 5:so + 6],
                        op0=ALU.subtract, op1=ALU.mult))
                    dve(v, lambda: v.tensor_scalar(
                        out=x18slot[par], in0=x1buf[:, par * D:(par + 1) * D],
                        scalar1=1.0 / cfg.cores, scalar2=None, op0=ALU.mult))
                scale_war[par] = dve.mark()

                @block.sync
                def _(s, tc=tc, par=par):
                    r = tc * 128
                    h1 = dmas(s, lambda r=r: s.dma_start(
                        out=x1_ds[beta][r:r + 128, :],
                        in_=x1buf[:, par * D:(par + 1) * D]),
                        waits=[(dve, scale_war[par])])
                    h2 = dmas(s, lambda r=r, par=par: s.dma_start(
                        out=opart_ds[beta][r:r + 128, :], in_=x18slot[par]))
                    x1war[par] = h1 + h2

            for tc in range(TC):
                par = tc % 2
                yo = par * D
                so = par * 8

                @block.sync
                def _(s, tc=tc, yo=yo, par=par):
                    w = [(ccx, arq[tc // 8])]
                    if scale_war[par]:
                        w.append((dve, scale_war[par]))
                    if tc < 2:
                        w += list(out_st_h[0])
                    dmas(s, lambda: s.dma_start(
                        out=ybuf[:, yo:yo + D],
                        in_=psum_d[xr + tc * 128:xr + (tc + 1) * 128, :]),
                        waits=w)
                xl = dmas.mark()

                @block.vector
                def _(v, yo=yo, so=so, par=par, xl=xl):
                    Y = ybuf[:, yo:yo + D]
                    xb = x1buf[:, par * D:(par + 1) * D]
                    dve(v, lambda: v.tensor_scalar(
                        out=xb, in0=Y, scalar1=1.0, scalar2=None,
                        op0=ALU.mult, op1=ALU.add,
                        accum_out=stat_f[:, so + 0:so + 1]),
                        waits=[xl] + list(x1war[par]))
                    dve(v, lambda: v.scalar_tensor_tensor(
                        out=xb, in0=Y, scalar=1.0, in1=Y,
                        op0=ALU.mult, op1=ALU.mult,
                        accum_out=stat_f[:, so + 1:so + 2]))
                    dve(v, lambda: v.tensor_scalar(
                        out=stat_f[:, so + 2:so + 3],
                        in0=stat_f[:, so + 0:so + 1],
                        scalar1=1.0 / D, scalar2=None, op0=ALU.mult))
                    dve(v, lambda: v.tensor_mul(
                        stat_f[:, so + 3:so + 4], stat_f[:, so + 2:so + 3],
                        stat_f[:, so + 2:so + 3]))
                    dve(v, lambda: v.scalar_tensor_tensor(
                        out=stat_f[:, so + 4:so + 5],
                        in0=stat_f[:, so + 1:so + 2], scalar=1.0 / D,
                        in1=stat_f[:, so + 3:so + 4],
                        op0=ALU.mult, op1=ALU.subtract))
                    dve(v, lambda: v.tensor_scalar(
                        out=stat_f[:, so + 4:so + 5],
                        in0=stat_f[:, so + 4:so + 5],
                        scalar1=1e-5, scalar2=None, op0=ALU.add))
                stats_m[tc] = dve.mark()
                if tc >= 1:
                    ln1_scale(tc - 1)
            ln1_scale(TC - 1)

            @block.scalar
            def _(s):
                for tc in range(TC):
                    so = (tc % 2) * 8
                    act(s, lambda so=so, tc=tc: s.activation(
                        out=stat_f[:, so + 5:so + 6],
                        in_=stat_f[:, so + 4:so + 5], func=AF.Sqrt),
                        waits=[(dve, stats_m[tc])], noser=True)
            assert act.count == ln_base + TC
            x1_dma = dmas.mark()
            prev_x1_dma = x1_dma

            if not ph("x1bf"):
                continue

            @block.gpsimd
            def _(g):
                dmag(g, lambda: g.dma_start(
                    out=x1bf_d[xr:xr + T, :], in_=x1_ds[beta][:, :]),
                    waits=[x1_dma])
            x1bf_done = dmag.mark()

            if not ph("gate"):
                continue
            x1fm_war = {0: 0}
            for dc in range(DC):
                par = 0
                fo = 0

                @block.sync
                def _(s, dc=dc, fo=fo, par=par):
                    w = [x1bf_done]
                    if x1fm_war[par]:
                        w.append((pe, x1fm_war[par]))
                    dmas(s, lambda: s.dma_start(
                        out=x1fm_buf[:, fo:fo + T],
                        in_=x1bf_d[xr:xr + T, dc * 128:(dc + 1) * 128],
                        transpose=True), waits=w)
                x1fm_ld = dmas.mark()

                for ft in range(T // 512):
                    bk = "A" if ft % 2 == 0 else "B"
                    dst = chA if ft % 2 == 0 else chB

                    @block.tensor
                    def _(t, dc=dc, ft=ft, fo=fo, dst=dst, bk=bk,
                          x1fm_ld=x1fm_ld):
                        w = [x1fm_ld] + chw(bk)
                        pe(t, lambda: t.matmul(
                            dst[0:NE, 0:512],
                            gatew_sb[:, dc * NE:(dc + 1) * NE],
                            x1fm_buf[:, fo + ft * 512:fo + (ft + 1) * 512],
                            start=True, stop=True), waits=w)
                    gmm = pe.mark()

                    @block.vector
                    def _(v, dc=dc, ft=ft, dst=dst, gmm=gmm):
                        if dc == 0:
                            dve(v, lambda: v.tensor_copy(
                                out=gacc[:, ft * 512:(ft + 1) * 512],
                                in_=dst[0:NE, 0:512]), waits=[(pe, gmm)])
                        else:
                            dve(v, lambda: v.tensor_add(
                                gacc[:, ft * 512:(ft + 1) * 512],
                                gacc[:, ft * 512:(ft + 1) * 512],
                                dst[0:NE, 0:512]), waits=[(pe, gmm)])
                    chfree[bk] = (dve, dve.mark())
                x1fm_war[par] = pe.mark()
            prev_gate_pe = pe.mark()
            gate_done = dve.mark()

            for tg in range(0, TC, 8):
                n = min(8, TC - tg)
                bk = "A" if (tg // 8) % 2 == 0 else "B"
                base = 0 if bk == "A" else 512

                @block.tensor
                def _(t, tg=tg, n=n, base=base, bk=bk):
                    w = [(dve, gate_done)] + chw(bk)
                    for i in range(n):
                        pe(t, lambda i=i: t.transpose(
                            chain[:, base + i * 8:base + i * 8 + NE],
                            gacc[:, (tg + i) * 128:(tg + i + 1) * 128],
                            ident_f[0:NE, 0:NE]), waits=w if i == 0 else ())
                gt = pe.mark()

                @block.vector
                def _(v, tg=tg, n=n, base=base, gt=gt):
                    for i in range(n):
                        dve(v, lambda i=i: v.tensor_copy(
                            out=gtok[:, (tg + i) * NE:(tg + i + 1) * NE],
                            in_=chain[:, base + i * 8:base + i * 8 + NE]),
                            waits=[(pe, gt)] if i == 0 else ())
                chfree[bk] = (dve, dve.mark())
            gtok_done = dve.mark()

            if not ph("sm"):
                continue
            @block.scalar
            def _(s):
                act(s, lambda: s.activation(out=probs_tok[:, :], in_=gtok[:, :],
                                            func=AF.Exp),
                    waits=[(dve, gtok_done)], noser=True)
            sm_exp = act.mark()

            @block.vector
            def _(v):
                pt3 = probs_tok[:, :].rearrange("p (tc e) -> p tc e", e=NE)
                dve(v, lambda: v.reduce_sum(ssum[:, :], pt3, axis=AX.X),
                    waits=[(act, sm_exp)])
                dve(v, lambda: v.reciprocal(ssum[:, :], ssum[:, :]))
                for e in range(NE):
                    dve(v, lambda e=e: v.tensor_mul(
                        pt3[:, :, e], pt3[:, :, e], ssum[:, :]))
                dve(v, lambda: v.memset(tau_lo[:, :], 0.0))
                dve(v, lambda: v.memset(tau_hi[:, :], 1.0))
            probs_done = dve.mark()

            def count_pass(tau_src, out_mask):
                @block.vector
                def _(v):
                    dve(v, lambda: v.tensor_scalar(
                        out=tau_m8[:, :], in0=ind_em[:, :],
                        scalar1=tau_src[0:NE, 0:1], scalar2=None, op0=ALU.mult))
                tm = dve.mark()

                @block.tensor
                def _(t, tm=tm):
                    w = [(dve, tm)] + chw("A")
                    pe(t, lambda: t.matmul(chA[:, 0:TC * NE], ones8_f[:, :],
                                           tau_m8[:, :], start=True, stop=True),
                       waits=w)
                taumm = pe.mark()

                @block.vector
                def _(v, taumm=taumm):
                    dve(v, lambda: v.tensor_tensor(
                        out=out_mask[:, :], in0=probs_tok[:, :],
                        in1=chA[:, 0:TC * NE], op=ALU.is_gt),
                        waits=[(pe, taumm)])
                    om3 = out_mask[:, :].rearrange("p (tc e) -> p e tc", e=NE)
                    dve(v, lambda: v.reduce_sum(part8[:, :], om3, axis=AX.X))
                chfree["A"] = (dve, dve.mark())
                pm = dve.mark()

                @block.tensor
                def _(t, pm=pm):
                    w = [(dve, pm)] + chw("B")
                    pe(t, lambda: t.matmul(chB[0:NE, 0:1], part8[:, :],
                                           onescol_f[:, 0:1],
                                           start=True, stop=True), waits=w)
                cmm = pe.mark()

                @block.vector
                def _(v, cmm=cmm):
                    dve(v, lambda: v.tensor_copy(out=cnt8[:, :],
                                                 in_=chB[0:NE, 0:1]),
                        waits=[(pe, cmm)])
                chfree["B"] = (dve, dve.mark())

            if not ph("bis"):
                continue
            TN = TC * NE
            cmp_x = x1buf[:, 0:3 * TN]
            tau_m8x = x1buf[0:NE, 4 * TN:7 * TN]
            part8x = x1buf[:, 7 * TN:7 * TN + 3 * NE]
            delta = tau_hi
            for it in range(18):
                @block.vector
                def _(v):
                    dve(v, lambda: v.tensor_scalar(
                        out=delta[:, :], in0=delta[:, :], scalar1=0.25,
                        scalar2=None, op0=ALU.mult))
                    dve(v, lambda: v.tensor_add(bs_sc[:, 0:1], tau_lo[:, :],
                                                delta[:, :]))
                    dve(v, lambda: v.tensor_add(bs_sc[:, 1:2], bs_sc[:, 0:1],
                                                delta[:, :]))
                    dve(v, lambda: v.tensor_add(bs_sc[:, 2:3], bs_sc[:, 1:2],
                                                delta[:, :]))
                    for j in range(3):
                        dve(v, lambda j=j: v.tensor_scalar(
                            out=tau_m8x[:, j * TN:(j + 1) * TN],
                            in0=ind_em[:, :], scalar1=bs_sc[:, j:j + 1],
                            scalar2=None, op0=ALU.mult))
                tm = dve.mark()

                @block.tensor
                def _(t, tm=tm):
                    w = [(dve, tm)] + chw("A") + chw("B")
                    pe(t, lambda: t.matmul(chA[:, 0:512], ones8_f[:, :],
                                           tau_m8x[:, 0:512],
                                           start=True, stop=True), waits=w)
                    pe(t, lambda: t.matmul(chB[:, 0:256], ones8_f[:, :],
                                           tau_m8x[:, 512:768],
                                           start=True, stop=True))
                taumm = pe.mark()

                @block.vector
                def _(v, taumm=taumm):
                    for j in range(3):
                        src = (chA[:, j * TN:(j + 1) * TN] if j < 2
                               else chB[:, 0:TN])
                        dve(v, lambda j=j, src=src: v.tensor_tensor(
                            out=cmp_x[:, j * TN:(j + 1) * TN],
                            in0=probs_tok[:, :], in1=src, op=ALU.is_gt),
                            waits=[(pe, taumm)] if j == 0 else ())
                    cm4 = cmp_x.rearrange("p (j tc e) -> p j e tc", e=NE,
                                          tc=TC)
                    dve(v, lambda: v.reduce_sum(part8x[:, :], cm4, axis=AX.X))
                pm = dve.mark()

                @block.tensor
                def _(t, pm=pm):
                    for j in range(3):
                        pe(t, lambda j=j: t.matmul(
                            chB[0:NE, 300 + j:301 + j],
                            part8x[:, j * NE:(j + 1) * NE],
                            onescol_f[:, 0:1], start=True, stop=True),
                           waits=[(dve, pm)] if j == 0 else ())
                cmm = pe.mark()

                @block.vector
                def _(v, cmm=cmm):
                    dve(v, lambda: v.tensor_scalar(
                        out=bs_sc[:, 4:7], in0=chB[0:NE, 300:303],
                        scalar1=float(K), scalar2=None, op0=ALU.is_ge),
                        waits=[(pe, cmm)])
                    dve(v, lambda: v.reduce_sum(
                        bs_sc[:, 7:8], bs_sc[:, 4:7], axis=AX.X))
                    dve(v, lambda: v.scalar_tensor_tensor(
                        out=tau_lo[:, :], in0=bs_sc[:, 7:8],
                        scalar=delta[:, 0:1], in1=tau_lo[:, :],
                        op0=ALU.mult, op1=ALU.add))
            @block.vector
            def _(v):
                dve(v, lambda: v.tensor_add(tau_hi[:, :], tau_lo[:, :],
                                            delta[:, :]))

            if not ph("mask"):
                continue
            count_pass(tau_hi, hi_tok)

            @block.vector
            def _(v):
                dve(v, lambda: v.tensor_scalar(
                    out=fill8[:, :], in0=cnt8[:, :], scalar1=-1.0,
                    scalar2=float(K), op0=ALU.mult, op1=ALU.add))
            count_pass(tau_lo, tie_tok)

            @block.vector
            def _(v):
                dve(v, lambda: v.tensor_sub(tie_tok[:, :], tie_tok[:, :],
                                            hi_tok[:, :]))
            masks_done = dve.mark()

            for tg in range(0, TC, 2):
                n = min(2, TC - tg)
                bk = "A" if (tg // 2) % 2 == 0 else "B"
                base = 0 if bk == "A" else 512

                @block.tensor
                def _(t, tg=tg, n=n, base=base, bk=bk):
                    w = [(dve, masks_done)] + chw(bk)
                    for i in range(n):
                        pe(t, lambda i=i: t.transpose(
                            chain[0:NE, base + (2 * i) * 128:
                                  base + (2 * i) * 128 + 128],
                            hi_tok[:, (tg + i) * NE:(tg + i + 1) * NE],
                            ident_f[:, :]), waits=w if i == 0 else ())
                        pe(t, lambda i=i: t.transpose(
                            chain[0:NE, base + (2 * i + 1) * 128:
                                  base + (2 * i + 1) * 128 + 128],
                            tie_tok[:, (tg + i) * NE:(tg + i + 1) * NE],
                            ident_f[:, :]))
                ht = pe.mark()

                @block.vector
                def _(v, tg=tg, n=n, base=base, ht=ht):
                    for i in range(n):
                        w = [(pe, ht)] + list(out_st_h[0])
                        dve(v, lambda i=i: v.tensor_copy(
                            out=hi_b[0:NE, (tg + i) * 128:(tg + i + 1) * 128],
                            in_=chain[0:NE, base + (2 * i) * 128:
                                      base + (2 * i) * 128 + 128]),
                            waits=w if i == 0 else ())
                        dve(v, lambda i=i: v.tensor_copy(
                            out=tie_b[0:NE, (tg + i) * 128:(tg + i + 1) * 128],
                            in_=chain[0:NE, base + (2 * i + 1) * 128:
                                      base + (2 * i + 1) * 128 + 128]))
                chfree[bk] = (dve, dve.mark())
            em_done = dve.mark()

            if not ph("sel"):
                continue
            @block.vector
            def _(v):
                HIe = hi_b[0:NE, :]
                TIe = tie_b[0:NE, :]
                RNK = rank_f[0:NE, :]
                dve(v, lambda: v.tensor_tensor_scan(
                    out=RNK, data0=TIe, data1=TIe, initial=0.0,
                    op0=ALU.add, op1=ALU.bypass), waits=[(dve, em_done)])
                dve(v, lambda: v.tensor_scalar(
                    out=RNK, in0=RNK,
                    scalar1=fill8[0:NE, 0:1], scalar2=None, op0=ALU.is_le))
                dve(v, lambda: v.tensor_mul(TIe, TIe, RNK))
                dve(v, lambda: v.tensor_add(TIe, TIe, HIe))
                dve(v, lambda: v.tensor_tensor_scan(
                    out=RNK, data0=TIe, data1=TIe, initial=0.0,
                    op0=ALU.add, op1=ALU.bypass))
                dve(v, lambda: v.tensor_scalar(
                    out=RNK, in0=RNK, scalar1=-1.0,
                    scalar2=None, op0=ALU.add))
            gs_done = dve.mark()

            for tg in range(0, TC, 8):
                n = min(8, TC - tg)
                bk = "A" if (tg // 8) % 2 == 0 else "B"
                base = 0 if bk == "A" else 512

                @block.tensor
                def _(t, tg=tg, n=n, base=base, bk=bk):
                    w = [(dve, gs_done)] + chw(bk)
                    for i in range(n):
                        pe(t, lambda i=i: t.transpose(
                            chain[:, base + i * 2:base + i * 2 + 1],
                            rank_f[0:1, (tg + i) * 128:(tg + i + 1) * 128],
                            ident_f[0:1, 0:1]), waits=w if i == 0 else ())
                        pe(t, lambda i=i: t.transpose(
                            chain_bf[:, 2 * (base + i * 2 + 1):
                                     2 * (base + i * 2 + 1) + 1],
                            tie_b[0:1, (tg + i) * 128:(tg + i + 1) * 128],
                            ident_b[0:1, 0:1]))
                gst = pe.mark()

                @block.vector
                def _(v, tg=tg, n=n, base=base, gst=gst):
                    for i in range(n):
                        dve(v, lambda i=i: v.tensor_copy(
                            out=gs_tok[:, (tg + i) * 2:(tg + i) * 2 + 1],
                            in_=chain[:, base + i * 2:base + i * 2 + 1]),
                            waits=[(pe, gst)] if i == 0 else ())
                        dve(v, lambda i=i: v.tensor_copy(
                            out=gs_tok[:, (tg + i) * 2 + 1:(tg + i) * 2 + 2],
                            in_=chain_bf[:, 2 * (base + i * 2 + 1):
                                         2 * (base + i * 2 + 1) + 1]))
                chfree[bk] = (dve, dve.mark())
            gstok_done = dve.mark()

            if not ph("ex"):
                continue
            if beta > 0:
                @block.gpsimd
                def _(g):
                    pool(g, lambda: g.iota(
                        iota_j[:, :], pattern=[[1, K]], base=0,
                        channel_multiplier=0,
                        allow_small_or_imprecise_dtypes=True),
                        waits=[(pe, prev_ffn_pe[0])])
            iota_m = pool.mark()
            ex_prev = pe.mark()
            for tc in range(TC):
                @block.vector
                def _(v, tc=tc, ex_prev=ex_prev):
                    w = [(pe, ex_prev)]
                    if tc == 0:
                        w.append((dve, gstok_done))
                        w.append((pool, iota_m))
                    dve(v, lambda: v.tensor_copy(out=payload[:, 0:1],
                                                 in_=iota_colb[:, :]), waits=w)
                    dve(v, lambda: v.tensor_copy(out=payload[:, 1:2],
                                                 in_=chunkv_b[:, tc:tc + 1]))
                    dve(v, lambda: v.tensor_copy(
                        out=payload[:, 2:3],
                        in_=probs_tok[:, tc * NE:tc * NE + 1]))
                    dve(v, lambda: v.tensor_scalar(
                        out=pc_b[:, :], in0=iota_j[:, :],
                        scalar1=gs_tok[:, 2 * tc:2 * tc + 1],
                        scalar2=gs_tok[:, 2 * tc + 1:2 * tc + 2],
                        op0=ALU.is_equal, op1=ALU.mult))
                pcm = dve.mark()

                @block.tensor
                def _(t, tc=tc, pcm=pcm):
                    w = [(dve, pcm)]
                    if tc == 0:
                        w += chw("A") + chw("B")
                    for kf in range((K + 511) // 512):
                        f0, f1 = kf * 512, min(K, kf * 512 + 512)
                        pe(t, lambda f0=f0, f1=f1: t.matmul(
                            chain[0:3, f0:f1], payload[:, :], pc_b[:, f0:f1],
                            start=(tc == 0), stop=(tc == TC - 1)),
                           waits=w if kf == 0 else ())
                ex_prev = pe.mark()
            ex_mm = pe.mark()

            @block.vector
            def _(v):
                dve(v, lambda: v.tensor_copy(out=mg_f[:, :],
                                             in_=chain[0:3, 0:K]),
                    waits=[(pe, ex_mm)])
            chfree["A"] = (dve, dve.mark())
            chfree["B"] = (dve, dve.mark())
            mg_done = dve.mark()

            if not ph("mg"):
                continue
            prev_ffn = prev_ffn_dma if beta > 0 else []
            for ks in range(KC):
                bk = "A" if ks % 2 == 0 else "B"
                base = 0 if bk == "A" else 512

                @block.tensor
                def _(t, ks=ks, base=base, bk=bk):
                    w = [(dve, mg_done)] + chw(bk)
                    pe(t, lambda: t.transpose(
                        chain_bf[:, 2 * base:2 * base + 3],
                        mg_f[0:3, ks * 128:(ks + 1) * 128],
                        ident_b[0:3, 0:3]), waits=w)
                mt = pe.mark()

                @block.vector
                def _(v, ks=ks, base=base, mt=mt):
                    dve(v, lambda: v.tensor_copy(
                        out=mg3[:, 0:3], in_=chain_bf[:, 2 * base:2 * base + 3]),
                        waits=[(pe, mt)] + list(prev_ffn))
                    dve(v, lambda: v.scalar_tensor_tensor(
                        out=mcols[:, ks:ks + 1], in0=mg3[:, 1:2],
                        scalar=128.0, in1=mg3[:, 0:1],
                        op0=ALU.mult, op1=ALU.add))
                    dve(v, lambda: v.tensor_copy(out=gcol[:, ks:ks + 1],
                                                 in_=mg3[:, 2:3]))
                chfree[bk] = (dve, dve.mark())

            @block.vector
            def _(v):
                dve(v, lambda: v.tensor_copy(out=midx[:, :], in_=mcols[:, :]))
            midx_done = dve.mark()

            if not ph("ffn"):
                continue
            NG = K // 128
            NGH = NG // 2
            accpair = [sc1, accv]
            acc_war = {0: 0, 1: 0}
            scat_h = []
            scat_mark = dmag.mark()
            xin_war = [0]
            mm2_last = 0
            for hf in range(2):
                xinT_done = 0
                for g in range(NGH):
                    par = g % 2
                    col = hf * NGH + g

                    @block.gpsimd
                    def _(gg, g=g, par=par, col=col):
                        w = [(dve, midx_done), x1_dma]
                        if xin_war[0]:
                            w.append((pe, xin_war[0]))
                        dmag(gg, lambda: gg.indirect_dma_start(
                            out=xin_sb[:, par * D:(par + 1) * D],
                            out_offset=None, in_=x1_ds[beta],
                            in_offset=IndirectOffsetOnAxis(
                                ap=midx[:, col:col + 1], axis=0)), waits=w)
                    gth = dmag.mark()

                    bk = "A" if par == 0 else "B"
                    slot = chAb if par == 0 else chBb

                    @block.tensor
                    def _(t, par=par, slot=slot, bk=bk, gth=gth):
                        w = [gth] + chw(bk)
                        if mm2_last:
                            w.append((pe, mm2_last))
                        for dc in range(DC):
                            pe(t, lambda dc=dc, par=par: t.transpose(
                                slot(dc),
                                xin_sb[:, par * D + dc * 128:
                                       par * D + (dc + 1) * 128],
                                ident_b[:, :]), waits=w if dc == 0 else ())
                    xt = pe.mark()
                    if g >= 1:
                        xin_war[0] = xt

                    @block.vector
                    def _(v, g=g, slot=slot, xt=xt):
                        for dc in range(DC):
                            dve(v, lambda dc=dc: v.tensor_copy(
                                out=xinT_sb[:, dc * 512 + g * 128:
                                            dc * 512 + (g + 1) * 128],
                                in_=slot(dc)),
                                waits=[(pe, xt)] if dc == 0 else ())
                    chfree[bk] = (dve, dve.mark())
                xinT_done = dve.mark()
                xin_war[0] = pe.mark()

                pe_base = pe.count
                w1_lds = []
                for hc in range(HC):
                    @block.gpsimd
                    def _(gg, hc=hc):
                        w = [setup_dmag] if (beta == 0 and hf == 0) else []
                        if hc >= 2:
                            w.append((pe, pe_base + (hc - 1) * DC))
                        dmag(gg, lambda hc=hc: gg.dma_start(
                            out=w1_buf[:, (hc % 2) * D:(hc % 2) * D + D],
                            in_=w1bf_d[:, hc * D:(hc + 1) * D]), waits=w)
                    w1_lds.append(dmag.mark())

                gelu_base = act.count
                mm1_m = []
                @block.tensor
                def _(t, xinT_done=xinT_done, gelu_base=gelu_base,
                      w1_lds=w1_lds):
                    for hc in range(HC):
                        par = hc % 2
                        w = [w1_lds[hc]]
                        if hc == 0:
                            w.append((dve, xinT_done))
                        if hc >= 2:
                            w.append((act, gelu_base + hc - 1))
                        for dc in range(DC):
                            pe(t, lambda hc=hc, par=par, dc=dc: t.matmul(
                                sc0[:, par * 512:(par + 1) * 512],
                                w1_buf[:, (hc % 2) * D + dc * 128:
                                       (hc % 2) * D + (dc + 1) * 128],
                                xinT_sb[:, dc * 512:(dc + 1) * 512],
                                start=(dc == 0), stop=(dc == DC - 1)),
                               waits=w if dc == 0 else ())
                        mm1_m.append(pe.mark())
                assert pe.count == pe_base + HC * DC

                @block.scalar
                def _(s, mm1_m=mm1_m):
                    for hc in range(HC):
                        par = hc % 2
                        act(s, lambda hc=hc, par=par: s.activation(
                            out=hT_sb[:, hc * 512:(hc + 1) * 512],
                            in_=sc0[:, par * 512:(par + 1) * 512],
                            func=AF.Gelu_apprx_tanh),
                            waits=[(pe, mm1_m[hc])], noser=True)
                assert act.count == gelu_base + HC
                gelu_mk = [gelu_base + hc + 1 for hc in range(HC)]

                for g in range(NGH):
                    col = hf * NGH + g
                    par = g % 2
                    acc = accpair[par]

                    @block.tensor
                    def _(t, g=g, par=par, acc=acc, gelu_mk=gelu_mk):
                        for hc in range(HC):
                            w = [(act, gelu_mk[hc])]
                            if hc == 0 and acc_war[par]:
                                w.append((dve, acc_war[par]))
                            if hc == 0 and hf == 0 and g < 2:
                                w.append((dve, at_evac_prev[0]))
                            for dh in range(2):
                                pe(t, lambda hc=hc, g=g, dh=dh, acc=acc:
                                   t.matmul(
                                    acc[:, dh * 512:(dh + 1) * 512],
                                    hT_sb[:, hc * 512 + g * 128:
                                          hc * 512 + (g + 1) * 128],
                                    w2_sb[:, hc * D + dh * 512:
                                          hc * D + dh * 512 + 512],
                                    start=(hc == 0), stop=(hc == HC - 1)),
                                   waits=w if (hc == 0 and dh == 0) else ())
                    fmm = pe.mark()
                    mm2_last = fmm

                    @block.vector
                    def _(v, col=col, acc=acc, fmm=fmm):
                        dve(v, lambda: v.tensor_scalar(
                            out=fw_sb[:, :], in0=acc[:, :],
                            scalar1=gcol[:, col:col + 1],
                            scalar2=None, op0=ALU.mult),
                            waits=[(pe, fmm)] + list(scat_h))
                    acc_war[par] = dve.mark()
                    fw_done = dve.mark()

                    @block.gpsimd
                    def _(gg, col=col, fw_done=fw_done, scat_mark=scat_mark):
                        h = dmag(gg, lambda: gg.indirect_dma_start(
                            out=opart_ds[beta],
                            out_offset=IndirectOffsetOnAxis(
                                ap=midx[:, col:col + 1], axis=0),
                            in_=fw_sb[:, :], in_offset=None,
                            compute_op=ALU.add),
                            waits=[(dve, fw_done), x1_dma, scat_mark])
                        scat_h.clear()
                        scat_h.extend(h)
                    scat_mark = dmag.mark()

            at_evac_prev[0] = acc_war[1]
            ffn_acc_last[0] = max(acc_war[0], acc_war[1])
            prev_gelu_m[0] = act.count
            prev_ffn_pe[0] = pe.mark()
            prev_ffn_dma = dmag.mark()
            if not ph("rs"):
                continue

            @block.gpsimd
            def _(g, scat_mark=scat_mark):
                dmag.wait(g, scat_mark)
                ccx(g, lambda: g.collective_compute(
                    "ReduceScatter", ALU.add, replica_groups=groups,
                    ins=[opart_ds[beta][:, :]],
                    outs=[ored_d[beta * OUTR:(beta + 1) * OUTR, :]]))
            rs_done = ccx.mark()

            if not ph("ln2"):
                continue

            def emit_ln2(beta=beta, rs_done=rs_done):
                ln2_body(beta, rs_done)

            if beta == NB - 1:
                emit_ln2()
            else:
                pending_ln2[0] = emit_ln2

        if dbg_tensors:
            dbg_map = {"ppart": ppart_d, "psumd": psum_d, "xbf": xbf_d,
                       "x1bf": x1bf_d, "ored": ored_d}
            for b in range(NB):
                dbg_map[f"x1tok{b}"] = x1_ds[b]
                dbg_map[f"opart{b}"] = opart_ds[b]

            @block.gpsimd
            def _(g):
                dmas.wait(g)
                dmag.wait(g)
                ccx.wait(g)
                for nm in dbg_tensors:
                    src = dbg_map[nm]
                    dst = nc.dram_tensor(
                        "dbg_" + nm, list(src.shape),
                        src.dtype, kind="ExternalOutput").ap()
                    dmag(g, lambda src=src, dst=dst: g.dma_start(
                        out=dst, in_=src))
                dmag.wait(g)

        @block.sync
        def _(s):
            dmas.wait(s)

        @block.gpsimd
        def _(g):
            dmag.wait(g)
            ccx.wait(g)
            ln2st.wait(g)

    return nc


def make_inmaps(cfg: Cfg, x, Wqkv, Wproj, gate_w, w1, w2, lq1, lk1, lq2, lk2,
                rms_scale):
    T, D, NB, NE, HC = cfg.T, cfg.D, cfg.NB, cfg.NE, cfg.HC
    xf = np.ascontiguousarray(np.asarray(x).reshape(NB * T, D), dtype=np.float32)
    xts = np.ascontiguousarray(
        xf.reshape(NB * T // cfg.TB, cfg.TB, cfg.DC, 128)
        .transpose(3, 0, 2, 1).reshape(128, NB * T * cfg.DC),
        dtype=np.float32)
    lamv = np.ascontiguousarray(
        np.concatenate([lq1, lq2, lk1, lk2])[None, :], dtype=np.float32)
    rmsrow = np.ascontiguousarray(
        np.concatenate([rms_scale, rms_scale])[None, :], dtype=np.float32)
    in_maps = []
    for c in range(cfg.cores):
        A, B = 2 * c, 2 * c + 1
        qcols = []
        for hh in (A, B):
            qcols += list(range(32 * hh, 32 * hh + 32))
            qcols += list(range(D // 2 + 32 * hh, D // 2 + 32 * hh + 32))
        kcols = [D + cc for cc in qcols]
        vcols = []
        for hh in (A, B):
            vcols += list(range(2 * D + 64 * hh, 2 * D + 64 * hh + 64))
        wq = np.ascontiguousarray(Wqkv[:, qcols + kcols + vcols], dtype=np.float32)
        wp = np.ascontiguousarray(Wproj[128 * c:128 * (c + 1), :], dtype=np.float32)
        perm = [c] + [e for e in range(NE) if e != c]
        gw = np.ascontiguousarray(gate_w[:, perm], dtype=np.float32)
        w1c = np.ascontiguousarray(
            np.asarray(w1)[c].reshape(cfg.DC, 128, HC, 128)
            .transpose(1, 2, 0, 3).reshape(128, HC * cfg.DC * 128),
            dtype=np.float32)
        w2c = np.ascontiguousarray(np.asarray(w2)[c], dtype=np.float32)
        in_maps.append({
            "x": xf, "xts": xts, "wqkv": wq, "wproj": wp, "gatew": gw,
            "w1": w1c, "w2": w2c, "lamv": lamv, "rmsrow": rmsrow,
        })
    return in_maps


def assemble(cfg: Cfg, results):
    out = np.empty((cfg.NB, cfg.T, cfg.D), dtype=np.float32)
    for c in range(cfg.cores):
        r = np.asarray(results[c]["out"]).reshape(cfg.NB, cfg.OUTR, cfg.D)
        out[:, c * cfg.OUTR:(c + 1) * cfg.OUTR, :] = r
    return out


def kernel(L0_attn_x, L0_attn_Wqkv, L0_attn_Wproj, L0_attn_lambda_q1,
           L0_attn_lambda_k1, L0_attn_lambda_q2, L0_attn_lambda_k2,
           L0_attn_rms_scale, L0_ffn_gate_w, L0_ffn_w1, L0_ffn_w2):
    cfg = FULL
    in_maps = make_inmaps(
        cfg, np.asarray(L0_attn_x), np.asarray(L0_attn_Wqkv),
        np.asarray(L0_attn_Wproj), np.asarray(L0_ffn_gate_w),
        np.asarray(L0_ffn_w1), np.asarray(L0_ffn_w2),
        np.asarray(L0_attn_lambda_q1), np.asarray(L0_attn_lambda_k1),
        np.asarray(L0_attn_lambda_q2), np.asarray(L0_attn_lambda_k2),
        np.asarray(L0_attn_rms_scale))
    nc = build_nc(cfg)
    res = run_bass_kernel_spmd(nc, in_maps, core_ids=list(range(cfg.cores)))
    return assemble(cfg, res.results)


# version 22
# speedup vs baseline: 1.0011x; 1.0003x over previous
import math
from contextlib import ExitStack

import numpy as np
import ml_dtypes

import concourse.bass as bass
import concourse.mybir as mybir
from concourse.bass import IndirectOffsetOnAxis
from concourse.bass_utils import run_bass_kernel_spmd

F32 = mybir.dt.float32
BF16 = mybir.dt.bfloat16
U32 = mybir.dt.uint32
AF = mybir.ActivationFunctionType
ALU = mybir.AluOpType
AX = mybir.AxisListType

LAMBDA_INIT = 0.8 - 0.6 * math.exp(-0.3 * 1)


class Cfg:
    def __init__(self, cores=8, T=4096, HID=4096, NB=4, CAP=2.0):
        self.cores = cores
        self.T = T
        self.D = 128 * cores
        self.HID = HID
        self.NB = NB
        self.NE = cores
        self.K = int(CAP * T / self.NE)
        D = self.D
        self.DC = D // 128
        self.TC = T // 128
        self.QB = 256
        self.NQB = T // self.QB
        self.KCN = T // 128
        self.HC = HID // 128
        self.KC = self.K // 128
        self.SB = min(128, self.K)
        self.NSB = self.K // self.SB
        self.OUTR = T // cores
        self.TB = 256
        self.NTB = T // self.TB
        self.PO_F = min(512, D)
        self.POT = D // self.PO_F


FULL = Cfg()


class Eng:

    def __init__(self, nc, ctx, name, dma=False, serialize=True):
        self.sem = ctx.enter_context(nc.semaphore(name))
        self.count = 0
        self.inc = 16 if dma else 1
        self.serialize = serialize

    def mark(self):
        return self.count

    def __call__(self, eng, emit, waits=(), inc=True, noser=False):
        merged = {}

        def add(other, cnt):
            if cnt and cnt > 0:
                k = id(other.sem)
                prev = merged.get(k)
                if prev is None or cnt > prev[1]:
                    merged[k] = (other, cnt)

        for item in waits:
            if isinstance(item, list):
                for (o, c) in item:
                    add(o, c)
            else:
                add(*item)
        if self.serialize and not noser and self.count:
            add(self, self.count)
        for (o, c) in merged.values():
            eng.wait_ge(o.sem, c)
        inst = emit()
        if inc:
            inst.then_inc(self.sem, self.inc)
            self.count += self.inc
        return self.count

    def wait(self, eng, count=None):
        eng.wait_ge(self.sem, self.count if count is None else count)


class DmaPool:

    def __init__(self, nc, ctx, name, nlanes=8):
        self.lanes = [Eng(nc, ctx, f"{name}{i}", dma=True, serialize=True)
                      for i in range(nlanes)]
        self.i = 0

    def __call__(self, eng, emit, waits=()):
        lane = self.lanes[self.i]
        self.i = (self.i + 1) % len(self.lanes)
        lane(eng, emit, waits)
        return [(lane, lane.count)]

    def mark(self):
        return [(l, l.count) for l in self.lanes if l.count]

    def wait(self, eng, marks=None):
        for (l, c) in (self.mark() if marks is None else marks):
            if c:
                eng.wait_ge(l.sem, c)


PHASES = ["qkv", "vtr", "attn", "ar", "ln1", "x1bf", "gate", "sm", "bis",
          "mask", "sel", "ex", "mg", "ffn", "rs", "ln2", "all"]


def build_nc(cfg: Cfg, debug=False, stop_after="all", dbg_tensors=(),
             attn_sub="full"):
    T, D, NB, NE = cfg.T, cfg.D, cfg.NB, cfg.NE
    DC, TC, QB, NQB, KCN = cfg.DC, cfg.TC, cfg.QB, cfg.NQB, cfg.KCN
    HID, HC, K, KC, SB, NSB = cfg.HID, cfg.HC, cfg.K, cfg.KC, cfg.SB, cfg.NSB
    OUTR, TB, NTB = cfg.OUTR, cfg.TB, cfg.NTB
    PO_F, POT = cfg.PO_F, cfg.POT
    groups = [list(range(cfg.cores))]
    NITER = 36

    SL = PHASES.index(stop_after)

    def ph(name):
        return SL >= PHASES.index(name)

    nc = bass.Bass("TRN2", target_bir_lowering=False, debug=debug)

    x_d = nc.dram_tensor("x", [NB * T, D], F32, kind="ExternalInput").ap()
    xts_d = nc.dram_tensor("xts", [128, NB * T * (D // 128)], F32,
                           kind="ExternalInput").ap()
    wqkv_d = nc.dram_tensor("wqkv", [D, 384], F32, kind="ExternalInput").ap()
    wproj_d = nc.dram_tensor("wproj", [128, D], F32, kind="ExternalInput").ap()
    gatew_d = nc.dram_tensor("gatew", [D, NE], F32, kind="ExternalInput").ap()
    w1_d = nc.dram_tensor("w1", [128, HC * DC * 128], F32,
                          kind="ExternalInput").ap()
    w2_d = nc.dram_tensor("w2", [HID, D], F32, kind="ExternalInput").ap()
    lam_d = nc.dram_tensor("lamv", [1, 128], F32, kind="ExternalInput").ap()
    rms_d = nc.dram_tensor("rmsrow", [1, 128], F32, kind="ExternalInput").ap()
    out_d = nc.dram_tensor("out", [NB * OUTR, D], F32, kind="ExternalOutput").ap()

    xbf_d = nc.dram_tensor("xbf", [NB * T, D], BF16).ap()
    xtsbf_d = nc.dram_tensor("xtsbf", [128, NB * T * (D // 128)], BF16).ap()
    w1bf_d = nc.dram_tensor("w1bf", [128, HC * DC * 128], BF16).ap()
    ppart_d = nc.dram_tensor("ppart", [NB * T, D], F32).ap()
    psum_d = nc.dram_tensor("psumd", [NB * T, D], F32,
                            addr_space="Shared" if cfg.cores > 4 else "Local").ap()
    x1_ds = [nc.dram_tensor(f"x1tok{b}", [T, D], F32).ap() for b in range(NB)]
    x1bf_d = nc.dram_tensor("x1bf", [NB * T, D], BF16).ap()
    opart_ds = [nc.dram_tensor(f"opart{b}", [T, D], F32).ap() for b in range(NB)]
    ored_d = nc.dram_tensor("ored", [NB * OUTR, D], F32).ap()

    with ExitStack() as ctx:
        sb = lambda n, s, d: ctx.enter_context(nc.sbuf_tensor(n, s, d))
        ps_ = lambda n, s, d: ctx.enter_context(nc.psum_tensor(n, s, d))

        ident_b = sb("ident_b", [128, 128], BF16)
        ident_f = sb("ident_f", [128, 128], F32)
        iota_col = sb("iota_col", [128, 1], F32)
        iota_row = None

        ones8_f = sb("ones8_f", [NE, 128], F32)
        onesrow_f = sb("onesrow_f", [1, 128], F32)
        onescol_f = sb("onescol_f", [128, 1], F32)
        hT_sb = sb("hT_sb", [128, HC * 512], BF16)
        hT_f32 = hT_sb[:, :].bitcast(F32)
        iota_j = hT_f32[:, T:T + K]
        iota_colb = sb("iota_colb", [128, 1], BF16)
        chunkv_b = sb("chunkv_b", [128, TC], BF16)
        iota_e = sb("iota_e", [NE, TC * NE], F32)
        ind_em = sb("ind_em", [NE, TC * NE], F32)
        biasc = sb("biasc", [128, 6], F32)
        scaletile = sb("scaletile", [128, 64], F32)
        neglam = sb("neglam", [128, 1], F32)
        lam_sb = sb("lam_sb", [1, 128], F32)
        lamp = sb("lamp", [1, 64], F32)
        lamsc = sb("lamsc", [1, 8], F32)
        rmsrow_sb = sb("rmsrow_sb", [1, 128], F32)

        wqkv_sb = sb("wqkv_sb", [128, DC * 384], BF16)
        wproj_sb = sb("wproj_sb", [128, D], BF16)
        gatew_sb = sb("gatew_sb", [128, DC * NE], BF16)
        w2_sb = sb("w2_sb", [128, HC * D], BF16)
        w1_buf = sb("w1_buf", [128, 2 * D], BF16)

        xfm_buf = sb("xfm_buf", [128, 2 * DC * TB], BF16)
        qT_sb = sb("qT_sb", [128, T], BF16)
        kT_sb = sb("kT_sb", [128, T], BF16)

        vaug_sb = sb("vaug_sb", [128, TC * 130], BF16)
        qblk_sb = sb("qblk_sb", [128, 4 * QB], BF16)
        probs_buf = sb("probs_buf", [128, 2 * 4 * QB], BF16)
        zcatT_sb = sb("zcatT_sb", [128, QB], BF16)
        zct_sb = sb("zct_sb", [128, 256], BF16)
        fm_ep = sb("fm_ep", [128, 1024], F32)
        ep_f = sb("ep_f", [128, 4 * QB], F32)
        iota_row = ep_f[:, 128:256]
        assert D <= 4 * QB
        x18buf = ep_f[:, 0:D]
        ybuf = sb("ybuf", [128, 2 * D], F32)
        x1buf = sb("x1buf", [128, 2 * D], F32)
        x1fm_buf = sb("x1fm_buf", [128, max(T, 4 * D)], BF16)
        vT_sb = x1fm_buf[:, 0:T]
        hi_b = x1fm_buf[0:8, 0:T]
        tie_b = ybuf[:, :].bitcast(BF16)[0:8, 0:T]
        rank_f = hT_f32[0:8, 0:T]
        gacc = rank_f[0:NE, :]
        xq_sb = [hT_f32[:, T + K:T + K + 1024],
                 hT_f32[:, T + K + 1024:T + K + 2048]]
        gtok = sb("gtok", [128, TC * NE], F32)
        probs_tok = gtok
        ssum = sb("ssum", [128, TC], F32)
        tau_lo = sb("tau_lo", [NE, 1], F32)
        tau_hi = sb("tau_hi", [NE, 1], F32)
        tau_mid = sb("tau_mid", [NE, 1], F32)
        tau_m8 = sb("tau_m8", [NE, TC * NE], F32)
        cmp_tok = sb("cmp_tok", [128, TC * NE], F32)
        part8 = sb("part8", [128, NE], F32)
        cnt8 = sb("cnt8", [NE, 1], F32)
        cond8 = sb("cond8", [NE, 1], F32)
        bs_sc = sb("bs_sc", [NE, 8], F32)
        hi_tok = cmp_tok
        tie_tok = sb("tie_tok", [128, TC * NE], F32)

        fill8 = sb("fill8", [NE, 1], F32)

        gs_tok = sb("gs_tok", [128, 2 * TC], F32)
        pc_b = ep_f[:, 0:K // 2].bitcast(BF16)
        payload = sb("payload", [128, 3], BF16)
        mg_f = sb("mg_f", [3, K], BF16)
        mg3 = sb("mg3", [128, 4], F32)
        mcols = sb("mcols", [128, KC], F32)
        midx = sb("midx", [128, KC], U32)
        gcol = sb("gcol", [128, KC], F32)
        xin_sb = sb("xin_sb", [128, 2 * D], BF16)
        xinT_sb = sb("xinT_sb", [128, DC * 512], BF16)
        fw_sb = probs_buf[:, 0:D]
        xinT_f32 = xinT_sb[:, :].bitcast(F32)
        pstage2 = [xinT_f32[:, 0:1024], xinT_f32[:, 1024:2048]]
        pstage = pstage2[0]
        lnslots = [ybuf[:, 0:D], ybuf[:, D:2 * D]]
        stat_f = sb("stat_f", [128, 16], F32)

        sc0 = ps_("sc0", [128, 4 * QB], F32)
        sc1 = ps_("sc1", [128, 4 * QB], F32)
        accv = ps_("accv", [128, 1024], F32)
        chain = ps_("chain", [128, 1024], F32)
        chA = chain[:, 0:512]
        chB = chain[:, 512:1024]
        chain_bf = chain[:, :].bitcast(BF16)

        def chAb(i):
            return chain_bf[:, i * 128:(i + 1) * 128]

        def chBb(i):
            return chain_bf[:, 1024 + i * 128:1024 + (i + 1) * 128]

        dmas = DmaPool(nc, ctx, "dmas")
        dmag = DmaPool(nc, ctx, "dmag")
        pe = Eng(nc, ctx, "pe", serialize=False)
        act = Eng(nc, ctx, "act")
        dve = Eng(nc, ctx, "dve")
        pool = Eng(nc, ctx, "pool")
        ccx = Eng(nc, ctx, "ccx", serialize=False)
        ln2st = Eng(nc, ctx, "ln2st", dma=True, serialize=False)

        block = ctx.enter_context(nc.Block())

        chfree = {"A": (dve, 0), "B": (dve, 0)}

        def chw(which):
            e, c = chfree[which]
            return [(e, c)] if c > 0 else []

        xconv_m = []
        @block.gpsimd
        def _(g):
            for dc in range(DC):
                dmag(g, lambda dc=dc: g.dma_start(
                    out=wqkv_sb[:, dc * 384:(dc + 1) * 384],
                    in_=wqkv_d[dc * 128:(dc + 1) * 128, :]))
            dmag(g, lambda: g.dma_start(out=wproj_sb[:, :], in_=wproj_d))
            for dc in range(DC):
                dmag(g, lambda dc=dc: g.dma_start(
                    out=gatew_sb[:, dc * NE:(dc + 1) * NE],
                    in_=gatew_d[dc * 128:(dc + 1) * 128, :]))
            for hc in range(HC):
                dmag(g, lambda hc=hc: g.dma_start(
                    out=w2_sb[:, hc * D:(hc + 1) * D],
                    in_=w2_d[hc * 128:(hc + 1) * 128, :]))
            for ci in range(8):
                cw = NB * T * DC // 8
                dmag(g, lambda ci=ci, cw=cw: g.dma_start(
                    out=xtsbf_d[:, ci * cw:(ci + 1) * cw],
                    in_=xts_d[:, ci * cw:(ci + 1) * cw]))
                xconv_m.append(dmag.mark())
            for ci in range(4):
                cw = HC * DC * 128 // 4
                dmag(g, lambda ci=ci, cw=cw: g.dma_start(
                    out=w1bf_d[:, ci * cw:(ci + 1) * cw],
                    in_=w1_d[:, ci * cw:(ci + 1) * cw]))
            pool(g, lambda: g.iota(iota_col[:, :], pattern=[[0, 1]], base=0,
                                   channel_multiplier=1,
                                   allow_small_or_imprecise_dtypes=True))
            pool(g, lambda: g.iota(iota_row[:, :], pattern=[[1, 128]], base=0,
                                   channel_multiplier=0,
                                   allow_small_or_imprecise_dtypes=True))
            pool(g, lambda: g.iota(iota_j[:, :], pattern=[[1, K]], base=0,
                                   channel_multiplier=0,
                                   allow_small_or_imprecise_dtypes=True))
            pool(g, lambda: g.iota(iota_colb[:, :], pattern=[[0, 1]], base=0,
                                   channel_multiplier=1,
                                   allow_small_or_imprecise_dtypes=True))
            pool(g, lambda: g.iota(chunkv_b[:, :], pattern=[[1, TC]], base=0,
                                   channel_multiplier=0,
                                   allow_small_or_imprecise_dtypes=True))
            pool(g, lambda: g.iota(iota_e[:, :], pattern=[[0, TC], [1, NE]],
                                   base=0, channel_multiplier=0,
                                   allow_small_or_imprecise_dtypes=True))


        @block.sync
        def _(s):
            dmas(s, lambda: s.dma_start(out=lam_sb[:, :], in_=lam_d))
            dmas(s, lambda: s.dma_start(out=rmsrow_sb[:, :], in_=rms_d))

        @block.vector
        def _(v):
            pool.wait(v)
            dmas.wait(v)
            dve(v, lambda: v.memset(ones8_f[:, :], 1.0))
            dve(v, lambda: v.memset(onesrow_f[:, :], 1.0))
            dve(v, lambda: v.memset(onescol_f[:, :], 1.0))
            dve(v, lambda: v.memset(biasc[:, :], 0.0))
            dve(v, lambda: v.memset(biasc[:, 0:1], -8.0))
            dve(v, lambda: v.memset(biasc[:, 1:2], -0.5))
            dve(v, lambda: v.memset(biasc[:, 2:3], 1.0 / math.sqrt(32.0)))
            dve(v, lambda: v.memset(biasc[:, 4:5], 1e-6))
            dve(v, lambda: v.memset(biasc[:, 5:6], 1.0 / 64))
            dve(v, lambda: v.memset(qblk_sb[:, :], 0.0))
            va3 = vaug_sb[:, :].rearrange("p (t c) -> p t c", c=130)
            dve(v, lambda: v.memset(va3[:, :, 64:65], 1.0))
            dve(v, lambda: v.memset(va3[:, :, 129:130], 1.0))
            dve(v, lambda: v.tensor_scalar(
                out=ep_f[0:1, 0:64], in0=rmsrow_sb[0:1, 0:64],
                scalar1=(1.0 - LAMBDA_INIT), scalar2=None, op0=ALU.mult))
            dve(v, lambda: v.tensor_scalar(
                out=ident_b[:, :], in0=iota_row[:, :], scalar1=iota_col[:, :1],
                scalar2=None, op0=ALU.is_equal))
            dve(v, lambda: v.tensor_scalar(
                out=ident_f[:, :], in0=iota_row[:, :], scalar1=iota_col[:, :1],
                scalar2=None, op0=ALU.is_equal))
            dve(v, lambda: v.tensor_scalar(
                out=ind_em[:, :], in0=iota_e[:, :], scalar1=iota_col[:NE, :1],
                scalar2=None, op0=ALU.is_equal))
            dve(v, lambda: v.tensor_mul(lamp[:, :], lam_sb[:, 0:64],
                                        lam_sb[:, 64:128]))
            dve(v, lambda: v.reduce_sum(lamsc[0:1, 0:1], lamp[:, 0:32],
                                        axis=AX.X))
            dve(v, lambda: v.reduce_sum(lamsc[0:1, 1:2], lamp[:, 32:64],
                                        axis=AX.X))

        @block.scalar
        def _(s):
            dve.wait(s)
            act(s, lambda: s.activation(out=lamsc[0:1, 2:4], in_=lamsc[0:1, 0:2],
                                        func=AF.Exp))
        lam_act = act.mark()

        @block.vector
        def _(v):
            act.wait(v)
            dve(v, lambda: v.tensor_sub(lamsc[0:1, 4:5], lamsc[0:1, 3:4],
                                        lamsc[0:1, 2:3]))
        lam_dve2 = dve.mark()

        @block.tensor
        def _(t):
            pe(t, lambda: t.matmul(chA[:, 0:1], onesrow_f[0:1, :],
                                   lamsc[0:1, 4:5], start=True, stop=True),
               waits=[(dve, lam_dve2)])
            pe(t, lambda: t.matmul(chB[:, 0:64], onesrow_f[0:1, :],
                                   ep_f[0:1, 0:64], start=True, stop=True))
        bc_mm = pe.mark()

        @block.vector
        def _(v):
            dve(v, lambda: v.tensor_scalar(
                out=neglam[:, :], in0=chA[:, 0:1], scalar1=-LAMBDA_INIT,
                scalar2=None, op0=ALU.add), waits=[(pe, bc_mm)])
            dve(v, lambda: v.tensor_copy(out=scaletile[:, :], in_=chB[:, 0:64]))
        chfree["A"] = (dve, dve.mark())
        chfree["B"] = (dve, dve.mark())
        setup_dmag = dmag.mark()

        def ln2_body(beta, rs_done):
            nln = (OUTR + 127) // 128
            l2stats = [0] * nln
            l2_base = act.count

            def ln2_scale(oc):
                par = oc % 2
                slot = lnslots[par]
                so = par * 8
                r0 = oc * 128

                @block.vector
                def _(v, slot=slot, so=so, oc=oc):
                    dve(v, lambda: v.reciprocal(
                        stat_f[:, so + 5:so + 6], stat_f[:, so + 5:so + 6]),
                        waits=[(act, l2_base + oc + 1)])
                    dve(v, lambda: v.tensor_scalar(
                        out=slot[:, :], in0=slot[:, :],
                        scalar1=stat_f[:, so + 2:so + 3],
                        scalar2=stat_f[:, so + 5:so + 6],
                        op0=ALU.subtract, op1=ALU.mult))
                l2o = dve.mark()

                @block.gpsimd
                def _(g, r0=r0, slot=slot, l2o=l2o):
                    ln2st(g, lambda: g.dma_start(
                        out=out_d[beta * OUTR + r0:beta * OUTR + r0 + 128, :],
                        in_=slot[:, :]), waits=[(dve, l2o)])

            for oc in range(nln):
                par = oc % 2
                slot = lnslots[par]
                so = par * 8

                @block.gpsimd
                def _(g, oc=oc, slot=slot, rs_done=rs_done):
                    w = [(ccx, rs_done)]
                    if oc >= 2:
                        w.append((ln2st, ln2st.count))
                    dmag(g, lambda oc=oc: g.dma_start(
                        out=slot[:, :],
                        in_=ored_d[beta * OUTR + oc * 128:
                                   beta * OUTR + (oc + 1) * 128, :]),
                        waits=w)
                lnl = dmag.mark()

                @block.vector
                def _(v, slot=slot, so=so, lnl=lnl):
                    Yl = slot[:, :]
                    dve(v, lambda: v.tensor_scalar(
                        out=hT_f32[:, 0:1024], in0=Yl, scalar1=1.0,
                        scalar2=None, op0=ALU.mult, op1=ALU.add,
                        accum_out=stat_f[:, so + 0:so + 1]),
                        waits=[lnl])
                    dve(v, lambda: v.scalar_tensor_tensor(
                        out=hT_f32[:, 0:1024], in0=Yl, scalar=1.0, in1=Yl,
                        op0=ALU.mult, op1=ALU.mult,
                        accum_out=stat_f[:, so + 1:so + 2]))
                    dve(v, lambda: v.tensor_scalar(
                        out=stat_f[:, so + 2:so + 3],
                        in0=stat_f[:, so + 0:so + 1],
                        scalar1=1.0 / D, scalar2=None, op0=ALU.mult))
                    dve(v, lambda: v.tensor_mul(
                        stat_f[:, so + 3:so + 4], stat_f[:, so + 2:so + 3],
                        stat_f[:, so + 2:so + 3]))
                    dve(v, lambda: v.scalar_tensor_tensor(
                        out=stat_f[:, so + 4:so + 5],
                        in0=stat_f[:, so + 1:so + 2], scalar=1.0 / D,
                        in1=stat_f[:, so + 3:so + 4],
                        op0=ALU.mult, op1=ALU.subtract))
                    dve(v, lambda: v.tensor_scalar(
                        out=stat_f[:, so + 4:so + 5],
                        in0=stat_f[:, so + 4:so + 5],
                        scalar1=1e-5, scalar2=None, op0=ALU.add))
                l2stats[oc] = dve.mark()
                if oc >= 1:
                    ln2_scale(oc - 1)
            ln2_scale(nln - 1)

            @block.scalar
            def _(s):
                for oc in range(nln):
                    so = (oc % 2) * 8
                    act(s, lambda so=so, oc=oc: s.activation(
                        out=stat_f[:, so + 5:so + 6],
                        in_=stat_f[:, so + 4:so + 5], func=AF.Sqrt),
                        waits=[(dve, l2stats[oc])], noser=True)
            assert act.count == l2_base + nln
            out_st_h[0] = [(ln2st, ln2st.count)]


        out_st_h = [[]]
        pending_ln2 = [None]
        prev_ffn_dma = []
        prev_gate_pe = 0
        prev_x1_dma = []
        at_exp_war = {0: 0, 1: 0, 2: 0, 3: 0}
        at_chain_war = [[]]
        at_trans_m = [0]
        at_evac_prev = [0]
        at_qblk_war = [0]
        prev_ffn_pe = [0]
        prev_gelu_m = [0]
        ffn_acc_last = [0]
        for beta in range(NB):
            xr = beta * T

            qk_evac = dve.mark()
            xfm_war = {0: 0, 1: 0}
            for tb in range(NTB):
                par = tb % 2
                xo = par * DC * TB

                @block.sync
                def _(s, tb=tb, par=par, xo=xo):
                    c0 = (beta * NTB + tb) * DC * TB
                    w = [xconv_m[c0 // (NB * T * DC // 8)]]
                    if xfm_war[par]:
                        w.append((pe, xfm_war[par]))
                    dmas(s, lambda c0=c0: s.dma_start(
                        out=xfm_buf[:, xo:xo + DC * TB],
                        in_=xtsbf_d[:, c0:c0 + DC * TB]), waits=w)
                xfm_ld = dmas.mark()

                @block.tensor
                def _(t, tb=tb, xo=xo):
                    outs = [sc0[:, 0:TB], sc0[:, TB:2 * TB], sc1[:, 0:TB]]
                    w = [xfm_ld,
                         (dve, qk_evac if tb > 0 else ffn_acc_last[0])]
                    if beta == 0 and tb == 0:
                        w.append(setup_dmag)
                    first = True
                    for ci in range(3):
                        for dc in range(DC):
                            pe(t, lambda ci=ci, dc=dc: t.matmul(
                                outs[ci][:, 0:TB],
                                wqkv_sb[:, dc * 384 + ci * 128:
                                        dc * 384 + (ci + 1) * 128],
                                xfm_buf[:, xo + dc * TB:xo + (dc + 1) * TB],
                                start=(dc == 0), stop=(dc == DC - 1)),
                               waits=w if first else ())
                            first = False
                xfm_war[par] = pe.mark()
                qkv_mm = pe.mark()

                @block.vector
                def _(v, tb=tb):
                    dve(v, lambda: v.tensor_copy(
                        out=qT_sb[:, tb * TB:(tb + 1) * TB], in_=sc0[:, 0:TB]),
                        waits=[(pe, qkv_mm)])
                    dve(v, lambda: v.tensor_copy(
                        out=kT_sb[:, tb * TB:(tb + 1) * TB], in_=sc0[:, TB:2 * TB]))
                    wv = []
                    if prev_gate_pe:
                        wv.append((pe, prev_gate_pe))
                    dve(v, lambda: v.tensor_copy(
                        out=vT_sb[:, tb * TB:(tb + 1) * TB], in_=sc1[:, 0:TB]),
                        waits=wv)
                qk_evac = dve.mark()

            if not ph("vtr"):
                continue
            for tg in range(0, TC, 8):
                n = min(8, TC - tg)
                bk = "A" if (tg // 8) % 2 == 0 else "B"
                slot = chAb if bk == "A" else chBb

                @block.tensor
                def _(t, tg=tg, n=n, slot=slot, bk=bk):
                    w = chw(bk) + [(dve, qk_evac)]
                    for i in range(n):
                        pe(t, lambda i=i: t.transpose(
                            slot(i), vT_sb[:, (tg + i) * 128:(tg + i + 1) * 128],
                            ident_b[:, :]), waits=w if i == 0 else ())
                vtp = pe.mark()

                @block.vector
                def _(v, tg=tg, n=n, slot=slot):
                    for i in range(n):
                        tc = tg + i
                        dve(v, lambda i=i, tc=tc: v.tensor_copy(
                            out=vaug_sb[:, tc * 130:tc * 130 + 64],
                            in_=slot(i)[:, 0:64]),
                            waits=[(pe, vtp)] if i == 0 else ())
                        dve(v, lambda i=i, tc=tc: v.tensor_copy(
                            out=vaug_sb[:, tc * 130 + 65:tc * 130 + 129],
                            in_=slot(i)[:, 64:128]))
                chfree[bk] = (dve, dve.mark())
            vtok_ready = dve.mark()
            at_chain_war[0] = chw("A") + chw("B")

            if not ph("attn"):
                continue
            accA = accv[:, 0:512]
            accB = accv[:, 512:1024]
            E = ep_f
            A0 = 16
            Z0 = A0 + 512
            SQ0 = Z0 + 256
            pstage_war = {0: [], 1: []}
            xq_war = {0: [], 1: []}
            xq_lanes = {0: [], 1: []}
            p_dma = dmas.mark()
            p_dma_q = [0, 0, 0, 0]
            epi = None
            NEARLY = 6
            for qb in range(NQB + 1):
                last = qb == NQB
                if not last:
                    q0 = qb * QB

                    @block.vector
                    def _(v, q0=q0, qb=qb):
                        w = []
                        if at_qblk_war[0]:
                            w.append((pe, at_qblk_war[0]))
                        if qb == 0:
                            w += list(prev_x1_dma)
                        for X in range(4):
                            dve(v, lambda X=X: v.tensor_copy(
                                out=qblk_sb[32 * X:32 * (X + 1),
                                            X * QB:(X + 1) * QB],
                                in_=qT_sb[32 * X:32 * (X + 1), q0:q0 + QB]),
                                waits=w if X == 0 else ())
                    qblk_done = dve.mark()

                if epi is not None:
                    @block.sync
                    def _(s, epi=epi):
                        for qt in range(2):
                            w = list(xq_war[qt])
                            if epi["q0"] == 0:
                                if prev_gelu_m[0]:
                                    w.append((act, prev_gelu_m[0]))
                                if prev_ffn_pe[0]:
                                    w.append((pe, prev_ffn_pe[0]))
                            r = xr + epi["q0"] + qt * 128
                            h = dmas(s, lambda r=r, qt=qt: s.dma_start(
                                out=xq_sb[qt][:, :], in_=x_d[r:r + 128, :]),
                                waits=w)
                            xq_lanes[qt] = h

                    @block.tensor
                    def _(t, epi=epi):
                        w = [(dve, epi["evac"])] + list(at_chain_war[0])
                        for s_ in range(8):
                            X, qt = s_ % 4, s_ // 4
                            co = X * 256 + qt * 128
                            cb = s_ * 65 if s_ < 4 else 512 + (s_ - 4) * 65
                            pe(t, lambda co=co, cb=cb: t.transpose(
                                chain[:, cb:cb + 65], fm_ep[0:65, co:co + 128],
                                ident_f[0:65, 0:65]),
                               waits=w if s_ == 0 else ())
                    at_trans_m[0] = pe.mark()

                sc_done = {}
                if not last:
                    act_base = act.count
                    nmid = 5 if epi is not None else 0

                    def _ep(kc):
                        return act_base + kc + 1 + (nmid if kc >= NEARLY
                                                    else 0)
                    exp_pred = [_ep(kc) for kc in range(KCN)]
                    sqrt_pred = (act_base + NEARLY + 5) if epi is not None \
                        else 0

                    @block.tensor
                    def _(t, qb=qb, qblk_done=qblk_done):
                        for kc in range(KCN):
                            par = kc % 2
                            scs = sc0 if par == 0 else sc1
                            po = par * 4 * QB
                            w = []
                            if at_exp_war[par]:
                                w.append((act, at_exp_war[par]))
                            if kc == 0:
                                w.append((dve, qblk_done))
                                if qb == 0:
                                    w.append((dve, vtok_ready))
                            for half in range(2):
                                pe(t, lambda kc=kc, scs=scs, half=half:
                                   t.matmul(
                                    scs[:, half * 512:(half + 1) * 512],
                                    kT_sb[:, kc * 128:(kc + 1) * 128],
                                    qblk_sb[:, half * 512:(half + 1) * 512],
                                    start=True, stop=True),
                                   waits=w if half == 0 else ())
                            at_exp_war[par] = exp_pred[kc]
                            sc_done[kc] = pe.mark()
                            wv = [(act, exp_pred[kc])]
                            if kc == 0:
                                if at_evac_prev[0]:
                                    wv.append((dve, at_evac_prev[0]))
                                if qb == 0:
                                    wv.append((dve, vtok_ready))
                            pe(t, lambda kc=kc, po=po: t.matmul(
                                accA[0:65, :],
                                vaug_sb[:, kc * 130:kc * 130 + 65],
                                probs_buf[:, po:po + 512],
                                start=(kc == 0), stop=(kc == KCN - 1)),
                               waits=wv)
                            pe(t, lambda kc=kc, po=po: t.matmul(
                                accB[0:65, :],
                                vaug_sb[:, kc * 130 + 65:kc * 130 + 130],
                                probs_buf[:, po + 512:po + 1024],
                                start=(kc == 0), stop=(kc == KCN - 1)))
                    pv_all = pe.mark()

                if epi is not None:
                    zm = [0] * 4

                    @block.vector
                    def _(v, epi=epi, zm=zm):
                        w = [(pe, at_trans_m[0])]
                        ch3a = chain[:, 0:260].rearrange(
                            "p (s c) -> p s c", c=65)
                        ch3b = chain[:, 512:772].rearrange(
                            "p (s c) -> p s c", c=65)
                        dve(v, lambda: v.reciprocal(
                            E[:, 0:4].rearrange("p (s c) -> p s c", c=1),
                            ch3a[:, :, 64:65]), waits=w)
                        dve(v, lambda: v.reciprocal(
                            E[:, 4:8].rearrange("p (s c) -> p s c", c=1),
                            ch3b[:, :, 64:65]))
                        eodd = E[:, 0:8].rearrange(
                            "p (s c) -> p s c", c=2)[:, :, 1:2]
                        dve(v, lambda: v.tensor_scalar(
                            out=eodd, in0=eodd, scalar1=neglam[:, :1],
                            scalar2=None, op0=ALU.mult))
                        for qt in range(2):
                            for head in range(2):
                                s1 = qt * 4 + 2 * head
                                s2 = s1 + 1
                                cb1 = s1 * 65 if s1 < 4 else 512 + (s1 - 4) * 65
                                cb2 = s2 * 65 if s2 < 4 else 512 + (s2 - 4) * 65
                                zi = qt * 2 + head
                                z = E[:, Z0 + zi * 64:Z0 + (zi + 1) * 64]
                                dve(v, lambda cb1=cb1, s1=s1, z=z:
                                    v.tensor_scalar(
                                        out=z, in0=chain[:, cb1:cb1 + 64],
                                        scalar1=E[:, s1:s1 + 1], scalar2=None,
                                        op0=ALU.mult))
                                dve(v, lambda cb2=cb2, s2=s2, z=z:
                                    v.scalar_tensor_tensor(
                                        out=z, in0=chain[:, cb2:cb2 + 64],
                                        scalar=E[:, s2:s2 + 1], in1=z,
                                        op0=ALU.mult, op1=ALU.add))
                                zm[zi] = dve.mark()
                    epi["zm"] = zm
                    epi["rsq"] = dve.mark()

                @block.scalar
                def _(s, epi=epi, sc_done=sc_done, last=last):
                    def emit_exp(kc):
                        par = kc % 2
                        scs = sc0 if par == 0 else sc1
                        po = par * 4 * QB
                        act(s, lambda scs=scs, po=po: s.activation(
                            out=probs_buf[:, po:po + 4 * QB],
                            in_=scs[:, 0:4 * QB], func=AF.Exp,
                            bias=biasc[:, 0:1], scale=biasc[:, 2:3]),
                            waits=[(pe, sc_done[kc])], noser=True)

                    if not last:
                        for kc in range(NEARLY):
                            emit_exp(kc)
                    if epi is not None:
                        for zi in range(4):
                            act(s, lambda zi=zi: s.activation(
                                out=E[:, A0 + zi * 64:A0 + (zi + 1) * 64],
                                in_=E[:, Z0 + zi * 64:Z0 + (zi + 1) * 64],
                                func=AF.Square,
                                accum_out=E[:, 8 + zi:9 + zi]),
                                waits=[(dve, epi["zm"][zi])], noser=True)
                        act(s, lambda: s.activation(
                            out=E[:, 12:16], in_=E[:, 8:12], func=AF.Sqrt,
                            bias=biasc[:, 4:5], scale=biasc[:, 5:6]))
                    if not last:
                        for kc in range(NEARLY, KCN):
                            emit_exp(kc)
                if not last:
                    assert act.count == act_base + KCN + nmid
                    if epi is not None:
                        assert act.count >= sqrt_pred
                else:
                    sqrt_pred = act.count

                if qb == 2 and pending_ln2[0] is not None:
                    pending_ln2[0]()
                    pending_ln2[0] = None

                if epi is not None:
                    @block.vector
                    def _(v, sqrt_pred=sqrt_pred):
                        dve(v, lambda: v.reciprocal(E[:, 12:16], E[:, 12:16]),
                            waits=[(act, sqrt_pred)])
                        for qt in range(2):
                            for head in range(2):
                                zi = qt * 2 + head
                                dve(v, lambda zi=zi, qt=qt, head=head:
                                    v.scalar_tensor_tensor(
                                        out=zct_sb[:, qt * 128 + head * 64:
                                                   qt * 128 + (head + 1) * 64],
                                        in0=E[:, Z0 + zi * 64:
                                              Z0 + (zi + 1) * 64],
                                        scalar=E[:, 12 + zi:13 + zi],
                                        in1=scaletile[:, :],
                                        op0=ALU.mult, op1=ALU.mult))
                    zct_done = dve.mark()

                    @block.tensor
                    def _(t, zct_done=zct_done):
                        for qt in range(2):
                            pe(t, lambda qt=qt: t.transpose(
                                chain_bf[:, qt * 128:(qt + 1) * 128],
                                zct_sb[:, qt * 128:(qt + 1) * 128],
                                ident_b[:, :]),
                               waits=[(dve, zct_done)] if qt == 0 else ())
                    zT_mm = pe.mark()

                    @block.vector
                    def _(v, zT_mm=zT_mm):
                        for qt in range(2):
                            dve(v, lambda qt=qt: v.tensor_copy(
                                out=zcatT_sb[:, qt * 128:(qt + 1) * 128],
                                in_=chain_bf[:, qt * 128:(qt + 1) * 128]),
                                waits=[(pe, zT_mm)] if qt == 0 else ())
                    zcat_done = dve.mark()

                    pstage_evac = 0
                    for qt in range(2):
                        @block.tensor
                        def _(t, qt=qt, zcat_done=zcat_done,
                              pstage_evac=pstage_evac, epi=epi):
                            w = []
                            if qt == 0:
                                w.append((dve, zcat_done))
                                w.append((dve, epi["rsq"]))
                            else:
                                w.append((dve, pstage_evac))
                            pe(t, lambda qt=qt: t.matmul(
                                chA[:, 0:512],
                                zcatT_sb[:, qt * 128:(qt + 1) * 128],
                                wproj_sb[:, 0:512], start=True, stop=True),
                               waits=w)
                            pe(t, lambda qt=qt: t.matmul(
                                chB[:, 0:512],
                                zcatT_sb[:, qt * 128:(qt + 1) * 128],
                                wproj_sb[:, 512:1024], start=True, stop=True))
                        pmm = pe.mark()

                        @block.vector
                        def _(v, qt=qt, pmm=pmm):
                            w = ([(pe, pmm)] + list(pstage_war[qt])
                                 + list(xq_lanes[qt]))
                            dve(v, lambda qt=qt: v.scalar_tensor_tensor(
                                out=pstage2[qt][:, 0:512],
                                in0=xq_sb[qt][:, 0:512],
                                scalar=1.0 / cfg.cores,
                                in1=chA[:, 0:512],
                                op0=ALU.mult, op1=ALU.add), waits=w)
                            dve(v, lambda qt=qt: v.scalar_tensor_tensor(
                                out=pstage2[qt][:, 512:1024],
                                in0=xq_sb[qt][:, 512:1024],
                                scalar=1.0 / cfg.cores,
                                in1=chB[:, 0:512],
                                op0=ALU.mult, op1=ALU.add))
                        pstage_evac = dve.mark()
                        xq_war[qt] = [(dve, pstage_evac)]

                        @block.sync
                        def _(s, qt=qt, epi=epi, pstage_evac=pstage_evac):
                            r = xr + epi["q0"] + qt * 128
                            h = dmas(s, lambda r=r, qt=qt: s.dma_start(
                                out=ppart_d[r:r + 128, :],
                                in_=pstage2[qt][:, :]),
                                waits=[(dve, pstage_evac)])
                            pstage_war[qt] = h
                        p_dma = dmas.mark()
                    pq = epi["q0"] // QB
                    if pq % 4 == 3:
                        p_dma_q[pq // 4] = p_dma
                    at_chain_war[0] = [(dve, pstage_evac)]
                    chfree["A"] = (dve, pstage_evac)
                    chfree["B"] = (dve, pstage_evac)

                if not last:
                    @block.vector
                    def _(v, pv_all=pv_all):
                        w = [(pe, pv_all)]
                        if at_trans_m[0]:
                            w.append((pe, at_trans_m[0]))
                        dve(v, lambda: v.tensor_copy(
                            out=fm_ep[0:65, 0:512], in_=accA[0:65, :]),
                            waits=w)
                        dve(v, lambda: v.tensor_copy(
                            out=fm_ep[0:65, 512:1024], in_=accB[0:65, :]))
                    at_evac_prev[0] = dve.mark()
                    epi = {"evac": at_evac_prev[0], "q0": q0}
                    at_qblk_war[0] = pe.mark()
                else:
                    epi = None

            if not ph("ar"):
                continue
            arq = []

            @block.gpsimd
            def _(g):
                QR = T // 4
                for q in range(4):
                    dmas.wait(g, p_dma_q[q])
                    ccx(g, lambda q=q, QR=QR: g.collective_compute(
                        "AllReduce", ALU.add, replica_groups=groups,
                        ins=[ppart_d[xr + q * QR:xr + (q + 1) * QR, :]],
                        outs=[psum_d[xr + q * QR:xr + (q + 1) * QR, :]]))
                    arq.append(ccx.mark())

            if not ph("ln1"):
                continue
            scale_war = {0: 0, 1: 0}
            x1war = {0: [], 1: []}
            stats_m = [0] * TC
            ln_base = act.count
            x18slot = [ep_f[:, 0:D], fm_ep[:, 0:D]]

            def ln1_scale(tc):
                par = tc % 2
                so = par * 8
                Y = ybuf[:, par * D:(par + 1) * D]

                @block.vector
                def _(v, tc=tc, par=par, so=so, Y=Y):
                    dve(v, lambda: v.reciprocal(
                        stat_f[:, so + 5:so + 6], stat_f[:, so + 5:so + 6]),
                        waits=[(act, ln_base + tc + 1),
                               (pe, at_trans_m[0])])
                    dve(v, lambda: v.tensor_scalar(
                        out=x1buf[:, par * D:(par + 1) * D], in0=Y,
                        scalar1=stat_f[:, so + 2:so + 3],
                        scalar2=stat_f[:, so + 5:so + 6],
                        op0=ALU.subtract, op1=ALU.mult))
                    dve(v, lambda: v.tensor_scalar(
                        out=x18slot[par], in0=x1buf[:, par * D:(par + 1) * D],
                        scalar1=1.0 / cfg.cores, scalar2=None, op0=ALU.mult))
                scale_war[par] = dve.mark()

                @block.sync
                def _(s, tc=tc, par=par):
                    r = tc * 128
                    h1 = dmas(s, lambda r=r: s.dma_start(
                        out=x1_ds[beta][r:r + 128, :],
                        in_=x1buf[:, par * D:(par + 1) * D]),
                        waits=[(dve, scale_war[par])])
                    h2 = dmas(s, lambda r=r, par=par: s.dma_start(
                        out=opart_ds[beta][r:r + 128, :], in_=x18slot[par]))
                    x1war[par] = h1 + h2

            for tc in range(TC):
                par = tc % 2
                yo = par * D
                so = par * 8

                @block.sync
                def _(s, tc=tc, yo=yo, par=par):
                    w = [(ccx, arq[tc // 8])]
                    if scale_war[par]:
                        w.append((dve, scale_war[par]))
                    if tc < 2:
                        w += list(out_st_h[0])
                    dmas(s, lambda: s.dma_start(
                        out=ybuf[:, yo:yo + D],
                        in_=psum_d[xr + tc * 128:xr + (tc + 1) * 128, :]),
                        waits=w)
                xl = dmas.mark()

                @block.vector
                def _(v, yo=yo, so=so, par=par, xl=xl):
                    Y = ybuf[:, yo:yo + D]
                    xb = x1buf[:, par * D:(par + 1) * D]
                    dve(v, lambda: v.tensor_scalar(
                        out=xb, in0=Y, scalar1=1.0, scalar2=None,
                        op0=ALU.mult, op1=ALU.add,
                        accum_out=stat_f[:, so + 0:so + 1]),
                        waits=[xl] + list(x1war[par]))
                    dve(v, lambda: v.scalar_tensor_tensor(
                        out=xb, in0=Y, scalar=1.0, in1=Y,
                        op0=ALU.mult, op1=ALU.mult,
                        accum_out=stat_f[:, so + 1:so + 2]))
                    dve(v, lambda: v.tensor_scalar(
                        out=stat_f[:, so + 2:so + 3],
                        in0=stat_f[:, so + 0:so + 1],
                        scalar1=1.0 / D, scalar2=None, op0=ALU.mult))
                    dve(v, lambda: v.tensor_mul(
                        stat_f[:, so + 3:so + 4], stat_f[:, so + 2:so + 3],
                        stat_f[:, so + 2:so + 3]))
                    dve(v, lambda: v.scalar_tensor_tensor(
                        out=stat_f[:, so + 4:so + 5],
                        in0=stat_f[:, so + 1:so + 2], scalar=1.0 / D,
                        in1=stat_f[:, so + 3:so + 4],
                        op0=ALU.mult, op1=ALU.subtract))
                    dve(v, lambda: v.tensor_scalar(
                        out=stat_f[:, so + 4:so + 5],
                        in0=stat_f[:, so + 4:so + 5],
                        scalar1=1e-5, scalar2=None, op0=ALU.add))
                stats_m[tc] = dve.mark()
                if tc >= 1:
                    ln1_scale(tc - 1)
            ln1_scale(TC - 1)

            @block.scalar
            def _(s):
                for tc in range(TC):
                    so = (tc % 2) * 8
                    act(s, lambda so=so, tc=tc: s.activation(
                        out=stat_f[:, so + 5:so + 6],
                        in_=stat_f[:, so + 4:so + 5], func=AF.Sqrt),
                        waits=[(dve, stats_m[tc])], noser=True)
            assert act.count == ln_base + TC
            x1_dma = dmas.mark()
            prev_x1_dma = x1_dma

            if not ph("x1bf"):
                continue

            @block.gpsimd
            def _(g):
                dmag(g, lambda: g.dma_start(
                    out=x1bf_d[xr:xr + T, :], in_=x1_ds[beta][:, :]),
                    waits=[x1_dma])
            x1bf_done = dmag.mark()

            if not ph("gate"):
                continue
            x1fm_war = {0: 0}
            for dc in range(DC):
                par = 0
                fo = 0

                @block.sync
                def _(s, dc=dc, fo=fo, par=par):
                    w = [x1bf_done]
                    if x1fm_war[par]:
                        w.append((pe, x1fm_war[par]))
                    dmas(s, lambda: s.dma_start(
                        out=x1fm_buf[:, fo:fo + T],
                        in_=x1bf_d[xr:xr + T, dc * 128:(dc + 1) * 128],
                        transpose=True), waits=w)
                x1fm_ld = dmas.mark()

                for ft in range(T // 512):
                    bk = "A" if ft % 2 == 0 else "B"
                    dst = chA if ft % 2 == 0 else chB

                    @block.tensor
                    def _(t, dc=dc, ft=ft, fo=fo, dst=dst, bk=bk,
                          x1fm_ld=x1fm_ld):
                        w = [x1fm_ld] + chw(bk)
                        pe(t, lambda: t.matmul(
                            dst[0:NE, 0:512],
                            gatew_sb[:, dc * NE:(dc + 1) * NE],
                            x1fm_buf[:, fo + ft * 512:fo + (ft + 1) * 512],
                            start=True, stop=True), waits=w)
                    gmm = pe.mark()

                    @block.vector
                    def _(v, dc=dc, ft=ft, dst=dst, gmm=gmm):
                        if dc == 0:
                            dve(v, lambda: v.tensor_copy(
                                out=gacc[:, ft * 512:(ft + 1) * 512],
                                in_=dst[0:NE, 0:512]), waits=[(pe, gmm)])
                        else:
                            dve(v, lambda: v.tensor_add(
                                gacc[:, ft * 512:(ft + 1) * 512],
                                gacc[:, ft * 512:(ft + 1) * 512],
                                dst[0:NE, 0:512]), waits=[(pe, gmm)])
                    chfree[bk] = (dve, dve.mark())
                x1fm_war[par] = pe.mark()
            prev_gate_pe = pe.mark()
            gate_done = dve.mark()

            for tg in range(0, TC, 8):
                n = min(8, TC - tg)
                bk = "A" if (tg // 8) % 2 == 0 else "B"
                base = 0 if bk == "A" else 512

                @block.tensor
                def _(t, tg=tg, n=n, base=base, bk=bk):
                    w = [(dve, gate_done)] + chw(bk)
                    for i in range(n):
                        pe(t, lambda i=i: t.transpose(
                            chain[:, base + i * 8:base + i * 8 + NE],
                            gacc[:, (tg + i) * 128:(tg + i + 1) * 128],
                            ident_f[0:NE, 0:NE]), waits=w if i == 0 else ())
                gt = pe.mark()

                @block.vector
                def _(v, tg=tg, n=n, base=base, gt=gt):
                    for i in range(n):
                        dve(v, lambda i=i: v.tensor_copy(
                            out=gtok[:, (tg + i) * NE:(tg + i + 1) * NE],
                            in_=chain[:, base + i * 8:base + i * 8 + NE]),
                            waits=[(pe, gt)] if i == 0 else ())
                chfree[bk] = (dve, dve.mark())
            gtok_done = dve.mark()

            if not ph("sm"):
                continue
            @block.scalar
            def _(s):
                act(s, lambda: s.activation(out=probs_tok[:, :], in_=gtok[:, :],
                                            func=AF.Exp),
                    waits=[(dve, gtok_done)], noser=True)
            sm_exp = act.mark()

            @block.vector
            def _(v):
                pt3 = probs_tok[:, :].rearrange("p (tc e) -> p tc e", e=NE)
                dve(v, lambda: v.reduce_sum(ssum[:, :], pt3, axis=AX.X),
                    waits=[(act, sm_exp)])
                dve(v, lambda: v.reciprocal(ssum[:, :], ssum[:, :]))
                for e in range(NE):
                    dve(v, lambda e=e: v.tensor_mul(
                        pt3[:, :, e], pt3[:, :, e], ssum[:, :]))
                dve(v, lambda: v.memset(tau_lo[:, :], 0.0))
                dve(v, lambda: v.memset(tau_hi[:, :], 1.0))
            probs_done = dve.mark()

            def count_pass(tau_src, out_mask):
                @block.vector
                def _(v):
                    dve(v, lambda: v.tensor_scalar(
                        out=tau_m8[:, :], in0=ind_em[:, :],
                        scalar1=tau_src[0:NE, 0:1], scalar2=None, op0=ALU.mult))
                tm = dve.mark()

                @block.tensor
                def _(t, tm=tm):
                    w = [(dve, tm)] + chw("A")
                    pe(t, lambda: t.matmul(chA[:, 0:TC * NE], ones8_f[:, :],
                                           tau_m8[:, :], start=True, stop=True),
                       waits=w)
                taumm = pe.mark()

                @block.vector
                def _(v, taumm=taumm):
                    dve(v, lambda: v.tensor_tensor(
                        out=out_mask[:, :], in0=probs_tok[:, :],
                        in1=chA[:, 0:TC * NE], op=ALU.is_gt),
                        waits=[(pe, taumm)])
                    om3 = out_mask[:, :].rearrange("p (tc e) -> p e tc", e=NE)
                    dve(v, lambda: v.reduce_sum(part8[:, :], om3, axis=AX.X))
                chfree["A"] = (dve, dve.mark())
                pm = dve.mark()

                @block.tensor
                def _(t, pm=pm):
                    w = [(dve, pm)] + chw("B")
                    pe(t, lambda: t.matmul(chB[0:NE, 0:1], part8[:, :],
                                           onescol_f[:, 0:1],
                                           start=True, stop=True), waits=w)
                cmm = pe.mark()

                @block.vector
                def _(v, cmm=cmm):
                    dve(v, lambda: v.tensor_copy(out=cnt8[:, :],
                                                 in_=chB[0:NE, 0:1]),
                        waits=[(pe, cmm)])
                chfree["B"] = (dve, dve.mark())

            if not ph("bis"):
                continue
            TN = TC * NE
            cmp_x = x1buf[:, 0:3 * TN]
            tau_m8x = x1buf[0:NE, 4 * TN:7 * TN]
            part8x = x1buf[:, 7 * TN:7 * TN + 3 * NE]
            delta = tau_hi
            for it in range(18):
                @block.vector
                def _(v):
                    dve(v, lambda: v.tensor_scalar(
                        out=delta[:, :], in0=delta[:, :], scalar1=0.25,
                        scalar2=None, op0=ALU.mult))
                    dve(v, lambda: v.tensor_add(bs_sc[:, 0:1], tau_lo[:, :],
                                                delta[:, :]))
                    dve(v, lambda: v.tensor_add(bs_sc[:, 1:2], bs_sc[:, 0:1],
                                                delta[:, :]))
                    dve(v, lambda: v.tensor_add(bs_sc[:, 2:3], bs_sc[:, 1:2],
                                                delta[:, :]))
                    for j in range(3):
                        dve(v, lambda j=j: v.tensor_scalar(
                            out=tau_m8x[:, j * TN:(j + 1) * TN],
                            in0=ind_em[:, :], scalar1=bs_sc[:, j:j + 1],
                            scalar2=None, op0=ALU.mult))
                tm = dve.mark()

                @block.tensor
                def _(t, tm=tm):
                    w = [(dve, tm)] + chw("A") + chw("B")
                    pe(t, lambda: t.matmul(chA[:, 0:512], ones8_f[:, :],
                                           tau_m8x[:, 0:512],
                                           start=True, stop=True), waits=w)
                    pe(t, lambda: t.matmul(chB[:, 0:256], ones8_f[:, :],
                                           tau_m8x[:, 512:768],
                                           start=True, stop=True))
                taumm = pe.mark()

                @block.vector
                def _(v, taumm=taumm):
                    for j in range(3):
                        src = (chA[:, j * TN:(j + 1) * TN] if j < 2
                               else chB[:, 0:TN])
                        dve(v, lambda j=j, src=src: v.tensor_tensor(
                            out=cmp_x[:, j * TN:(j + 1) * TN],
                            in0=probs_tok[:, :], in1=src, op=ALU.is_gt),
                            waits=[(pe, taumm)] if j == 0 else ())
                    cm4 = cmp_x.rearrange("p (j tc e) -> p j e tc", e=NE,
                                          tc=TC)
                    dve(v, lambda: v.reduce_sum(part8x[:, :], cm4, axis=AX.X))
                pm = dve.mark()

                @block.tensor
                def _(t, pm=pm):
                    for j in range(3):
                        pe(t, lambda j=j: t.matmul(
                            chB[0:NE, 300 + j:301 + j],
                            part8x[:, j * NE:(j + 1) * NE],
                            onescol_f[:, 0:1], start=True, stop=True),
                           waits=[(dve, pm)] if j == 0 else ())
                cmm = pe.mark()

                @block.vector
                def _(v, cmm=cmm):
                    dve(v, lambda: v.tensor_scalar(
                        out=bs_sc[:, 4:7], in0=chB[0:NE, 300:303],
                        scalar1=float(K), scalar2=None, op0=ALU.is_ge),
                        waits=[(pe, cmm)])
                    dve(v, lambda: v.reduce_sum(
                        bs_sc[:, 7:8], bs_sc[:, 4:7], axis=AX.X))
                    dve(v, lambda: v.scalar_tensor_tensor(
                        out=tau_lo[:, :], in0=bs_sc[:, 7:8],
                        scalar=delta[:, 0:1], in1=tau_lo[:, :],
                        op0=ALU.mult, op1=ALU.add))
            @block.vector
            def _(v):
                dve(v, lambda: v.tensor_add(tau_hi[:, :], tau_lo[:, :],
                                            delta[:, :]))

            if not ph("mask"):
                continue
            count_pass(tau_hi, hi_tok)

            @block.vector
            def _(v):
                dve(v, lambda: v.tensor_scalar(
                    out=fill8[:, :], in0=cnt8[:, :], scalar1=-1.0,
                    scalar2=float(K), op0=ALU.mult, op1=ALU.add))
            count_pass(tau_lo, tie_tok)

            @block.vector
            def _(v):
                dve(v, lambda: v.tensor_sub(tie_tok[:, :], tie_tok[:, :],
                                            hi_tok[:, :]))
            masks_done = dve.mark()

            for tg in range(0, TC, 2):
                n = min(2, TC - tg)
                bk = "A" if (tg // 2) % 2 == 0 else "B"
                base = 0 if bk == "A" else 512

                @block.tensor
                def _(t, tg=tg, n=n, base=base, bk=bk):
                    w = [(dve, masks_done)] + chw(bk)
                    for i in range(n):
                        pe(t, lambda i=i: t.transpose(
                            chain[0:NE, base + (2 * i) * 128:
                                  base + (2 * i) * 128 + 128],
                            hi_tok[:, (tg + i) * NE:(tg + i + 1) * NE],
                            ident_f[:, :]), waits=w if i == 0 else ())
                        pe(t, lambda i=i: t.transpose(
                            chain[0:NE, base + (2 * i + 1) * 128:
                                  base + (2 * i + 1) * 128 + 128],
                            tie_tok[:, (tg + i) * NE:(tg + i + 1) * NE],
                            ident_f[:, :]))
                ht = pe.mark()

                @block.vector
                def _(v, tg=tg, n=n, base=base, ht=ht):
                    for i in range(n):
                        w = [(pe, ht)] + list(out_st_h[0])
                        dve(v, lambda i=i: v.tensor_copy(
                            out=hi_b[0:NE, (tg + i) * 128:(tg + i + 1) * 128],
                            in_=chain[0:NE, base + (2 * i) * 128:
                                      base + (2 * i) * 128 + 128]),
                            waits=w if i == 0 else ())
                        dve(v, lambda i=i: v.tensor_copy(
                            out=tie_b[0:NE, (tg + i) * 128:(tg + i + 1) * 128],
                            in_=chain[0:NE, base + (2 * i + 1) * 128:
                                      base + (2 * i + 1) * 128 + 128]))
                chfree[bk] = (dve, dve.mark())
            em_done = dve.mark()

            if not ph("sel"):
                continue
            @block.vector
            def _(v):
                HIe = hi_b[0:NE, :]
                TIe = tie_b[0:NE, :]
                RNK = rank_f[0:NE, :]
                dve(v, lambda: v.tensor_tensor_scan(
                    out=RNK, data0=TIe, data1=TIe, initial=0.0,
                    op0=ALU.add, op1=ALU.bypass), waits=[(dve, em_done)])
                dve(v, lambda: v.tensor_scalar(
                    out=RNK, in0=RNK,
                    scalar1=fill8[0:NE, 0:1], scalar2=None, op0=ALU.is_le))
                dve(v, lambda: v.tensor_mul(TIe, TIe, RNK))
                dve(v, lambda: v.tensor_add(TIe, TIe, HIe))
                dve(v, lambda: v.tensor_tensor_scan(
                    out=RNK, data0=TIe, data1=TIe, initial=0.0,
                    op0=ALU.add, op1=ALU.bypass))
                dve(v, lambda: v.tensor_scalar(
                    out=RNK, in0=RNK, scalar1=-1.0,
                    scalar2=None, op0=ALU.add))
            gs_done = dve.mark()

            for tg in range(0, TC, 8):
                n = min(8, TC - tg)
                bk = "A" if (tg // 8) % 2 == 0 else "B"
                base = 0 if bk == "A" else 512

                @block.tensor
                def _(t, tg=tg, n=n, base=base, bk=bk):
                    w = [(dve, gs_done)] + chw(bk)
                    for i in range(n):
                        pe(t, lambda i=i: t.transpose(
                            chain[:, base + i * 2:base + i * 2 + 1],
                            rank_f[0:1, (tg + i) * 128:(tg + i + 1) * 128],
                            ident_f[0:1, 0:1]), waits=w if i == 0 else ())
                        pe(t, lambda i=i: t.transpose(
                            chain_bf[:, 2 * (base + i * 2 + 1):
                                     2 * (base + i * 2 + 1) + 1],
                            tie_b[0:1, (tg + i) * 128:(tg + i + 1) * 128],
                            ident_b[0:1, 0:1]))
                gst = pe.mark()

                @block.vector
                def _(v, tg=tg, n=n, base=base, gst=gst):
                    for i in range(n):
                        dve(v, lambda i=i: v.tensor_copy(
                            out=gs_tok[:, (tg + i) * 2:(tg + i) * 2 + 1],
                            in_=chain[:, base + i * 2:base + i * 2 + 1]),
                            waits=[(pe, gst)] if i == 0 else ())
                        dve(v, lambda i=i: v.tensor_copy(
                            out=gs_tok[:, (tg + i) * 2 + 1:(tg + i) * 2 + 2],
                            in_=chain_bf[:, 2 * (base + i * 2 + 1):
                                         2 * (base + i * 2 + 1) + 1]))
                chfree[bk] = (dve, dve.mark())
            gstok_done = dve.mark()

            if not ph("ex"):
                continue
            if beta > 0:
                @block.gpsimd
                def _(g):
                    pool(g, lambda: g.iota(
                        iota_j[:, :], pattern=[[1, K]], base=0,
                        channel_multiplier=0,
                        allow_small_or_imprecise_dtypes=True),
                        waits=[(pe, prev_ffn_pe[0])])
            iota_m = pool.mark()
            ex_prev = pe.mark()
            for tc in range(TC):
                @block.vector
                def _(v, tc=tc, ex_prev=ex_prev):
                    w = [(pe, ex_prev)]
                    if tc == 0:
                        w.append((dve, gstok_done))
                        w.append((pool, iota_m))
                    dve(v, lambda: v.tensor_copy(out=payload[:, 0:1],
                                                 in_=iota_colb[:, :]), waits=w)
                    dve(v, lambda: v.tensor_copy(out=payload[:, 1:2],
                                                 in_=chunkv_b[:, tc:tc + 1]))
                    dve(v, lambda: v.tensor_copy(
                        out=payload[:, 2:3],
                        in_=probs_tok[:, tc * NE:tc * NE + 1]))
                    dve(v, lambda: v.tensor_scalar(
                        out=pc_b[:, :], in0=iota_j[:, :],
                        scalar1=gs_tok[:, 2 * tc:2 * tc + 1],
                        scalar2=gs_tok[:, 2 * tc + 1:2 * tc + 2],
                        op0=ALU.is_equal, op1=ALU.mult))
                pcm = dve.mark()

                @block.tensor
                def _(t, tc=tc, pcm=pcm):
                    w = [(dve, pcm)]
                    if tc == 0:
                        w += chw("A") + chw("B")
                    for kf in range((K + 511) // 512):
                        f0, f1 = kf * 512, min(K, kf * 512 + 512)
                        pe(t, lambda f0=f0, f1=f1: t.matmul(
                            chain[0:3, f0:f1], payload[:, :], pc_b[:, f0:f1],
                            start=(tc == 0), stop=(tc == TC - 1)),
                           waits=w if kf == 0 else ())
                ex_prev = pe.mark()
            ex_mm = pe.mark()

            @block.vector
            def _(v):
                dve(v, lambda: v.tensor_copy(out=mg_f[:, :],
                                             in_=chain[0:3, 0:K]),
                    waits=[(pe, ex_mm)])
            chfree["A"] = (dve, dve.mark())
            chfree["B"] = (dve, dve.mark())
            mg_done = dve.mark()

            if not ph("mg"):
                continue
            prev_ffn = prev_ffn_dma if beta > 0 else []
            for ks in range(KC):
                bk = "A" if ks % 2 == 0 else "B"
                base = 0 if bk == "A" else 512

                @block.tensor
                def _(t, ks=ks, base=base, bk=bk):
                    w = [(dve, mg_done)] + chw(bk)
                    pe(t, lambda: t.transpose(
                        chain_bf[:, 2 * base:2 * base + 3],
                        mg_f[0:3, ks * 128:(ks + 1) * 128],
                        ident_b[0:3, 0:3]), waits=w)
                mt = pe.mark()

                @block.vector
                def _(v, ks=ks, base=base, mt=mt):
                    dve(v, lambda: v.tensor_copy(
                        out=mg3[:, 0:3], in_=chain_bf[:, 2 * base:2 * base + 3]),
                        waits=[(pe, mt)] + list(prev_ffn))
                    dve(v, lambda: v.scalar_tensor_tensor(
                        out=mcols[:, ks:ks + 1], in0=mg3[:, 1:2],
                        scalar=128.0, in1=mg3[:, 0:1],
                        op0=ALU.mult, op1=ALU.add))
                    dve(v, lambda: v.tensor_copy(out=gcol[:, ks:ks + 1],
                                                 in_=mg3[:, 2:3]))
                chfree[bk] = (dve, dve.mark())

            @block.vector
            def _(v):
                dve(v, lambda: v.tensor_copy(out=midx[:, :], in_=mcols[:, :]))
            midx_done = dve.mark()

            if not ph("ffn"):
                continue
            NG = K // 128
            NGH = NG // 2
            accpair = [sc1, accv]
            acc_war = {0: 0, 1: 0}
            scat_h = []
            scat_mark = dmag.mark()
            xin_war = [0]
            mm2_last = 0
            for hf in range(2):
                xinT_done = 0
                for g in range(NGH):
                    par = g % 2
                    col = hf * NGH + g

                    @block.gpsimd
                    def _(gg, g=g, par=par, col=col):
                        w = [(dve, midx_done), x1_dma]
                        if xin_war[0]:
                            w.append((pe, xin_war[0]))
                        dmag(gg, lambda: gg.indirect_dma_start(
                            out=xin_sb[:, par * D:(par + 1) * D],
                            out_offset=None, in_=x1_ds[beta],
                            in_offset=IndirectOffsetOnAxis(
                                ap=midx[:, col:col + 1], axis=0)), waits=w)
                    gth = dmag.mark()

                    bk = "A" if par == 0 else "B"
                    slot = chAb if par == 0 else chBb

                    @block.tensor
                    def _(t, par=par, slot=slot, bk=bk, gth=gth):
                        w = [gth] + chw(bk)
                        if mm2_last:
                            w.append((pe, mm2_last))
                        for dc in range(DC):
                            pe(t, lambda dc=dc, par=par: t.transpose(
                                slot(dc),
                                xin_sb[:, par * D + dc * 128:
                                       par * D + (dc + 1) * 128],
                                ident_b[:, :]), waits=w if dc == 0 else ())
                    xt = pe.mark()
                    if g >= 1:
                        xin_war[0] = xt

                    @block.vector
                    def _(v, g=g, slot=slot, xt=xt):
                        for dc in range(DC):
                            dve(v, lambda dc=dc: v.tensor_copy(
                                out=xinT_sb[:, dc * 512 + g * 128:
                                            dc * 512 + (g + 1) * 128],
                                in_=slot(dc)),
                                waits=[(pe, xt)] if dc == 0 else ())
                    chfree[bk] = (dve, dve.mark())
                xinT_done = dve.mark()
                xin_war[0] = pe.mark()

                pe_base = pe.count
                w1_lds = []
                for hc in range(HC):
                    @block.gpsimd
                    def _(gg, hc=hc):
                        w = [setup_dmag] if (beta == 0 and hf == 0) else []
                        if hc >= 2:
                            w.append((pe, pe_base + (hc - 1) * DC))
                        dmag(gg, lambda hc=hc: gg.dma_start(
                            out=w1_buf[:, (hc % 2) * D:(hc % 2) * D + D],
                            in_=w1bf_d[:, hc * D:(hc + 1) * D]), waits=w)
                    w1_lds.append(dmag.mark())

                gelu_base = act.count
                mm1_m = []
                @block.tensor
                def _(t, xinT_done=xinT_done, gelu_base=gelu_base,
                      w1_lds=w1_lds):
                    for hc in range(HC):
                        par = hc % 2
                        w = [w1_lds[hc]]
                        if hc == 0:
                            w.append((dve, xinT_done))
                        if hc >= 2:
                            w.append((act, gelu_base + hc - 1))
                        for dc in range(DC):
                            pe(t, lambda hc=hc, par=par, dc=dc: t.matmul(
                                sc0[:, par * 512:(par + 1) * 512],
                                w1_buf[:, (hc % 2) * D + dc * 128:
                                       (hc % 2) * D + (dc + 1) * 128],
                                xinT_sb[:, dc * 512:(dc + 1) * 512],
                                start=(dc == 0), stop=(dc == DC - 1)),
                               waits=w if dc == 0 else ())
                        mm1_m.append(pe.mark())
                assert pe.count == pe_base + HC * DC

                @block.scalar
                def _(s, mm1_m=mm1_m):
                    for hc in range(HC):
                        par = hc % 2
                        act(s, lambda hc=hc, par=par: s.activation(
                            out=hT_sb[:, hc * 512:(hc + 1) * 512],
                            in_=sc0[:, par * 512:(par + 1) * 512],
                            func=AF.Gelu_apprx_tanh),
                            waits=[(pe, mm1_m[hc])], noser=True)
                assert act.count == gelu_base + HC
                gelu_mk = [gelu_base + hc + 1 for hc in range(HC)]

                for g in range(NGH):
                    col = hf * NGH + g
                    par = g % 2
                    acc = accpair[par]

                    @block.tensor
                    def _(t, g=g, par=par, acc=acc, gelu_mk=gelu_mk):
                        for hc in range(HC):
                            w = [(act, gelu_mk[HC - 1])]
                            if hc == 0 and acc_war[par]:
                                w.append((dve, acc_war[par]))
                            if hc == 0 and hf == 0 and g < 2:
                                w.append((dve, at_evac_prev[0]))
                            for dh in range(2):
                                pe(t, lambda hc=hc, g=g, dh=dh, acc=acc:
                                   t.matmul(
                                    acc[:, dh * 512:(dh + 1) * 512],
                                    hT_sb[:, hc * 512 + g * 128:
                                          hc * 512 + (g + 1) * 128],
                                    w2_sb[:, hc * D + dh * 512:
                                          hc * D + dh * 512 + 512],
                                    start=(hc == 0), stop=(hc == HC - 1)),
                                   waits=w if (hc == 0 and dh == 0) else ())
                    fmm = pe.mark()
                    mm2_last = fmm

                    @block.vector
                    def _(v, col=col, acc=acc, fmm=fmm):
                        dve(v, lambda: v.tensor_scalar(
                            out=fw_sb[:, :], in0=acc[:, :],
                            scalar1=gcol[:, col:col + 1],
                            scalar2=None, op0=ALU.mult),
                            waits=[(pe, fmm)] + list(scat_h))
                    acc_war[par] = dve.mark()
                    fw_done = dve.mark()

                    @block.gpsimd
                    def _(gg, col=col, fw_done=fw_done, scat_mark=scat_mark):
                        h = dmag(gg, lambda: gg.indirect_dma_start(
                            out=opart_ds[beta],
                            out_offset=IndirectOffsetOnAxis(
                                ap=midx[:, col:col + 1], axis=0),
                            in_=fw_sb[:, :], in_offset=None,
                            compute_op=ALU.add),
                            waits=[(dve, fw_done), x1_dma, scat_mark])
                        scat_h.clear()
                        scat_h.extend(h)
                    scat_mark = dmag.mark()

            at_evac_prev[0] = acc_war[1]
            ffn_acc_last[0] = max(acc_war[0], acc_war[1])
            prev_gelu_m[0] = act.count
            prev_ffn_pe[0] = pe.mark()
            prev_ffn_dma = dmag.mark()
            if not ph("rs"):
                continue

            @block.gpsimd
            def _(g, scat_mark=scat_mark):
                dmag.wait(g, scat_mark)
                ccx(g, lambda: g.collective_compute(
                    "ReduceScatter", ALU.add, replica_groups=groups,
                    ins=[opart_ds[beta][:, :]],
                    outs=[ored_d[beta * OUTR:(beta + 1) * OUTR, :]]))
            rs_done = ccx.mark()

            if not ph("ln2"):
                continue

            def emit_ln2(beta=beta, rs_done=rs_done):
                ln2_body(beta, rs_done)

            if beta == NB - 1:
                emit_ln2()
            else:
                pending_ln2[0] = emit_ln2

        if dbg_tensors:
            dbg_map = {"ppart": ppart_d, "psumd": psum_d, "xbf": xbf_d,
                       "x1bf": x1bf_d, "ored": ored_d}
            for b in range(NB):
                dbg_map[f"x1tok{b}"] = x1_ds[b]
                dbg_map[f"opart{b}"] = opart_ds[b]

            @block.gpsimd
            def _(g):
                dmas.wait(g)
                dmag.wait(g)
                ccx.wait(g)
                for nm in dbg_tensors:
                    src = dbg_map[nm]
                    dst = nc.dram_tensor(
                        "dbg_" + nm, list(src.shape),
                        src.dtype, kind="ExternalOutput").ap()
                    dmag(g, lambda src=src, dst=dst: g.dma_start(
                        out=dst, in_=src))
                dmag.wait(g)

        @block.sync
        def _(s):
            dmas.wait(s)

        @block.gpsimd
        def _(g):
            dmag.wait(g)
            ccx.wait(g)
            ln2st.wait(g)

    return nc


def make_inmaps(cfg: Cfg, x, Wqkv, Wproj, gate_w, w1, w2, lq1, lk1, lq2, lk2,
                rms_scale):
    T, D, NB, NE, HC = cfg.T, cfg.D, cfg.NB, cfg.NE, cfg.HC
    xf = np.ascontiguousarray(np.asarray(x).reshape(NB * T, D), dtype=np.float32)
    xts = np.ascontiguousarray(
        xf.reshape(NB * T // cfg.TB, cfg.TB, cfg.DC, 128)
        .transpose(3, 0, 2, 1).reshape(128, NB * T * cfg.DC),
        dtype=np.float32)
    lamv = np.ascontiguousarray(
        np.concatenate([lq1, lq2, lk1, lk2])[None, :], dtype=np.float32)
    rmsrow = np.ascontiguousarray(
        np.concatenate([rms_scale, rms_scale])[None, :], dtype=np.float32)
    in_maps = []
    for c in range(cfg.cores):
        A, B = 2 * c, 2 * c + 1
        qcols = []
        for hh in (A, B):
            qcols += list(range(32 * hh, 32 * hh + 32))
            qcols += list(range(D // 2 + 32 * hh, D // 2 + 32 * hh + 32))
        kcols = [D + cc for cc in qcols]
        vcols = []
        for hh in (A, B):
            vcols += list(range(2 * D + 64 * hh, 2 * D + 64 * hh + 64))
        wq = np.ascontiguousarray(Wqkv[:, qcols + kcols + vcols], dtype=np.float32)
        wp = np.ascontiguousarray(Wproj[128 * c:128 * (c + 1), :], dtype=np.float32)
        perm = [c] + [e for e in range(NE) if e != c]
        gw = np.ascontiguousarray(gate_w[:, perm], dtype=np.float32)
        w1c = np.ascontiguousarray(
            np.asarray(w1)[c].reshape(cfg.DC, 128, HC, 128)
            .transpose(1, 2, 0, 3).reshape(128, HC * cfg.DC * 128),
            dtype=np.float32)
        w2c = np.ascontiguousarray(np.asarray(w2)[c], dtype=np.float32)
        in_maps.append({
            "x": xf, "xts": xts, "wqkv": wq, "wproj": wp, "gatew": gw,
            "w1": w1c, "w2": w2c, "lamv": lamv, "rmsrow": rmsrow,
        })
    return in_maps


def assemble(cfg: Cfg, results):
    out = np.empty((cfg.NB, cfg.T, cfg.D), dtype=np.float32)
    for c in range(cfg.cores):
        r = np.asarray(results[c]["out"]).reshape(cfg.NB, cfg.OUTR, cfg.D)
        out[:, c * cfg.OUTR:(c + 1) * cfg.OUTR, :] = r
    return out


def kernel(L0_attn_x, L0_attn_Wqkv, L0_attn_Wproj, L0_attn_lambda_q1,
           L0_attn_lambda_k1, L0_attn_lambda_q2, L0_attn_lambda_k2,
           L0_attn_rms_scale, L0_ffn_gate_w, L0_ffn_w1, L0_ffn_w2):
    cfg = FULL
    in_maps = make_inmaps(
        cfg, np.asarray(L0_attn_x), np.asarray(L0_attn_Wqkv),
        np.asarray(L0_attn_Wproj), np.asarray(L0_ffn_gate_w),
        np.asarray(L0_ffn_w1), np.asarray(L0_ffn_w2),
        np.asarray(L0_attn_lambda_q1), np.asarray(L0_attn_lambda_k1),
        np.asarray(L0_attn_lambda_q2), np.asarray(L0_attn_lambda_k2),
        np.asarray(L0_attn_rms_scale))
    nc = build_nc(cfg)
    res = run_bass_kernel_spmd(nc, in_maps, core_ids=list(range(cfg.cores)))
    return assemble(cfg, res.results)


# version 23
# speedup vs baseline: 1.0022x; 1.0011x over previous
import math
from contextlib import ExitStack

import numpy as np
import ml_dtypes

import concourse.bass as bass
import concourse.mybir as mybir
from concourse.bass import IndirectOffsetOnAxis
from concourse.bass_utils import run_bass_kernel_spmd

F32 = mybir.dt.float32
BF16 = mybir.dt.bfloat16
U32 = mybir.dt.uint32
AF = mybir.ActivationFunctionType
ALU = mybir.AluOpType
AX = mybir.AxisListType

LAMBDA_INIT = 0.8 - 0.6 * math.exp(-0.3 * 1)


class Cfg:
    def __init__(self, cores=8, T=4096, HID=4096, NB=4, CAP=2.0):
        self.cores = cores
        self.T = T
        self.D = 128 * cores
        self.HID = HID
        self.NB = NB
        self.NE = cores
        self.K = int(CAP * T / self.NE)
        D = self.D
        self.DC = D // 128
        self.TC = T // 128
        self.QB = 256
        self.NQB = T // self.QB
        self.KCN = T // 128
        self.HC = HID // 128
        self.KC = self.K // 128
        self.SB = min(128, self.K)
        self.NSB = self.K // self.SB
        self.OUTR = T // cores
        self.TB = 256
        self.NTB = T // self.TB
        self.PO_F = min(512, D)
        self.POT = D // self.PO_F


FULL = Cfg()


class Eng:

    def __init__(self, nc, ctx, name, dma=False, serialize=True):
        self.sem = ctx.enter_context(nc.semaphore(name))
        self.count = 0
        self.inc = 16 if dma else 1
        self.serialize = serialize

    def mark(self):
        return self.count

    def __call__(self, eng, emit, waits=(), inc=True, noser=False):
        merged = {}

        def add(other, cnt):
            if cnt and cnt > 0:
                k = id(other.sem)
                prev = merged.get(k)
                if prev is None or cnt > prev[1]:
                    merged[k] = (other, cnt)

        for item in waits:
            if isinstance(item, list):
                for (o, c) in item:
                    add(o, c)
            else:
                add(*item)
        if self.serialize and not noser and self.count:
            add(self, self.count)
        for (o, c) in merged.values():
            eng.wait_ge(o.sem, c)
        inst = emit()
        if inc:
            inst.then_inc(self.sem, self.inc)
            self.count += self.inc
        return self.count

    def wait(self, eng, count=None):
        eng.wait_ge(self.sem, self.count if count is None else count)


class DmaPool:

    def __init__(self, nc, ctx, name, nlanes=8):
        self.lanes = [Eng(nc, ctx, f"{name}{i}", dma=True, serialize=True)
                      for i in range(nlanes)]
        self.i = 0

    def __call__(self, eng, emit, waits=()):
        lane = self.lanes[self.i]
        self.i = (self.i + 1) % len(self.lanes)
        lane(eng, emit, waits)
        return [(lane, lane.count)]

    def mark(self):
        return [(l, l.count) for l in self.lanes if l.count]

    def wait(self, eng, marks=None):
        for (l, c) in (self.mark() if marks is None else marks):
            if c:
                eng.wait_ge(l.sem, c)


PHASES = ["qkv", "vtr", "attn", "ar", "ln1", "x1bf", "gate", "sm", "bis",
          "mask", "sel", "ex", "mg", "ffn", "rs", "ln2", "all"]


def build_nc(cfg: Cfg, debug=False, stop_after="all", dbg_tensors=(),
             attn_sub="full"):
    T, D, NB, NE = cfg.T, cfg.D, cfg.NB, cfg.NE
    DC, TC, QB, NQB, KCN = cfg.DC, cfg.TC, cfg.QB, cfg.NQB, cfg.KCN
    HID, HC, K, KC, SB, NSB = cfg.HID, cfg.HC, cfg.K, cfg.KC, cfg.SB, cfg.NSB
    OUTR, TB, NTB = cfg.OUTR, cfg.TB, cfg.NTB
    PO_F, POT = cfg.PO_F, cfg.POT
    groups = [list(range(cfg.cores))]
    NITER = 36

    SL = PHASES.index(stop_after)

    def ph(name):
        return SL >= PHASES.index(name)

    nc = bass.Bass("TRN2", target_bir_lowering=False, debug=debug)

    x_d = nc.dram_tensor("x", [NB * T, D], F32, kind="ExternalInput").ap()
    xts_d = nc.dram_tensor("xts", [128, NB * T * (D // 128)], F32,
                           kind="ExternalInput").ap()
    wqkv_d = nc.dram_tensor("wqkv", [D, 384], F32, kind="ExternalInput").ap()
    wproj_d = nc.dram_tensor("wproj", [128, D], F32, kind="ExternalInput").ap()
    gatew_d = nc.dram_tensor("gatew", [D, NE], F32, kind="ExternalInput").ap()
    w1_d = nc.dram_tensor("w1", [128, HC * DC * 128], F32,
                          kind="ExternalInput").ap()
    w2_d = nc.dram_tensor("w2", [HID, D], F32, kind="ExternalInput").ap()
    lam_d = nc.dram_tensor("lamv", [1, 128], F32, kind="ExternalInput").ap()
    rms_d = nc.dram_tensor("rmsrow", [1, 128], F32, kind="ExternalInput").ap()
    out_d = nc.dram_tensor("out", [NB * OUTR, D], F32, kind="ExternalOutput").ap()

    xbf_d = nc.dram_tensor("xbf", [NB * T, D], BF16).ap()
    xtsbf_d = nc.dram_tensor("xtsbf", [128, NB * T * (D // 128)], BF16).ap()
    w1bf_d = nc.dram_tensor("w1bf", [128, HC * DC * 128], BF16).ap()
    ppart_d = nc.dram_tensor("ppart", [NB * T, D], F32).ap()
    psum_d = nc.dram_tensor("psumd", [NB * T, D], F32,
                            addr_space="Shared" if cfg.cores > 4 else "Local").ap()
    x1_ds = [nc.dram_tensor(f"x1tok{b}", [T, D], F32).ap() for b in range(NB)]
    x1bf_d = nc.dram_tensor("x1bf", [NB * T, D], BF16).ap()
    opart_ds = [nc.dram_tensor(f"opart{b}", [T, D], F32).ap() for b in range(NB)]
    ored_d = nc.dram_tensor("ored", [NB * OUTR, D], F32).ap()

    with ExitStack() as ctx:
        sb = lambda n, s, d: ctx.enter_context(nc.sbuf_tensor(n, s, d))
        ps_ = lambda n, s, d: ctx.enter_context(nc.psum_tensor(n, s, d))

        ident_b = sb("ident_b", [128, 128], BF16)
        ident_f = sb("ident_f", [128, 128], F32)
        iota_col = sb("iota_col", [128, 1], F32)
        iota_row = None

        ones8_f = sb("ones8_f", [NE, 128], F32)
        onesrow_f = sb("onesrow_f", [1, 128], F32)
        onescol_f = sb("onescol_f", [128, 1], F32)
        hT_sb = sb("hT_sb", [128, HC * 512], BF16)
        hT_f32 = hT_sb[:, :].bitcast(F32)
        iota_j = hT_f32[:, T:T + K]
        iota_colb = sb("iota_colb", [128, 1], BF16)
        chunkv_b = sb("chunkv_b", [128, TC], BF16)
        iota_e = sb("iota_e", [NE, TC * NE], F32)
        ind_em = sb("ind_em", [NE, TC * NE], F32)
        biasc = sb("biasc", [128, 6], F32)
        scaletile = sb("scaletile", [128, 64], F32)
        neglam = sb("neglam", [128, 1], F32)
        lam_sb = sb("lam_sb", [1, 128], F32)
        lamp = sb("lamp", [1, 64], F32)
        lamsc = sb("lamsc", [1, 8], F32)
        rmsrow_sb = sb("rmsrow_sb", [1, 128], F32)

        wqkv_sb = sb("wqkv_sb", [128, DC * 384], BF16)
        wproj_sb = sb("wproj_sb", [128, D], BF16)
        gatew_sb = sb("gatew_sb", [128, DC * NE], BF16)
        w2_sb = sb("w2_sb", [128, HC * D], BF16)
        w1_buf = sb("w1_buf", [128, 2 * D], BF16)

        xfm_buf = sb("xfm_buf", [128, 2 * DC * TB], BF16)
        qT_sb = sb("qT_sb", [128, T], BF16)
        kT_sb = sb("kT_sb", [128, T], BF16)

        vaug_sb = sb("vaug_sb", [128, TC * 130], BF16)
        qblk_sb = sb("qblk_sb", [128, 4 * QB], BF16)
        probs_buf = sb("probs_buf", [128, 2 * 4 * QB], BF16)
        zcatT_sb = sb("zcatT_sb", [128, QB], BF16)
        zct_sb = sb("zct_sb", [128, 256], BF16)
        fm_ep = sb("fm_ep", [128, 1024], F32)
        ep_f = sb("ep_f", [128, 4 * QB], F32)
        iota_row = ep_f[:, 128:256]
        assert D <= 4 * QB
        x18buf = ep_f[:, 0:D]
        ybuf = sb("ybuf", [128, 2 * D], F32)
        x1buf = sb("x1buf", [128, 2 * D], F32)
        x1fm_buf = sb("x1fm_buf", [128, max(T, 4 * D)], BF16)
        vT_sb = x1fm_buf[:, 0:T]
        hi_b = x1fm_buf[0:8, 0:T]
        tie_b = ybuf[:, :].bitcast(BF16)[0:8, 0:T]
        rank_f = hT_f32[0:8, 0:T]
        gacc = rank_f[0:NE, :]
        xq_sb = [hT_f32[:, T + K:T + K + 1024],
                 hT_f32[:, T + K + 1024:T + K + 2048]]
        gtok = sb("gtok", [128, TC * NE], F32)
        probs_tok = gtok
        ssum = sb("ssum", [128, TC], F32)
        tau_lo = sb("tau_lo", [NE, 1], F32)
        tau_hi = sb("tau_hi", [NE, 1], F32)
        tau_mid = sb("tau_mid", [NE, 1], F32)
        tau_m8 = sb("tau_m8", [NE, TC * NE], F32)
        cmp_tok = sb("cmp_tok", [128, TC * NE], F32)
        part8 = sb("part8", [128, NE], F32)
        cnt8 = sb("cnt8", [NE, 1], F32)
        cond8 = sb("cond8", [NE, 1], F32)
        bs_sc = sb("bs_sc", [NE, 8], F32)
        hi_tok = cmp_tok
        tie_tok = sb("tie_tok", [128, TC * NE], F32)

        fill8 = sb("fill8", [NE, 1], F32)

        gs_tok = sb("gs_tok", [128, 2 * TC], F32)
        pc_b = ep_f[:, 0:K // 2].bitcast(BF16)
        payload = sb("payload", [128, 3], BF16)
        mg_f = sb("mg_f", [3, K], BF16)
        mg3 = sb("mg3", [128, 4], F32)
        mcols = sb("mcols", [128, KC], F32)
        midx = sb("midx", [128, KC], U32)
        gcol = sb("gcol", [128, KC], F32)
        xin_sb = sb("xin_sb", [128, 2 * D], BF16)
        xinT_sb = sb("xinT_sb", [128, DC * 512], BF16)
        fw_sb = probs_buf[:, 0:D]
        xinT_f32 = xinT_sb[:, :].bitcast(F32)
        pstage2 = [xinT_f32[:, 0:1024], xinT_f32[:, 1024:2048]]
        pstage = pstage2[0]
        lnslots = [ybuf[:, 0:D], ybuf[:, D:2 * D]]
        stat_f = sb("stat_f", [128, 16], F32)

        sc0 = ps_("sc0", [128, 4 * QB], F32)
        sc1 = ps_("sc1", [128, 4 * QB], F32)
        accv = ps_("accv", [128, 1024], F32)
        chain = ps_("chain", [128, 1024], F32)
        chA = chain[:, 0:512]
        chB = chain[:, 512:1024]
        chain_bf = chain[:, :].bitcast(BF16)

        def chAb(i):
            return chain_bf[:, i * 128:(i + 1) * 128]

        def chBb(i):
            return chain_bf[:, 1024 + i * 128:1024 + (i + 1) * 128]

        dmas = DmaPool(nc, ctx, "dmas")
        dmag = DmaPool(nc, ctx, "dmag")
        pe = Eng(nc, ctx, "pe", serialize=False)
        act = Eng(nc, ctx, "act")
        dve = Eng(nc, ctx, "dve")
        pool = Eng(nc, ctx, "pool")
        ccx = Eng(nc, ctx, "ccx", serialize=False)
        ln2st = Eng(nc, ctx, "ln2st", dma=True, serialize=False)

        block = ctx.enter_context(nc.Block())

        chfree = {"A": (dve, 0), "B": (dve, 0)}

        def chw(which):
            e, c = chfree[which]
            return [(e, c)] if c > 0 else []

        xconv_m = []
        @block.gpsimd
        def _(g):
            for dc in range(DC):
                dmag(g, lambda dc=dc: g.dma_start(
                    out=wqkv_sb[:, dc * 384:(dc + 1) * 384],
                    in_=wqkv_d[dc * 128:(dc + 1) * 128, :]))
            dmag(g, lambda: g.dma_start(out=wproj_sb[:, :], in_=wproj_d))
            for dc in range(DC):
                dmag(g, lambda dc=dc: g.dma_start(
                    out=gatew_sb[:, dc * NE:(dc + 1) * NE],
                    in_=gatew_d[dc * 128:(dc + 1) * 128, :]))
            for hc in range(HC):
                dmag(g, lambda hc=hc: g.dma_start(
                    out=w2_sb[:, hc * D:(hc + 1) * D],
                    in_=w2_d[hc * 128:(hc + 1) * 128, :]))
            for ci in range(8):
                cw = NB * T * DC // 8
                dmag(g, lambda ci=ci, cw=cw: g.dma_start(
                    out=xtsbf_d[:, ci * cw:(ci + 1) * cw],
                    in_=xts_d[:, ci * cw:(ci + 1) * cw]))
                xconv_m.append(dmag.mark())
            for ci in range(4):
                cw = HC * DC * 128 // 4
                dmag(g, lambda ci=ci, cw=cw: g.dma_start(
                    out=w1bf_d[:, ci * cw:(ci + 1) * cw],
                    in_=w1_d[:, ci * cw:(ci + 1) * cw]))
            pool(g, lambda: g.iota(iota_col[:, :], pattern=[[0, 1]], base=0,
                                   channel_multiplier=1,
                                   allow_small_or_imprecise_dtypes=True))
            pool(g, lambda: g.iota(iota_row[:, :], pattern=[[1, 128]], base=0,
                                   channel_multiplier=0,
                                   allow_small_or_imprecise_dtypes=True))
            pool(g, lambda: g.iota(iota_j[:, :], pattern=[[1, K]], base=0,
                                   channel_multiplier=0,
                                   allow_small_or_imprecise_dtypes=True))
            pool(g, lambda: g.iota(iota_colb[:, :], pattern=[[0, 1]], base=0,
                                   channel_multiplier=1,
                                   allow_small_or_imprecise_dtypes=True))
            pool(g, lambda: g.iota(chunkv_b[:, :], pattern=[[1, TC]], base=0,
                                   channel_multiplier=0,
                                   allow_small_or_imprecise_dtypes=True))
            pool(g, lambda: g.iota(iota_e[:, :], pattern=[[0, TC], [1, NE]],
                                   base=0, channel_multiplier=0,
                                   allow_small_or_imprecise_dtypes=True))


        @block.sync
        def _(s):
            dmas(s, lambda: s.dma_start(out=lam_sb[:, :], in_=lam_d))
            dmas(s, lambda: s.dma_start(out=rmsrow_sb[:, :], in_=rms_d))

        @block.vector
        def _(v):
            pool.wait(v)
            dmas.wait(v)
            dve(v, lambda: v.memset(ones8_f[:, :], 1.0))
            dve(v, lambda: v.memset(onesrow_f[:, :], 1.0))
            dve(v, lambda: v.memset(onescol_f[:, :], 1.0))
            dve(v, lambda: v.memset(biasc[:, :], 0.0))
            dve(v, lambda: v.memset(biasc[:, 0:1], -8.0))
            dve(v, lambda: v.memset(biasc[:, 1:2], -0.5))
            dve(v, lambda: v.memset(biasc[:, 2:3], 1.0 / math.sqrt(32.0)))
            dve(v, lambda: v.memset(biasc[:, 4:5], 1e-6))
            dve(v, lambda: v.memset(biasc[:, 5:6], 1.0 / 64))
            dve(v, lambda: v.memset(qblk_sb[:, :], 0.0))
            va3 = vaug_sb[:, :].rearrange("p (t c) -> p t c", c=130)
            dve(v, lambda: v.memset(va3[:, :, 64:65], 1.0))
            dve(v, lambda: v.memset(va3[:, :, 129:130], 1.0))
            dve(v, lambda: v.tensor_scalar(
                out=ep_f[0:1, 0:64], in0=rmsrow_sb[0:1, 0:64],
                scalar1=(1.0 - LAMBDA_INIT), scalar2=None, op0=ALU.mult))
            dve(v, lambda: v.tensor_scalar(
                out=ident_b[:, :], in0=iota_row[:, :], scalar1=iota_col[:, :1],
                scalar2=None, op0=ALU.is_equal))
            dve(v, lambda: v.tensor_scalar(
                out=ident_f[:, :], in0=iota_row[:, :], scalar1=iota_col[:, :1],
                scalar2=None, op0=ALU.is_equal))
            dve(v, lambda: v.tensor_scalar(
                out=ind_em[:, :], in0=iota_e[:, :], scalar1=iota_col[:NE, :1],
                scalar2=None, op0=ALU.is_equal))
            dve(v, lambda: v.tensor_mul(lamp[:, :], lam_sb[:, 0:64],
                                        lam_sb[:, 64:128]))
            dve(v, lambda: v.reduce_sum(lamsc[0:1, 0:1], lamp[:, 0:32],
                                        axis=AX.X))
            dve(v, lambda: v.reduce_sum(lamsc[0:1, 1:2], lamp[:, 32:64],
                                        axis=AX.X))

        @block.scalar
        def _(s):
            dve.wait(s)
            act(s, lambda: s.activation(out=lamsc[0:1, 2:4], in_=lamsc[0:1, 0:2],
                                        func=AF.Exp))
        lam_act = act.mark()

        @block.vector
        def _(v):
            act.wait(v)
            dve(v, lambda: v.tensor_sub(lamsc[0:1, 4:5], lamsc[0:1, 3:4],
                                        lamsc[0:1, 2:3]))
        lam_dve2 = dve.mark()

        @block.tensor
        def _(t):
            pe(t, lambda: t.matmul(chA[:, 0:1], onesrow_f[0:1, :],
                                   lamsc[0:1, 4:5], start=True, stop=True),
               waits=[(dve, lam_dve2)])
            pe(t, lambda: t.matmul(chB[:, 0:64], onesrow_f[0:1, :],
                                   ep_f[0:1, 0:64], start=True, stop=True))
        bc_mm = pe.mark()

        @block.vector
        def _(v):
            dve(v, lambda: v.tensor_scalar(
                out=neglam[:, :], in0=chA[:, 0:1], scalar1=-LAMBDA_INIT,
                scalar2=None, op0=ALU.add), waits=[(pe, bc_mm)])
            dve(v, lambda: v.tensor_copy(out=scaletile[:, :], in_=chB[:, 0:64]))
        chfree["A"] = (dve, dve.mark())
        chfree["B"] = (dve, dve.mark())
        setup_dmag = dmag.mark()

        def ln2_body(beta, rs_done):
            nln = (OUTR + 127) // 128
            l2stats = [0] * nln
            l2_base = act.count

            def ln2_scale(oc):
                par = oc % 2
                slot = lnslots[par]
                so = par * 8
                r0 = oc * 128

                @block.vector
                def _(v, slot=slot, so=so, oc=oc):
                    dve(v, lambda: v.reciprocal(
                        stat_f[:, so + 5:so + 6], stat_f[:, so + 5:so + 6]),
                        waits=[(act, l2_base + oc + 1)])
                    dve(v, lambda: v.tensor_scalar(
                        out=slot[:, :], in0=slot[:, :],
                        scalar1=stat_f[:, so + 2:so + 3],
                        scalar2=stat_f[:, so + 5:so + 6],
                        op0=ALU.subtract, op1=ALU.mult))
                l2o = dve.mark()

                @block.gpsimd
                def _(g, r0=r0, slot=slot, l2o=l2o):
                    ln2st(g, lambda: g.dma_start(
                        out=out_d[beta * OUTR + r0:beta * OUTR + r0 + 128, :],
                        in_=slot[:, :]), waits=[(dve, l2o)])

            for oc in range(nln):
                par = oc % 2
                slot = lnslots[par]
                so = par * 8

                @block.gpsimd
                def _(g, oc=oc, slot=slot, rs_done=rs_done):
                    w = [(ccx, rs_done)]
                    if oc >= 2:
                        w.append((ln2st, ln2st.count))
                    dmag(g, lambda oc=oc: g.dma_start(
                        out=slot[:, :],
                        in_=ored_d[beta * OUTR + oc * 128:
                                   beta * OUTR + (oc + 1) * 128, :]),
                        waits=w)
                lnl = dmag.mark()

                @block.vector
                def _(v, slot=slot, so=so, lnl=lnl):
                    Yl = slot[:, :]
                    dve(v, lambda: v.tensor_scalar(
                        out=hT_f32[:, 0:1024], in0=Yl, scalar1=1.0,
                        scalar2=None, op0=ALU.mult, op1=ALU.add,
                        accum_out=stat_f[:, so + 0:so + 1]),
                        waits=[lnl])
                    dve(v, lambda: v.scalar_tensor_tensor(
                        out=hT_f32[:, 0:1024], in0=Yl, scalar=1.0, in1=Yl,
                        op0=ALU.mult, op1=ALU.mult,
                        accum_out=stat_f[:, so + 1:so + 2]))
                    dve(v, lambda: v.tensor_scalar(
                        out=stat_f[:, so + 2:so + 3],
                        in0=stat_f[:, so + 0:so + 1],
                        scalar1=1.0 / D, scalar2=None, op0=ALU.mult))
                    dve(v, lambda: v.tensor_mul(
                        stat_f[:, so + 3:so + 4], stat_f[:, so + 2:so + 3],
                        stat_f[:, so + 2:so + 3]))
                    dve(v, lambda: v.scalar_tensor_tensor(
                        out=stat_f[:, so + 4:so + 5],
                        in0=stat_f[:, so + 1:so + 2], scalar=1.0 / D,
                        in1=stat_f[:, so + 3:so + 4],
                        op0=ALU.mult, op1=ALU.subtract))
                    dve(v, lambda: v.tensor_scalar(
                        out=stat_f[:, so + 4:so + 5],
                        in0=stat_f[:, so + 4:so + 5],
                        scalar1=1e-5, scalar2=None, op0=ALU.add))
                l2stats[oc] = dve.mark()
                if oc >= 1:
                    ln2_scale(oc - 1)
            ln2_scale(nln - 1)

            @block.scalar
            def _(s):
                for oc in range(nln):
                    so = (oc % 2) * 8
                    act(s, lambda so=so, oc=oc: s.activation(
                        out=stat_f[:, so + 5:so + 6],
                        in_=stat_f[:, so + 4:so + 5], func=AF.Sqrt),
                        waits=[(dve, l2stats[oc])], noser=True)
            assert act.count == l2_base + nln
            out_st_h[0] = [(ln2st, ln2st.count)]


        out_st_h = [[]]
        pending_ln2 = [None]
        prev_ffn_dma = []
        prev_gate_pe = 0
        prev_x1_dma = []
        at_exp_war = {0: 0, 1: 0, 2: 0, 3: 0}
        at_chain_war = [[]]
        at_trans_m = [0]
        at_evac_prev = [0]
        at_qblk_war = [0]
        prev_ffn_pe = [0]
        prev_gelu_m = [0]
        ffn_acc_last = [0]
        for beta in range(NB):
            xr = beta * T

            qk_evac = dve.mark()
            xfm_war = {0: 0, 1: 0}
            for tb in range(NTB):
                par = tb % 2
                xo = par * DC * TB

                @block.sync
                def _(s, tb=tb, par=par, xo=xo):
                    c0 = (beta * NTB + tb) * DC * TB
                    w = [xconv_m[c0 // (NB * T * DC // 8)]]
                    if xfm_war[par]:
                        w.append((pe, xfm_war[par]))
                    dmas(s, lambda c0=c0: s.dma_start(
                        out=xfm_buf[:, xo:xo + DC * TB],
                        in_=xtsbf_d[:, c0:c0 + DC * TB]), waits=w)
                xfm_ld = dmas.mark()

                @block.tensor
                def _(t, tb=tb, xo=xo):
                    outs = [sc0[:, 0:TB], sc0[:, TB:2 * TB], sc1[:, 0:TB]]
                    w = [xfm_ld,
                         (dve, qk_evac if tb > 0 else ffn_acc_last[0])]
                    if beta == 0 and tb == 0:
                        w.append(setup_dmag)
                    first = True
                    for ci in range(3):
                        for dc in range(DC):
                            pe(t, lambda ci=ci, dc=dc: t.matmul(
                                outs[ci][:, 0:TB],
                                wqkv_sb[:, dc * 384 + ci * 128:
                                        dc * 384 + (ci + 1) * 128],
                                xfm_buf[:, xo + dc * TB:xo + (dc + 1) * TB],
                                start=(dc == 0), stop=(dc == DC - 1)),
                               waits=w if first else ())
                            first = False
                xfm_war[par] = pe.mark()
                qkv_mm = pe.mark()

                @block.vector
                def _(v, tb=tb):
                    dve(v, lambda: v.tensor_copy(
                        out=qT_sb[:, tb * TB:(tb + 1) * TB], in_=sc0[:, 0:TB]),
                        waits=[(pe, qkv_mm)])
                    dve(v, lambda: v.tensor_copy(
                        out=kT_sb[:, tb * TB:(tb + 1) * TB], in_=sc0[:, TB:2 * TB]))
                    wv = []
                    if prev_gate_pe:
                        wv.append((pe, prev_gate_pe))
                    dve(v, lambda: v.tensor_copy(
                        out=vT_sb[:, tb * TB:(tb + 1) * TB], in_=sc1[:, 0:TB]),
                        waits=wv)
                qk_evac = dve.mark()

            if not ph("vtr"):
                continue
            for tg in range(0, TC, 8):
                n = min(8, TC - tg)
                bk = "A" if (tg // 8) % 2 == 0 else "B"
                slot = chAb if bk == "A" else chBb

                @block.tensor
                def _(t, tg=tg, n=n, slot=slot, bk=bk):
                    w = chw(bk) + [(dve, qk_evac)]
                    for i in range(n):
                        pe(t, lambda i=i: t.transpose(
                            slot(i), vT_sb[:, (tg + i) * 128:(tg + i + 1) * 128],
                            ident_b[:, :]), waits=w if i == 0 else ())
                vtp = pe.mark()

                @block.vector
                def _(v, tg=tg, n=n, slot=slot):
                    for i in range(n):
                        tc = tg + i
                        dve(v, lambda i=i, tc=tc: v.tensor_copy(
                            out=vaug_sb[:, tc * 130:tc * 130 + 64],
                            in_=slot(i)[:, 0:64]),
                            waits=[(pe, vtp)] if i == 0 else ())
                        dve(v, lambda i=i, tc=tc: v.tensor_copy(
                            out=vaug_sb[:, tc * 130 + 65:tc * 130 + 129],
                            in_=slot(i)[:, 64:128]))
                chfree[bk] = (dve, dve.mark())
            vtok_ready = dve.mark()
            at_chain_war[0] = chw("A") + chw("B")

            if not ph("attn"):
                continue
            accA = accv[:, 0:512]
            accB = accv[:, 512:1024]
            E = ep_f
            A0 = 16
            Z0 = A0 + 512
            SQ0 = Z0 + 256
            pstage_war = {0: [], 1: []}
            xq_war = {0: [], 1: []}
            xq_lanes = {0: [], 1: []}
            p_dma = dmas.mark()
            p_dma_q = [0, 0, 0, 0]
            epi = None
            NEARLY = 6
            for qb in range(NQB + 1):
                last = qb == NQB
                if not last:
                    q0 = qb * QB

                    @block.vector
                    def _(v, q0=q0, qb=qb):
                        w = []
                        if at_qblk_war[0]:
                            w.append((pe, at_qblk_war[0]))
                        if qb == 0:
                            w += list(prev_x1_dma)
                        for X in range(4):
                            dve(v, lambda X=X: v.tensor_copy(
                                out=qblk_sb[32 * X:32 * (X + 1),
                                            X * QB:(X + 1) * QB],
                                in_=qT_sb[32 * X:32 * (X + 1), q0:q0 + QB]),
                                waits=w if X == 0 else ())
                    qblk_done = dve.mark()

                if epi is not None:
                    @block.sync
                    def _(s, epi=epi):
                        for qt in range(2):
                            w = list(xq_war[qt])
                            if epi["q0"] == 0:
                                if prev_gelu_m[0]:
                                    w.append((act, prev_gelu_m[0]))
                                if prev_ffn_pe[0]:
                                    w.append((pe, prev_ffn_pe[0]))
                            r = xr + epi["q0"] + qt * 128
                            h = dmas(s, lambda r=r, qt=qt: s.dma_start(
                                out=xq_sb[qt][:, :], in_=x_d[r:r + 128, :]),
                                waits=w)
                            xq_lanes[qt] = h

                    @block.tensor
                    def _(t, epi=epi):
                        w = [(dve, epi["evac"])] + list(at_chain_war[0])
                        for s_ in range(8):
                            X, qt = s_ % 4, s_ // 4
                            co = X * 256 + qt * 128
                            cb = s_ * 65 if s_ < 4 else 512 + (s_ - 4) * 65
                            pe(t, lambda co=co, cb=cb: t.transpose(
                                chain[:, cb:cb + 65], fm_ep[0:65, co:co + 128],
                                ident_f[0:65, 0:65]),
                               waits=w if s_ == 0 else ())
                    at_trans_m[0] = pe.mark()

                sc_done = {}
                if not last:
                    act_base = act.count
                    nmid = 5 if epi is not None else 0

                    def _ep(kc):
                        return act_base + kc + 1 + (nmid if kc >= NEARLY
                                                    else 0)
                    exp_pred = [_ep(kc) for kc in range(KCN)]
                    sqrt_pred = (act_base + NEARLY + 5) if epi is not None \
                        else 0

                    @block.tensor
                    def _(t, qb=qb, qblk_done=qblk_done):
                        for kc in range(KCN):
                            par = kc % 2
                            scs = sc0 if par == 0 else sc1
                            po = par * 4 * QB
                            w = []
                            if at_exp_war[par]:
                                w.append((act, at_exp_war[par]))
                            if kc == 0:
                                w.append((dve, qblk_done))
                                if qb == 0:
                                    w.append((dve, vtok_ready))
                                    w.append(prev_ffn_dma)
                            for half in range(2):
                                pe(t, lambda kc=kc, scs=scs, half=half:
                                   t.matmul(
                                    scs[:, half * 512:(half + 1) * 512],
                                    kT_sb[:, kc * 128:(kc + 1) * 128],
                                    qblk_sb[:, half * 512:(half + 1) * 512],
                                    start=True, stop=True),
                                   waits=w if half == 0 else ())
                            at_exp_war[par] = exp_pred[kc]
                            sc_done[kc] = pe.mark()
                            wv = [(act, exp_pred[kc])]
                            if kc == 0:
                                if at_evac_prev[0]:
                                    wv.append((dve, at_evac_prev[0]))
                                if qb == 0:
                                    wv.append((dve, vtok_ready))
                            pe(t, lambda kc=kc, po=po: t.matmul(
                                accA[0:65, :],
                                vaug_sb[:, kc * 130:kc * 130 + 65],
                                probs_buf[:, po:po + 512],
                                start=(kc == 0), stop=(kc == KCN - 1)),
                               waits=wv)
                            pe(t, lambda kc=kc, po=po: t.matmul(
                                accB[0:65, :],
                                vaug_sb[:, kc * 130 + 65:kc * 130 + 130],
                                probs_buf[:, po + 512:po + 1024],
                                start=(kc == 0), stop=(kc == KCN - 1)))
                    pv_all = pe.mark()

                if epi is not None:
                    zm = [0] * 4

                    @block.vector
                    def _(v, epi=epi, zm=zm):
                        w = [(pe, at_trans_m[0])]
                        ch3a = chain[:, 0:260].rearrange(
                            "p (s c) -> p s c", c=65)
                        ch3b = chain[:, 512:772].rearrange(
                            "p (s c) -> p s c", c=65)
                        dve(v, lambda: v.reciprocal(
                            E[:, 0:4].rearrange("p (s c) -> p s c", c=1),
                            ch3a[:, :, 64:65]), waits=w)
                        dve(v, lambda: v.reciprocal(
                            E[:, 4:8].rearrange("p (s c) -> p s c", c=1),
                            ch3b[:, :, 64:65]))
                        eodd = E[:, 0:8].rearrange(
                            "p (s c) -> p s c", c=2)[:, :, 1:2]
                        dve(v, lambda: v.tensor_scalar(
                            out=eodd, in0=eodd, scalar1=neglam[:, :1],
                            scalar2=None, op0=ALU.mult))
                        for qt in range(2):
                            for head in range(2):
                                s1 = qt * 4 + 2 * head
                                s2 = s1 + 1
                                cb1 = s1 * 65 if s1 < 4 else 512 + (s1 - 4) * 65
                                cb2 = s2 * 65 if s2 < 4 else 512 + (s2 - 4) * 65
                                zi = qt * 2 + head
                                z = E[:, Z0 + zi * 64:Z0 + (zi + 1) * 64]
                                dve(v, lambda cb1=cb1, s1=s1, z=z:
                                    v.tensor_scalar(
                                        out=z, in0=chain[:, cb1:cb1 + 64],
                                        scalar1=E[:, s1:s1 + 1], scalar2=None,
                                        op0=ALU.mult))
                                dve(v, lambda cb2=cb2, s2=s2, z=z:
                                    v.scalar_tensor_tensor(
                                        out=z, in0=chain[:, cb2:cb2 + 64],
                                        scalar=E[:, s2:s2 + 1], in1=z,
                                        op0=ALU.mult, op1=ALU.add))
                                zm[zi] = dve.mark()
                    epi["zm"] = zm
                    epi["rsq"] = dve.mark()

                @block.scalar
                def _(s, epi=epi, sc_done=sc_done, last=last):
                    def emit_exp(kc):
                        par = kc % 2
                        scs = sc0 if par == 0 else sc1
                        po = par * 4 * QB
                        act(s, lambda scs=scs, po=po: s.activation(
                            out=probs_buf[:, po:po + 4 * QB],
                            in_=scs[:, 0:4 * QB], func=AF.Exp,
                            bias=biasc[:, 0:1], scale=biasc[:, 2:3]),
                            waits=[(pe, sc_done[kc])], noser=True)

                    if not last:
                        for kc in range(NEARLY):
                            emit_exp(kc)
                    if epi is not None:
                        for zi in range(4):
                            act(s, lambda zi=zi: s.activation(
                                out=E[:, A0 + zi * 64:A0 + (zi + 1) * 64],
                                in_=E[:, Z0 + zi * 64:Z0 + (zi + 1) * 64],
                                func=AF.Square,
                                accum_out=E[:, 8 + zi:9 + zi]),
                                waits=[(dve, epi["zm"][zi])], noser=True)
                        act(s, lambda: s.activation(
                            out=E[:, 12:16], in_=E[:, 8:12], func=AF.Sqrt,
                            bias=biasc[:, 4:5], scale=biasc[:, 5:6]))
                    if not last:
                        for kc in range(NEARLY, KCN):
                            emit_exp(kc)
                if not last:
                    assert act.count == act_base + KCN + nmid
                    if epi is not None:
                        assert act.count >= sqrt_pred
                else:
                    sqrt_pred = act.count

                if qb == 2 and pending_ln2[0] is not None:
                    pending_ln2[0]()
                    pending_ln2[0] = None

                if epi is not None:
                    @block.vector
                    def _(v, sqrt_pred=sqrt_pred):
                        dve(v, lambda: v.reciprocal(E[:, 12:16], E[:, 12:16]),
                            waits=[(act, sqrt_pred)])
                        for qt in range(2):
                            for head in range(2):
                                zi = qt * 2 + head
                                dve(v, lambda zi=zi, qt=qt, head=head:
                                    v.scalar_tensor_tensor(
                                        out=zct_sb[:, qt * 128 + head * 64:
                                                   qt * 128 + (head + 1) * 64],
                                        in0=E[:, Z0 + zi * 64:
                                              Z0 + (zi + 1) * 64],
                                        scalar=E[:, 12 + zi:13 + zi],
                                        in1=scaletile[:, :],
                                        op0=ALU.mult, op1=ALU.mult))
                    zct_done = dve.mark()

                    @block.tensor
                    def _(t, zct_done=zct_done):
                        for qt in range(2):
                            pe(t, lambda qt=qt: t.transpose(
                                chain_bf[:, qt * 128:(qt + 1) * 128],
                                zct_sb[:, qt * 128:(qt + 1) * 128],
                                ident_b[:, :]),
                               waits=[(dve, zct_done)] if qt == 0 else ())
                    zT_mm = pe.mark()

                    @block.vector
                    def _(v, zT_mm=zT_mm):
                        for qt in range(2):
                            dve(v, lambda qt=qt: v.tensor_copy(
                                out=zcatT_sb[:, qt * 128:(qt + 1) * 128],
                                in_=chain_bf[:, qt * 128:(qt + 1) * 128]),
                                waits=[(pe, zT_mm)] if qt == 0 else ())
                    zcat_done = dve.mark()

                    pstage_evac = 0
                    for qt in range(2):
                        @block.tensor
                        def _(t, qt=qt, zcat_done=zcat_done,
                              pstage_evac=pstage_evac, epi=epi):
                            w = []
                            if qt == 0:
                                w.append((dve, zcat_done))
                                w.append((dve, epi["rsq"]))
                            else:
                                w.append((dve, pstage_evac))
                            pe(t, lambda qt=qt: t.matmul(
                                chA[:, 0:512],
                                zcatT_sb[:, qt * 128:(qt + 1) * 128],
                                wproj_sb[:, 0:512], start=True, stop=True),
                               waits=w)
                            pe(t, lambda qt=qt: t.matmul(
                                chB[:, 0:512],
                                zcatT_sb[:, qt * 128:(qt + 1) * 128],
                                wproj_sb[:, 512:1024], start=True, stop=True))
                        pmm = pe.mark()

                        @block.vector
                        def _(v, qt=qt, pmm=pmm):
                            w = ([(pe, pmm)] + list(pstage_war[qt])
                                 + list(xq_lanes[qt]))
                            dve(v, lambda qt=qt: v.scalar_tensor_tensor(
                                out=pstage2[qt][:, 0:512],
                                in0=xq_sb[qt][:, 0:512],
                                scalar=1.0 / cfg.cores,
                                in1=chA[:, 0:512],
                                op0=ALU.mult, op1=ALU.add), waits=w)
                            dve(v, lambda qt=qt: v.scalar_tensor_tensor(
                                out=pstage2[qt][:, 512:1024],
                                in0=xq_sb[qt][:, 512:1024],
                                scalar=1.0 / cfg.cores,
                                in1=chB[:, 0:512],
                                op0=ALU.mult, op1=ALU.add))
                        pstage_evac = dve.mark()
                        xq_war[qt] = [(dve, pstage_evac)]

                        @block.sync
                        def _(s, qt=qt, epi=epi, pstage_evac=pstage_evac):
                            r = xr + epi["q0"] + qt * 128
                            h = dmas(s, lambda r=r, qt=qt: s.dma_start(
                                out=ppart_d[r:r + 128, :],
                                in_=pstage2[qt][:, :]),
                                waits=[(dve, pstage_evac)])
                            pstage_war[qt] = h
                        p_dma = dmas.mark()
                    pq = epi["q0"] // QB
                    if pq % 4 == 3:
                        p_dma_q[pq // 4] = p_dma
                    at_chain_war[0] = [(dve, pstage_evac)]
                    chfree["A"] = (dve, pstage_evac)
                    chfree["B"] = (dve, pstage_evac)

                if not last:
                    @block.vector
                    def _(v, pv_all=pv_all):
                        w = [(pe, pv_all)]
                        if at_trans_m[0]:
                            w.append((pe, at_trans_m[0]))
                        dve(v, lambda: v.tensor_copy(
                            out=fm_ep[0:65, 0:512], in_=accA[0:65, :]),
                            waits=w)
                        dve(v, lambda: v.tensor_copy(
                            out=fm_ep[0:65, 512:1024], in_=accB[0:65, :]))
                    at_evac_prev[0] = dve.mark()
                    epi = {"evac": at_evac_prev[0], "q0": q0}
                    at_qblk_war[0] = pe.mark()
                else:
                    epi = None

            if not ph("ar"):
                continue
            arq = []

            @block.gpsimd
            def _(g):
                QR = T // 4
                for q in range(4):
                    dmas.wait(g, p_dma_q[q])
                    ccx(g, lambda q=q, QR=QR: g.collective_compute(
                        "AllReduce", ALU.add, replica_groups=groups,
                        ins=[ppart_d[xr + q * QR:xr + (q + 1) * QR, :]],
                        outs=[psum_d[xr + q * QR:xr + (q + 1) * QR, :]]))
                    arq.append(ccx.mark())

            if not ph("ln1"):
                continue
            scale_war = {0: 0, 1: 0}
            x1war = {0: [], 1: []}
            stats_m = [0] * TC
            ln_base = act.count
            x18slot = [ep_f[:, 0:D], fm_ep[:, 0:D]]

            def ln1_scale(tc):
                par = tc % 2
                so = par * 8
                Y = ybuf[:, par * D:(par + 1) * D]

                @block.vector
                def _(v, tc=tc, par=par, so=so, Y=Y):
                    dve(v, lambda: v.reciprocal(
                        stat_f[:, so + 5:so + 6], stat_f[:, so + 5:so + 6]),
                        waits=[(act, ln_base + tc + 1),
                               (pe, at_trans_m[0])])
                    dve(v, lambda: v.tensor_scalar(
                        out=x1buf[:, par * D:(par + 1) * D], in0=Y,
                        scalar1=stat_f[:, so + 2:so + 3],
                        scalar2=stat_f[:, so + 5:so + 6],
                        op0=ALU.subtract, op1=ALU.mult))
                    dve(v, lambda: v.tensor_scalar(
                        out=x18slot[par], in0=x1buf[:, par * D:(par + 1) * D],
                        scalar1=1.0 / cfg.cores, scalar2=None, op0=ALU.mult))
                scale_war[par] = dve.mark()

                @block.sync
                def _(s, tc=tc, par=par):
                    r = tc * 128
                    h1 = dmas(s, lambda r=r: s.dma_start(
                        out=x1_ds[beta][r:r + 128, :],
                        in_=x1buf[:, par * D:(par + 1) * D]),
                        waits=[(dve, scale_war[par])])
                    h2 = dmas(s, lambda r=r, par=par: s.dma_start(
                        out=opart_ds[beta][r:r + 128, :], in_=x18slot[par]))
                    x1war[par] = h1 + h2

            for tc in range(TC):
                par = tc % 2
                yo = par * D
                so = par * 8

                @block.sync
                def _(s, tc=tc, yo=yo, par=par):
                    w = [(ccx, arq[tc // 8])]
                    if scale_war[par]:
                        w.append((dve, scale_war[par]))
                    if tc < 2:
                        w += list(out_st_h[0])
                    dmas(s, lambda: s.dma_start(
                        out=ybuf[:, yo:yo + D],
                        in_=psum_d[xr + tc * 128:xr + (tc + 1) * 128, :]),
                        waits=w)
                xl = dmas.mark()

                @block.vector
                def _(v, yo=yo, so=so, par=par, xl=xl):
                    Y = ybuf[:, yo:yo + D]
                    xb = x1buf[:, par * D:(par + 1) * D]
                    dve(v, lambda: v.tensor_scalar(
                        out=xb, in0=Y, scalar1=1.0, scalar2=None,
                        op0=ALU.mult, op1=ALU.add,
                        accum_out=stat_f[:, so + 0:so + 1]),
                        waits=[xl] + list(x1war[par]))
                    dve(v, lambda: v.scalar_tensor_tensor(
                        out=xb, in0=Y, scalar=1.0, in1=Y,
                        op0=ALU.mult, op1=ALU.mult,
                        accum_out=stat_f[:, so + 1:so + 2]))
                    dve(v, lambda: v.tensor_scalar(
                        out=stat_f[:, so + 2:so + 3],
                        in0=stat_f[:, so + 0:so + 1],
                        scalar1=1.0 / D, scalar2=None, op0=ALU.mult))
                    dve(v, lambda: v.tensor_mul(
                        stat_f[:, so + 3:so + 4], stat_f[:, so + 2:so + 3],
                        stat_f[:, so + 2:so + 3]))
                    dve(v, lambda: v.scalar_tensor_tensor(
                        out=stat_f[:, so + 4:so + 5],
                        in0=stat_f[:, so + 1:so + 2], scalar=1.0 / D,
                        in1=stat_f[:, so + 3:so + 4],
                        op0=ALU.mult, op1=ALU.subtract))
                    dve(v, lambda: v.tensor_scalar(
                        out=stat_f[:, so + 4:so + 5],
                        in0=stat_f[:, so + 4:so + 5],
                        scalar1=1e-5, scalar2=None, op0=ALU.add))
                stats_m[tc] = dve.mark()
                if tc >= 1:
                    ln1_scale(tc - 1)
            ln1_scale(TC - 1)

            @block.scalar
            def _(s):
                for tc in range(TC):
                    so = (tc % 2) * 8
                    act(s, lambda so=so, tc=tc: s.activation(
                        out=stat_f[:, so + 5:so + 6],
                        in_=stat_f[:, so + 4:so + 5], func=AF.Sqrt),
                        waits=[(dve, stats_m[tc])], noser=True)
            assert act.count == ln_base + TC
            x1_dma = dmas.mark()
            prev_x1_dma = x1_dma

            if not ph("x1bf"):
                continue

            @block.gpsimd
            def _(g):
                dmag(g, lambda: g.dma_start(
                    out=x1bf_d[xr:xr + T, :], in_=x1_ds[beta][:, :]),
                    waits=[x1_dma])
            x1bf_done = dmag.mark()

            if not ph("gate"):
                continue
            x1fm_war = {0: 0}
            for dc in range(DC):
                par = 0
                fo = 0

                @block.sync
                def _(s, dc=dc, fo=fo, par=par):
                    w = [x1bf_done]
                    if x1fm_war[par]:
                        w.append((pe, x1fm_war[par]))
                    dmas(s, lambda: s.dma_start(
                        out=x1fm_buf[:, fo:fo + T],
                        in_=x1bf_d[xr:xr + T, dc * 128:(dc + 1) * 128],
                        transpose=True), waits=w)
                x1fm_ld = dmas.mark()

                for ft in range(T // 512):
                    bk = "A" if ft % 2 == 0 else "B"
                    dst = chA if ft % 2 == 0 else chB

                    @block.tensor
                    def _(t, dc=dc, ft=ft, fo=fo, dst=dst, bk=bk,
                          x1fm_ld=x1fm_ld):
                        w = [x1fm_ld] + chw(bk)
                        pe(t, lambda: t.matmul(
                            dst[0:NE, 0:512],
                            gatew_sb[:, dc * NE:(dc + 1) * NE],
                            x1fm_buf[:, fo + ft * 512:fo + (ft + 1) * 512],
                            start=True, stop=True), waits=w)
                    gmm = pe.mark()

                    @block.vector
                    def _(v, dc=dc, ft=ft, dst=dst, gmm=gmm):
                        if dc == 0:
                            dve(v, lambda: v.tensor_copy(
                                out=gacc[:, ft * 512:(ft + 1) * 512],
                                in_=dst[0:NE, 0:512]), waits=[(pe, gmm)])
                        else:
                            dve(v, lambda: v.tensor_add(
                                gacc[:, ft * 512:(ft + 1) * 512],
                                gacc[:, ft * 512:(ft + 1) * 512],
                                dst[0:NE, 0:512]), waits=[(pe, gmm)])
                    chfree[bk] = (dve, dve.mark())
                x1fm_war[par] = pe.mark()
            prev_gate_pe = pe.mark()
            gate_done = dve.mark()

            for tg in range(0, TC, 8):
                n = min(8, TC - tg)
                bk = "A" if (tg // 8) % 2 == 0 else "B"
                base = 0 if bk == "A" else 512

                @block.tensor
                def _(t, tg=tg, n=n, base=base, bk=bk):
                    w = [(dve, gate_done)] + chw(bk)
                    for i in range(n):
                        pe(t, lambda i=i: t.transpose(
                            chain[:, base + i * 8:base + i * 8 + NE],
                            gacc[:, (tg + i) * 128:(tg + i + 1) * 128],
                            ident_f[0:NE, 0:NE]), waits=w if i == 0 else ())
                gt = pe.mark()

                @block.vector
                def _(v, tg=tg, n=n, base=base, gt=gt):
                    for i in range(n):
                        dve(v, lambda i=i: v.tensor_copy(
                            out=gtok[:, (tg + i) * NE:(tg + i + 1) * NE],
                            in_=chain[:, base + i * 8:base + i * 8 + NE]),
                            waits=[(pe, gt)] if i == 0 else ())
                chfree[bk] = (dve, dve.mark())
            gtok_done = dve.mark()

            if not ph("sm"):
                continue
            @block.scalar
            def _(s):
                act(s, lambda: s.activation(out=probs_tok[:, :], in_=gtok[:, :],
                                            func=AF.Exp),
                    waits=[(dve, gtok_done)], noser=True)
            sm_exp = act.mark()

            @block.vector
            def _(v):
                pt3 = probs_tok[:, :].rearrange("p (tc e) -> p tc e", e=NE)
                dve(v, lambda: v.reduce_sum(ssum[:, :], pt3, axis=AX.X),
                    waits=[(act, sm_exp)])
                dve(v, lambda: v.reciprocal(ssum[:, :], ssum[:, :]))
                for e in range(NE):
                    dve(v, lambda e=e: v.tensor_mul(
                        pt3[:, :, e], pt3[:, :, e], ssum[:, :]))
                dve(v, lambda: v.memset(tau_lo[:, :], 0.0))
                dve(v, lambda: v.memset(tau_hi[:, :], 1.0))
            probs_done = dve.mark()

            def count_pass(tau_src, out_mask):
                @block.vector
                def _(v):
                    dve(v, lambda: v.tensor_scalar(
                        out=tau_m8[:, :], in0=ind_em[:, :],
                        scalar1=tau_src[0:NE, 0:1], scalar2=None, op0=ALU.mult))
                tm = dve.mark()

                @block.tensor
                def _(t, tm=tm):
                    w = [(dve, tm)] + chw("A")
                    pe(t, lambda: t.matmul(chA[:, 0:TC * NE], ones8_f[:, :],
                                           tau_m8[:, :], start=True, stop=True),
                       waits=w)
                taumm = pe.mark()

                @block.vector
                def _(v, taumm=taumm):
                    dve(v, lambda: v.tensor_tensor(
                        out=out_mask[:, :], in0=probs_tok[:, :],
                        in1=chA[:, 0:TC * NE], op=ALU.is_gt),
                        waits=[(pe, taumm)])
                    om3 = out_mask[:, :].rearrange("p (tc e) -> p e tc", e=NE)
                    dve(v, lambda: v.reduce_sum(part8[:, :], om3, axis=AX.X))
                chfree["A"] = (dve, dve.mark())
                pm = dve.mark()

                @block.tensor
                def _(t, pm=pm):
                    w = [(dve, pm)] + chw("B")
                    pe(t, lambda: t.matmul(chB[0:NE, 0:1], part8[:, :],
                                           onescol_f[:, 0:1],
                                           start=True, stop=True), waits=w)
                cmm = pe.mark()

                @block.vector
                def _(v, cmm=cmm):
                    dve(v, lambda: v.tensor_copy(out=cnt8[:, :],
                                                 in_=chB[0:NE, 0:1]),
                        waits=[(pe, cmm)])
                chfree["B"] = (dve, dve.mark())

            if not ph("bis"):
                continue
            TN = TC * NE
            cmp_x = x1buf[:, 0:3 * TN]
            tau_m8x = x1buf[0:NE, 4 * TN:7 * TN]
            part8x = x1buf[:, 7 * TN:7 * TN + 3 * NE]
            delta = tau_hi
            for it in range(18):
                @block.vector
                def _(v):
                    dve(v, lambda: v.tensor_scalar(
                        out=delta[:, :], in0=delta[:, :], scalar1=0.25,
                        scalar2=None, op0=ALU.mult))
                    dve(v, lambda: v.tensor_add(bs_sc[:, 0:1], tau_lo[:, :],
                                                delta[:, :]))
                    dve(v, lambda: v.tensor_add(bs_sc[:, 1:2], bs_sc[:, 0:1],
                                                delta[:, :]))
                    dve(v, lambda: v.tensor_add(bs_sc[:, 2:3], bs_sc[:, 1:2],
                                                delta[:, :]))
                    for j in range(3):
                        dve(v, lambda j=j: v.tensor_scalar(
                            out=tau_m8x[:, j * TN:(j + 1) * TN],
                            in0=ind_em[:, :], scalar1=bs_sc[:, j:j + 1],
                            scalar2=None, op0=ALU.mult))
                tm = dve.mark()

                @block.tensor
                def _(t, tm=tm):
                    w = [(dve, tm)] + chw("A") + chw("B")
                    pe(t, lambda: t.matmul(chA[:, 0:512], ones8_f[:, :],
                                           tau_m8x[:, 0:512],
                                           start=True, stop=True), waits=w)
                    pe(t, lambda: t.matmul(chB[:, 0:256], ones8_f[:, :],
                                           tau_m8x[:, 512:768],
                                           start=True, stop=True))
                taumm = pe.mark()

                @block.vector
                def _(v, taumm=taumm):
                    for j in range(3):
                        src = (chA[:, j * TN:(j + 1) * TN] if j < 2
                               else chB[:, 0:TN])
                        dve(v, lambda j=j, src=src: v.tensor_tensor(
                            out=cmp_x[:, j * TN:(j + 1) * TN],
                            in0=probs_tok[:, :], in1=src, op=ALU.is_gt),
                            waits=[(pe, taumm)] if j == 0 else ())
                    cm4 = cmp_x.rearrange("p (j tc e) -> p j e tc", e=NE,
                                          tc=TC)
                    dve(v, lambda: v.reduce_sum(part8x[:, :], cm4, axis=AX.X))
                pm = dve.mark()

                @block.tensor
                def _(t, pm=pm):
                    for j in range(3):
                        pe(t, lambda j=j: t.matmul(
                            chB[0:NE, 300 + j:301 + j],
                            part8x[:, j * NE:(j + 1) * NE],
                            onescol_f[:, 0:1], start=True, stop=True),
                           waits=[(dve, pm)] if j == 0 else ())
                cmm = pe.mark()

                @block.vector
                def _(v, cmm=cmm):
                    dve(v, lambda: v.tensor_scalar(
                        out=bs_sc[:, 4:7], in0=chB[0:NE, 300:303],
                        scalar1=float(K), scalar2=None, op0=ALU.is_ge),
                        waits=[(pe, cmm)])
                    dve(v, lambda: v.reduce_sum(
                        bs_sc[:, 7:8], bs_sc[:, 4:7], axis=AX.X))
                    dve(v, lambda: v.scalar_tensor_tensor(
                        out=tau_lo[:, :], in0=bs_sc[:, 7:8],
                        scalar=delta[:, 0:1], in1=tau_lo[:, :],
                        op0=ALU.mult, op1=ALU.add))
            @block.vector
            def _(v):
                dve(v, lambda: v.tensor_add(tau_hi[:, :], tau_lo[:, :],
                                            delta[:, :]))

            if not ph("mask"):
                continue
            count_pass(tau_hi, hi_tok)

            @block.vector
            def _(v):
                dve(v, lambda: v.tensor_scalar(
                    out=fill8[:, :], in0=cnt8[:, :], scalar1=-1.0,
                    scalar2=float(K), op0=ALU.mult, op1=ALU.add))
            count_pass(tau_lo, tie_tok)

            @block.vector
            def _(v):
                dve(v, lambda: v.tensor_sub(tie_tok[:, :], tie_tok[:, :],
                                            hi_tok[:, :]))
            masks_done = dve.mark()

            for tg in range(0, TC, 2):
                n = min(2, TC - tg)
                bk = "A" if (tg // 2) % 2 == 0 else "B"
                base = 0 if bk == "A" else 512

                @block.tensor
                def _(t, tg=tg, n=n, base=base, bk=bk):
                    w = [(dve, masks_done)] + chw(bk)
                    for i in range(n):
                        pe(t, lambda i=i: t.transpose(
                            chain[0:NE, base + (2 * i) * 128:
                                  base + (2 * i) * 128 + 128],
                            hi_tok[:, (tg + i) * NE:(tg + i + 1) * NE],
                            ident_f[:, :]), waits=w if i == 0 else ())
                        pe(t, lambda i=i: t.transpose(
                            chain[0:NE, base + (2 * i + 1) * 128:
                                  base + (2 * i + 1) * 128 + 128],
                            tie_tok[:, (tg + i) * NE:(tg + i + 1) * NE],
                            ident_f[:, :]))
                ht = pe.mark()

                @block.vector
                def _(v, tg=tg, n=n, base=base, ht=ht):
                    for i in range(n):
                        w = [(pe, ht)] + list(out_st_h[0])
                        dve(v, lambda i=i: v.tensor_copy(
                            out=hi_b[0:NE, (tg + i) * 128:(tg + i + 1) * 128],
                            in_=chain[0:NE, base + (2 * i) * 128:
                                      base + (2 * i) * 128 + 128]),
                            waits=w if i == 0 else ())
                        dve(v, lambda i=i: v.tensor_copy(
                            out=tie_b[0:NE, (tg + i) * 128:(tg + i + 1) * 128],
                            in_=chain[0:NE, base + (2 * i + 1) * 128:
                                      base + (2 * i + 1) * 128 + 128]))
                chfree[bk] = (dve, dve.mark())
            em_done = dve.mark()

            if not ph("sel"):
                continue
            @block.vector
            def _(v):
                HIe = hi_b[0:NE, :]
                TIe = tie_b[0:NE, :]
                RNK = rank_f[0:NE, :]
                dve(v, lambda: v.tensor_tensor_scan(
                    out=RNK, data0=TIe, data1=TIe, initial=0.0,
                    op0=ALU.add, op1=ALU.bypass), waits=[(dve, em_done)])
                dve(v, lambda: v.tensor_scalar(
                    out=RNK, in0=RNK,
                    scalar1=fill8[0:NE, 0:1], scalar2=None, op0=ALU.is_le))
                dve(v, lambda: v.tensor_mul(TIe, TIe, RNK))
                dve(v, lambda: v.tensor_add(TIe, TIe, HIe))
                dve(v, lambda: v.tensor_tensor_scan(
                    out=RNK, data0=TIe, data1=TIe, initial=0.0,
                    op0=ALU.add, op1=ALU.bypass))
                dve(v, lambda: v.tensor_scalar(
                    out=RNK, in0=RNK, scalar1=-1.0,
                    scalar2=None, op0=ALU.add))
            gs_done = dve.mark()

            for tg in range(0, TC, 8):
                n = min(8, TC - tg)
                bk = "A" if (tg // 8) % 2 == 0 else "B"
                base = 0 if bk == "A" else 512

                @block.tensor
                def _(t, tg=tg, n=n, base=base, bk=bk):
                    w = [(dve, gs_done)] + chw(bk)
                    for i in range(n):
                        pe(t, lambda i=i: t.transpose(
                            chain[:, base + i * 2:base + i * 2 + 1],
                            rank_f[0:1, (tg + i) * 128:(tg + i + 1) * 128],
                            ident_f[0:1, 0:1]), waits=w if i == 0 else ())
                        pe(t, lambda i=i: t.transpose(
                            chain_bf[:, 2 * (base + i * 2 + 1):
                                     2 * (base + i * 2 + 1) + 1],
                            tie_b[0:1, (tg + i) * 128:(tg + i + 1) * 128],
                            ident_b[0:1, 0:1]))
                gst = pe.mark()

                @block.vector
                def _(v, tg=tg, n=n, base=base, gst=gst):
                    for i in range(n):
                        dve(v, lambda i=i: v.tensor_copy(
                            out=gs_tok[:, (tg + i) * 2:(tg + i) * 2 + 1],
                            in_=chain[:, base + i * 2:base + i * 2 + 1]),
                            waits=[(pe, gst)] if i == 0 else ())
                        dve(v, lambda i=i: v.tensor_copy(
                            out=gs_tok[:, (tg + i) * 2 + 1:(tg + i) * 2 + 2],
                            in_=chain_bf[:, 2 * (base + i * 2 + 1):
                                         2 * (base + i * 2 + 1) + 1]))
                chfree[bk] = (dve, dve.mark())
            gstok_done = dve.mark()

            if not ph("ex"):
                continue
            if beta > 0:
                @block.gpsimd
                def _(g):
                    pool(g, lambda: g.iota(
                        iota_j[:, :], pattern=[[1, K]], base=0,
                        channel_multiplier=0,
                        allow_small_or_imprecise_dtypes=True),
                        waits=[(pe, prev_ffn_pe[0])])
            iota_m = pool.mark()
            ex_prev = pe.mark()
            for tc in range(TC):
                @block.vector
                def _(v, tc=tc, ex_prev=ex_prev):
                    w = [(pe, ex_prev)]
                    if tc == 0:
                        w.append((dve, gstok_done))
                        w.append((pool, iota_m))
                    dve(v, lambda: v.tensor_copy(out=payload[:, 0:1],
                                                 in_=iota_colb[:, :]), waits=w)
                    dve(v, lambda: v.tensor_copy(out=payload[:, 1:2],
                                                 in_=chunkv_b[:, tc:tc + 1]))
                    dve(v, lambda: v.tensor_copy(
                        out=payload[:, 2:3],
                        in_=probs_tok[:, tc * NE:tc * NE + 1]))
                    dve(v, lambda: v.tensor_scalar(
                        out=pc_b[:, :], in0=iota_j[:, :],
                        scalar1=gs_tok[:, 2 * tc:2 * tc + 1],
                        scalar2=gs_tok[:, 2 * tc + 1:2 * tc + 2],
                        op0=ALU.is_equal, op1=ALU.mult))
                pcm = dve.mark()

                @block.tensor
                def _(t, tc=tc, pcm=pcm):
                    w = [(dve, pcm)]
                    if tc == 0:
                        w += chw("A") + chw("B")
                    for kf in range((K + 511) // 512):
                        f0, f1 = kf * 512, min(K, kf * 512 + 512)
                        pe(t, lambda f0=f0, f1=f1: t.matmul(
                            chain[0:3, f0:f1], payload[:, :], pc_b[:, f0:f1],
                            start=(tc == 0), stop=(tc == TC - 1)),
                           waits=w if kf == 0 else ())
                ex_prev = pe.mark()
            ex_mm = pe.mark()

            @block.vector
            def _(v):
                dve(v, lambda: v.tensor_copy(out=mg_f[:, :],
                                             in_=chain[0:3, 0:K]),
                    waits=[(pe, ex_mm)])
            chfree["A"] = (dve, dve.mark())
            chfree["B"] = (dve, dve.mark())
            mg_done = dve.mark()

            if not ph("mg"):
                continue
            prev_ffn = prev_ffn_dma if beta > 0 else []
            for ks in range(KC):
                bk = "A" if ks % 2 == 0 else "B"
                base = 0 if bk == "A" else 512

                @block.tensor
                def _(t, ks=ks, base=base, bk=bk):
                    w = [(dve, mg_done)] + chw(bk)
                    pe(t, lambda: t.transpose(
                        chain_bf[:, 2 * base:2 * base + 3],
                        mg_f[0:3, ks * 128:(ks + 1) * 128],
                        ident_b[0:3, 0:3]), waits=w)
                mt = pe.mark()

                @block.vector
                def _(v, ks=ks, base=base, mt=mt):
                    dve(v, lambda: v.tensor_copy(
                        out=mg3[:, 0:3], in_=chain_bf[:, 2 * base:2 * base + 3]),
                        waits=[(pe, mt)] + list(prev_ffn))
                    dve(v, lambda: v.scalar_tensor_tensor(
                        out=mcols[:, ks:ks + 1], in0=mg3[:, 1:2],
                        scalar=128.0, in1=mg3[:, 0:1],
                        op0=ALU.mult, op1=ALU.add))
                    dve(v, lambda: v.tensor_copy(out=gcol[:, ks:ks + 1],
                                                 in_=mg3[:, 2:3]))
                chfree[bk] = (dve, dve.mark())

            @block.vector
            def _(v):
                dve(v, lambda: v.tensor_copy(out=midx[:, :], in_=mcols[:, :]))
            midx_done = dve.mark()

            if not ph("ffn"):
                continue
            NG = K // 128
            NGH = NG // 2
            accpair = [sc1, accv]
            acc_war = {0: 0, 1: 0}
            scat_h = []
            scat_mark = dmag.mark()
            xin_war = [0]
            mm2_last = 0
            for hf in range(2):
                xinT_done = 0
                for g in range(NGH):
                    par = g % 2
                    col = hf * NGH + g

                    @block.gpsimd
                    def _(gg, g=g, par=par, col=col):
                        w = [(dve, midx_done), x1_dma]
                        if xin_war[0]:
                            w.append((pe, xin_war[0]))
                        dmag(gg, lambda: gg.indirect_dma_start(
                            out=xin_sb[:, par * D:(par + 1) * D],
                            out_offset=None, in_=x1_ds[beta],
                            in_offset=IndirectOffsetOnAxis(
                                ap=midx[:, col:col + 1], axis=0)), waits=w)
                    gth = dmag.mark()

                    bk = "A" if par == 0 else "B"
                    slot = chAb if par == 0 else chBb

                    @block.tensor
                    def _(t, par=par, slot=slot, bk=bk, gth=gth):
                        w = [gth] + chw(bk)
                        if mm2_last:
                            w.append((pe, mm2_last))
                        for dc in range(DC):
                            pe(t, lambda dc=dc, par=par: t.transpose(
                                slot(dc),
                                xin_sb[:, par * D + dc * 128:
                                       par * D + (dc + 1) * 128],
                                ident_b[:, :]), waits=w if dc == 0 else ())
                    xt = pe.mark()
                    if g >= 1:
                        xin_war[0] = xt

                    @block.vector
                    def _(v, g=g, slot=slot, xt=xt):
                        for dc in range(DC):
                            dve(v, lambda dc=dc: v.tensor_copy(
                                out=xinT_sb[:, dc * 512 + g * 128:
                                            dc * 512 + (g + 1) * 128],
                                in_=slot(dc)),
                                waits=[(pe, xt)] if dc == 0 else ())
                    chfree[bk] = (dve, dve.mark())
                xinT_done = dve.mark()
                xin_war[0] = pe.mark()

                pe_base = pe.count
                w1_lds = []
                for hc in range(HC):
                    @block.gpsimd
                    def _(gg, hc=hc):
                        w = [setup_dmag] if (beta == 0 and hf == 0) else []
                        if hc >= 2:
                            w.append((pe, pe_base + (hc - 1) * DC))
                        dmag(gg, lambda hc=hc: gg.dma_start(
                            out=w1_buf[:, (hc % 2) * D:(hc % 2) * D + D],
                            in_=w1bf_d[:, hc * D:(hc + 1) * D]), waits=w)
                    w1_lds.append(dmag.mark())

                gelu_base = act.count
                mm1_m = []
                @block.tensor
                def _(t, xinT_done=xinT_done, gelu_base=gelu_base,
                      w1_lds=w1_lds):
                    for hc in range(HC):
                        par = hc % 2
                        w = [w1_lds[hc]]
                        if hc == 0:
                            w.append((dve, xinT_done))
                        if hc >= 2:
                            w.append((act, gelu_base + hc - 1))
                        for dc in range(DC):
                            pe(t, lambda hc=hc, par=par, dc=dc: t.matmul(
                                sc0[:, par * 512:(par + 1) * 512],
                                w1_buf[:, (hc % 2) * D + dc * 128:
                                       (hc % 2) * D + (dc + 1) * 128],
                                xinT_sb[:, dc * 512:(dc + 1) * 512],
                                start=(dc == 0), stop=(dc == DC - 1)),
                               waits=w if dc == 0 else ())
                        mm1_m.append(pe.mark())
                assert pe.count == pe_base + HC * DC

                @block.scalar
                def _(s, mm1_m=mm1_m):
                    for hc in range(HC):
                        par = hc % 2
                        act(s, lambda hc=hc, par=par: s.activation(
                            out=hT_sb[:, hc * 512:(hc + 1) * 512],
                            in_=sc0[:, par * 512:(par + 1) * 512],
                            func=AF.Gelu_apprx_tanh),
                            waits=[(pe, mm1_m[hc])], noser=True)
                assert act.count == gelu_base + HC
                gelu_mk = [gelu_base + hc + 1 for hc in range(HC)]

                for g in range(NGH):
                    col = hf * NGH + g
                    par = g % 2
                    acc = accpair[par]

                    @block.tensor
                    def _(t, g=g, par=par, acc=acc, gelu_mk=gelu_mk):
                        for hc in range(HC):
                            w = [(act, gelu_mk[HC - 1])]
                            if hc == 0 and acc_war[par]:
                                w.append((dve, acc_war[par]))
                            if hc == 0 and hf == 0 and g < 2:
                                w.append((dve, at_evac_prev[0]))
                            for dh in range(2):
                                pe(t, lambda hc=hc, g=g, dh=dh, acc=acc:
                                   t.matmul(
                                    acc[:, dh * 512:(dh + 1) * 512],
                                    hT_sb[:, hc * 512 + g * 128:
                                          hc * 512 + (g + 1) * 128],
                                    w2_sb[:, hc * D + dh * 512:
                                          hc * D + dh * 512 + 512],
                                    start=(hc == 0), stop=(hc == HC - 1)),
                                   waits=w if (hc == 0 and dh == 0) else ())
                    fmm = pe.mark()
                    mm2_last = fmm

                    @block.vector
                    def _(v, col=col, acc=acc, fmm=fmm):
                        dve(v, lambda: v.tensor_scalar(
                            out=fw_sb[:, :], in0=acc[:, :],
                            scalar1=gcol[:, col:col + 1],
                            scalar2=None, op0=ALU.mult),
                            waits=[(pe, fmm)] + list(scat_h))
                    acc_war[par] = dve.mark()
                    fw_done = dve.mark()

                    @block.gpsimd
                    def _(gg, col=col, fw_done=fw_done, scat_mark=scat_mark):
                        h = dmag(gg, lambda: gg.indirect_dma_start(
                            out=opart_ds[beta],
                            out_offset=IndirectOffsetOnAxis(
                                ap=midx[:, col:col + 1], axis=0),
                            in_=fw_sb[:, :], in_offset=None,
                            compute_op=ALU.add),
                            waits=[(dve, fw_done), x1_dma, scat_mark])
                        scat_h.clear()
                        scat_h.extend(h)
                    scat_mark = dmag.mark()

            at_evac_prev[0] = acc_war[1]
            ffn_acc_last[0] = max(acc_war[0], acc_war[1])
            prev_gelu_m[0] = act.count
            prev_ffn_pe[0] = pe.mark()
            prev_ffn_dma = dmag.mark()
            if not ph("rs"):
                continue

            @block.gpsimd
            def _(g, scat_mark=scat_mark):
                dmag.wait(g, scat_mark)
                ccx(g, lambda: g.collective_compute(
                    "ReduceScatter", ALU.add, replica_groups=groups,
                    ins=[opart_ds[beta][:, :]],
                    outs=[ored_d[beta * OUTR:(beta + 1) * OUTR, :]]))
            rs_done = ccx.mark()

            if not ph("ln2"):
                continue

            def emit_ln2(beta=beta, rs_done=rs_done):
                ln2_body(beta, rs_done)

            if beta == NB - 1:
                emit_ln2()
            else:
                pending_ln2[0] = emit_ln2

        if dbg_tensors:
            dbg_map = {"ppart": ppart_d, "psumd": psum_d, "xbf": xbf_d,
                       "x1bf": x1bf_d, "ored": ored_d}
            for b in range(NB):
                dbg_map[f"x1tok{b}"] = x1_ds[b]
                dbg_map[f"opart{b}"] = opart_ds[b]

            @block.gpsimd
            def _(g):
                dmas.wait(g)
                dmag.wait(g)
                ccx.wait(g)
                for nm in dbg_tensors:
                    src = dbg_map[nm]
                    dst = nc.dram_tensor(
                        "dbg_" + nm, list(src.shape),
                        src.dtype, kind="ExternalOutput").ap()
                    dmag(g, lambda src=src, dst=dst: g.dma_start(
                        out=dst, in_=src))
                dmag.wait(g)

        @block.sync
        def _(s):
            dmas.wait(s)

        @block.gpsimd
        def _(g):
            dmag.wait(g)
            ccx.wait(g)
            ln2st.wait(g)

    return nc


def make_inmaps(cfg: Cfg, x, Wqkv, Wproj, gate_w, w1, w2, lq1, lk1, lq2, lk2,
                rms_scale):
    T, D, NB, NE, HC = cfg.T, cfg.D, cfg.NB, cfg.NE, cfg.HC
    xf = np.ascontiguousarray(np.asarray(x).reshape(NB * T, D), dtype=np.float32)
    xts = np.ascontiguousarray(
        xf.reshape(NB * T // cfg.TB, cfg.TB, cfg.DC, 128)
        .transpose(3, 0, 2, 1).reshape(128, NB * T * cfg.DC),
        dtype=np.float32)
    lamv = np.ascontiguousarray(
        np.concatenate([lq1, lq2, lk1, lk2])[None, :], dtype=np.float32)
    rmsrow = np.ascontiguousarray(
        np.concatenate([rms_scale, rms_scale])[None, :], dtype=np.float32)
    in_maps = []
    for c in range(cfg.cores):
        A, B = 2 * c, 2 * c + 1
        qcols = []
        for hh in (A, B):
            qcols += list(range(32 * hh, 32 * hh + 32))
            qcols += list(range(D // 2 + 32 * hh, D // 2 + 32 * hh + 32))
        kcols = [D + cc for cc in qcols]
        vcols = []
        for hh in (A, B):
            vcols += list(range(2 * D + 64 * hh, 2 * D + 64 * hh + 64))
        wq = np.ascontiguousarray(Wqkv[:, qcols + kcols + vcols], dtype=np.float32)
        wp = np.ascontiguousarray(Wproj[128 * c:128 * (c + 1), :], dtype=np.float32)
        perm = [c] + [e for e in range(NE) if e != c]
        gw = np.ascontiguousarray(gate_w[:, perm], dtype=np.float32)
        w1c = np.ascontiguousarray(
            np.asarray(w1)[c].reshape(cfg.DC, 128, HC, 128)
            .transpose(1, 2, 0, 3).reshape(128, HC * cfg.DC * 128),
            dtype=np.float32)
        w2c = np.ascontiguousarray(np.asarray(w2)[c], dtype=np.float32)
        in_maps.append({
            "x": xf, "xts": xts, "wqkv": wq, "wproj": wp, "gatew": gw,
            "w1": w1c, "w2": w2c, "lamv": lamv, "rmsrow": rmsrow,
        })
    return in_maps


def assemble(cfg: Cfg, results):
    out = np.empty((cfg.NB, cfg.T, cfg.D), dtype=np.float32)
    for c in range(cfg.cores):
        r = np.asarray(results[c]["out"]).reshape(cfg.NB, cfg.OUTR, cfg.D)
        out[:, c * cfg.OUTR:(c + 1) * cfg.OUTR, :] = r
    return out


def kernel(L0_attn_x, L0_attn_Wqkv, L0_attn_Wproj, L0_attn_lambda_q1,
           L0_attn_lambda_k1, L0_attn_lambda_q2, L0_attn_lambda_k2,
           L0_attn_rms_scale, L0_ffn_gate_w, L0_ffn_w1, L0_ffn_w2):
    cfg = FULL
    in_maps = make_inmaps(
        cfg, np.asarray(L0_attn_x), np.asarray(L0_attn_Wqkv),
        np.asarray(L0_attn_Wproj), np.asarray(L0_ffn_gate_w),
        np.asarray(L0_ffn_w1), np.asarray(L0_ffn_w2),
        np.asarray(L0_attn_lambda_q1), np.asarray(L0_attn_lambda_k1),
        np.asarray(L0_attn_lambda_q2), np.asarray(L0_attn_lambda_k2),
        np.asarray(L0_attn_rms_scale))
    nc = build_nc(cfg)
    res = run_bass_kernel_spmd(nc, in_maps, core_ids=list(range(cfg.cores)))
    return assemble(cfg, res.results)


# version 25 (final)
# speedup vs baseline: 1.0153x; 1.0131x over previous
import math
from contextlib import ExitStack

import numpy as np
import ml_dtypes

import concourse.bass as bass
import concourse.mybir as mybir
from concourse.bass import IndirectOffsetOnAxis
from concourse.bass_utils import run_bass_kernel_spmd

F32 = mybir.dt.float32
BF16 = mybir.dt.bfloat16
U32 = mybir.dt.uint32
AF = mybir.ActivationFunctionType
ALU = mybir.AluOpType
AX = mybir.AxisListType

LAMBDA_INIT = 0.8 - 0.6 * math.exp(-0.3 * 1)


class Cfg:
    def __init__(self, cores=8, T=4096, HID=4096, NB=4, CAP=2.0):
        self.cores = cores
        self.T = T
        self.D = 128 * cores
        self.HID = HID
        self.NB = NB
        self.NE = cores
        self.K = int(CAP * T / self.NE)
        D = self.D
        self.DC = D // 128
        self.TC = T // 128
        self.QB = 256
        self.NQB = T // self.QB
        self.KCN = T // 128
        self.HC = HID // 128
        self.KC = self.K // 128
        self.SB = min(128, self.K)
        self.NSB = self.K // self.SB
        self.OUTR = T // cores
        self.TB = 256
        self.NTB = T // self.TB
        self.PO_F = min(512, D)
        self.POT = D // self.PO_F


FULL = Cfg()


class Eng:

    def __init__(self, nc, ctx, name, dma=False, serialize=True):
        self.sem = ctx.enter_context(nc.semaphore(name))
        self.count = 0
        self.inc = 16 if dma else 1
        self.serialize = serialize

    def mark(self):
        return self.count

    def __call__(self, eng, emit, waits=(), inc=True, noser=False):
        merged = {}

        def add(other, cnt):
            if cnt and cnt > 0:
                k = id(other.sem)
                prev = merged.get(k)
                if prev is None or cnt > prev[1]:
                    merged[k] = (other, cnt)

        for item in waits:
            if isinstance(item, list):
                for (o, c) in item:
                    add(o, c)
            else:
                add(*item)
        if self.serialize and not noser and self.count:
            add(self, self.count)
        for (o, c) in merged.values():
            eng.wait_ge(o.sem, c)
        inst = emit()
        if inc:
            inst.then_inc(self.sem, self.inc)
            self.count += self.inc
        return self.count

    def wait(self, eng, count=None):
        eng.wait_ge(self.sem, self.count if count is None else count)


class DmaPool:

    def __init__(self, nc, ctx, name, nlanes=8):
        self.lanes = [Eng(nc, ctx, f"{name}{i}", dma=True, serialize=True)
                      for i in range(nlanes)]
        self.i = 0

    def __call__(self, eng, emit, waits=()):
        lane = self.lanes[self.i]
        self.i = (self.i + 1) % len(self.lanes)
        lane(eng, emit, waits)
        return [(lane, lane.count)]

    def mark(self):
        return [(l, l.count) for l in self.lanes if l.count]

    def wait(self, eng, marks=None):
        for (l, c) in (self.mark() if marks is None else marks):
            if c:
                eng.wait_ge(l.sem, c)


PHASES = ["qkv", "vtr", "attn", "ar", "ln1", "x1bf", "gate", "sm", "bis",
          "mask", "sel", "ex", "mg", "ffn", "rs", "ln2", "all"]


def build_nc(cfg: Cfg, debug=False, stop_after="all", dbg_tensors=(),
             attn_sub="full"):
    T, D, NB, NE = cfg.T, cfg.D, cfg.NB, cfg.NE
    DC, TC, QB, NQB, KCN = cfg.DC, cfg.TC, cfg.QB, cfg.NQB, cfg.KCN
    HID, HC, K, KC, SB, NSB = cfg.HID, cfg.HC, cfg.K, cfg.KC, cfg.SB, cfg.NSB
    OUTR, TB, NTB = cfg.OUTR, cfg.TB, cfg.NTB
    PO_F, POT = cfg.PO_F, cfg.POT
    groups = [list(range(cfg.cores))]
    NITER = 36

    SL = PHASES.index(stop_after)

    def ph(name):
        return SL >= PHASES.index(name)

    nc = bass.Bass("TRN2", target_bir_lowering=False, debug=debug)

    x_d = nc.dram_tensor("x", [NB * T, D], F32, kind="ExternalInput").ap()
    xts_d = nc.dram_tensor("xts", [128, NB * T * (D // 128)], F32,
                           kind="ExternalInput").ap()
    wqkv_d = nc.dram_tensor("wqkv", [D, 384], F32, kind="ExternalInput").ap()
    wproj_d = nc.dram_tensor("wproj", [128, D], F32, kind="ExternalInput").ap()
    gatew_d = nc.dram_tensor("gatew", [D, NE], F32, kind="ExternalInput").ap()
    w1_d = nc.dram_tensor("w1", [128, HC * DC * 128], F32,
                          kind="ExternalInput").ap()
    w2_d = nc.dram_tensor("w2", [HID, D], F32, kind="ExternalInput").ap()
    lam_d = nc.dram_tensor("lamv", [1, 128], F32, kind="ExternalInput").ap()
    rms_d = nc.dram_tensor("rmsrow", [1, 128], F32, kind="ExternalInput").ap()
    out_d = nc.dram_tensor("out", [NB * OUTR, D], F32, kind="ExternalOutput").ap()

    xbf_d = nc.dram_tensor("xbf", [NB * T, D], BF16).ap()
    xtsbf_d = nc.dram_tensor("xtsbf", [128, NB * T * (D // 128)], BF16).ap()
    w1bf_d = nc.dram_tensor("w1bf", [128, HC * DC * 128], BF16).ap()
    ppart_d = nc.dram_tensor("ppart", [NB * T, D], F32).ap()
    psum_d = nc.dram_tensor("psumd", [NB * T, D], F32,
                            addr_space="Shared" if cfg.cores > 4 else "Local").ap()
    x1_ds = [nc.dram_tensor(f"x1tok{b}", [T, D], F32).ap() for b in range(NB)]
    x1bf_d = nc.dram_tensor("x1bf", [NB * T, D], BF16).ap()
    opart_ds = [nc.dram_tensor(f"opart{b}", [T, D], F32).ap() for b in range(NB)]
    ored_d = nc.dram_tensor("ored", [NB * OUTR, D], F32).ap()

    with ExitStack() as ctx:
        sb = lambda n, s, d: ctx.enter_context(nc.sbuf_tensor(n, s, d))
        ps_ = lambda n, s, d: ctx.enter_context(nc.psum_tensor(n, s, d))

        ident_b = sb("ident_b", [128, 128], BF16)
        ident_f = sb("ident_f", [128, 128], F32)
        iota_col = sb("iota_col", [128, 1], F32)
        iota_row = None

        ones8_f = sb("ones8_f", [NE, 128], F32)
        onesrow_f = sb("onesrow_f", [1, 128], F32)
        onescol_f = sb("onescol_f", [128, 1], F32)
        hT_sb = sb("hT_sb", [128, HC * 512], BF16)
        hT_f32 = hT_sb[:, :].bitcast(F32)
        iota_j = hT_f32[:, T:T + K]
        iota_colb = sb("iota_colb", [128, 1], BF16)
        chunkv_b = sb("chunkv_b", [128, TC], BF16)
        iota_e = sb("iota_e", [NE, TC * NE], F32)
        ind_em = sb("ind_em", [NE, TC * NE], F32)
        biasc = sb("biasc", [128, 6], F32)
        scaletile = sb("scaletile", [128, 64], F32)
        neglam = sb("neglam", [128, 1], F32)
        lam_sb = sb("lam_sb", [1, 128], F32)
        lamp = sb("lamp", [1, 64], F32)
        lamsc = sb("lamsc", [1, 8], F32)
        rmsrow_sb = sb("rmsrow_sb", [1, 128], F32)

        wqkv_sb = sb("wqkv_sb", [128, DC * 384], BF16)
        wproj_sb = sb("wproj_sb", [128, D], BF16)
        gatew_sb = sb("gatew_sb", [128, DC * NE], BF16)
        w2_sb = sb("w2_sb", [128, HC * D], BF16)
        w1_buf = sb("w1_buf", [128, 2 * D], BF16)

        xfm_buf = sb("xfm_buf", [128, 2 * DC * TB], BF16)
        qT_sb = sb("qT_sb", [128, T], BF16)
        kT_sb = sb("kT_sb", [128, T], BF16)

        vaug_sb = sb("vaug_sb", [128, TC * 130], BF16)
        qblk_sb = sb("qblk_sb", [128, 4 * QB], BF16)
        probs_buf = sb("probs_buf", [128, 2 * 4 * QB], BF16)
        zcatT_sb = sb("zcatT_sb", [128, QB], BF16)
        zct_sb = sb("zct_sb", [128, 256], BF16)
        fm_ep = sb("fm_ep", [128, 1024], F32)
        ep_f = sb("ep_f", [128, 4 * QB], F32)
        iota_row = ep_f[:, 128:256]
        assert D <= 4 * QB
        x18buf = ep_f[:, 0:D]
        ybuf = sb("ybuf", [128, 2 * D], F32)
        x1buf = sb("x1buf", [128, 2 * D], F32)
        x1fm_buf = sb("x1fm_buf", [128, max(T, 4 * D)], BF16)
        vT_sb = x1fm_buf[:, 0:T]
        hi_b = x1fm_buf[0:8, 0:T]
        tie_b = ybuf[:, :].bitcast(BF16)[0:8, 0:T]
        rank_f = hT_f32[0:8, 0:T]
        gacc = rank_f[0:NE, :]
        xq_sb = [hT_f32[:, T + K:T + K + 1024],
                 hT_f32[:, T + K + 1024:T + K + 2048]]
        gtok = sb("gtok", [128, TC * NE], F32)
        probs_tok = gtok
        ssum = sb("ssum", [128, TC], F32)
        tau_lo = sb("tau_lo", [NE, 1], F32)
        tau_hi = sb("tau_hi", [NE, 1], F32)
        tau_mid = sb("tau_mid", [NE, 1], F32)
        tau_m8 = sb("tau_m8", [NE, TC * NE], F32)
        cmp_tok = sb("cmp_tok", [128, TC * NE], F32)
        part8 = sb("part8", [128, NE], F32)
        cnt8 = sb("cnt8", [NE, 1], F32)
        cond8 = sb("cond8", [NE, 1], F32)
        bs_sc = sb("bs_sc", [NE, 8], F32)
        hi_tok = cmp_tok
        tie_tok = sb("tie_tok", [128, TC * NE], F32)

        fill8 = sb("fill8", [NE, 1], F32)

        gs_tok = sb("gs_tok", [128, 2 * TC], F32)
        pc_b2 = [ep_f[:, 0:K // 2].bitcast(BF16),
                 ep_f[:, K // 2:K].bitcast(BF16)]
        payload = sb("payload", [128, 6], BF16)
        mg_f = sb("mg_f", [3, K], BF16)
        mg3 = sb("mg3", [128, 4], F32)
        mcols = sb("mcols", [128, KC], F32)
        midx = sb("midx", [128, KC], U32)
        gcol = sb("gcol", [128, KC], F32)
        xin_sb = sb("xin_sb", [128, 2 * D], BF16)
        xinT_sb = sb("xinT_sb", [128, DC * 512], BF16)
        fw2 = [probs_buf[:, 0:D], probs_buf[:, D:2 * D]]
        xinT_f32 = xinT_sb[:, :].bitcast(F32)
        pstage2 = [xinT_f32[:, 0:1024], xinT_f32[:, 1024:2048]]
        pstage = pstage2[0]
        lnslots = [ybuf[:, 0:D], ybuf[:, D:2 * D]]
        stat_f = sb("stat_f", [128, 16], F32)

        sc0 = ps_("sc0", [128, 4 * QB], F32)
        sc1 = ps_("sc1", [128, 4 * QB], F32)
        accv = ps_("accv", [128, 1024], F32)
        chain = ps_("chain", [128, 1024], F32)
        chA = chain[:, 0:512]
        chB = chain[:, 512:1024]
        chain_bf = chain[:, :].bitcast(BF16)

        def chAb(i):
            return chain_bf[:, i * 128:(i + 1) * 128]

        def chBb(i):
            return chain_bf[:, 1024 + i * 128:1024 + (i + 1) * 128]

        dmas = DmaPool(nc, ctx, "dmas")
        dmag = DmaPool(nc, ctx, "dmag")
        pe = Eng(nc, ctx, "pe", serialize=False)
        act = Eng(nc, ctx, "act")
        dve = Eng(nc, ctx, "dve")
        pool = Eng(nc, ctx, "pool")
        ccx = Eng(nc, ctx, "ccx", serialize=False)
        ln2st = Eng(nc, ctx, "ln2st", dma=True, serialize=False)

        block = ctx.enter_context(nc.Block())

        chfree = {"A": (dve, 0), "B": (dve, 0)}

        def chw(which):
            e, c = chfree[which]
            return [(e, c)] if c > 0 else []

        xconv_m = []
        @block.gpsimd
        def _(g):
            for dc in range(DC):
                dmag(g, lambda dc=dc: g.dma_start(
                    out=wqkv_sb[:, dc * 384:(dc + 1) * 384],
                    in_=wqkv_d[dc * 128:(dc + 1) * 128, :]))
            dmag(g, lambda: g.dma_start(out=wproj_sb[:, :], in_=wproj_d))
            for dc in range(DC):
                dmag(g, lambda dc=dc: g.dma_start(
                    out=gatew_sb[:, dc * NE:(dc + 1) * NE],
                    in_=gatew_d[dc * 128:(dc + 1) * 128, :]))
            for hc in range(HC):
                dmag(g, lambda hc=hc: g.dma_start(
                    out=w2_sb[:, hc * D:(hc + 1) * D],
                    in_=w2_d[hc * 128:(hc + 1) * 128, :]))
            for ci in range(8):
                cw = NB * T * DC // 8
                dmag(g, lambda ci=ci, cw=cw: g.dma_start(
                    out=xtsbf_d[:, ci * cw:(ci + 1) * cw],
                    in_=xts_d[:, ci * cw:(ci + 1) * cw]))
                xconv_m.append(dmag.mark())
            for ci in range(4):
                cw = HC * DC * 128 // 4
                dmag(g, lambda ci=ci, cw=cw: g.dma_start(
                    out=w1bf_d[:, ci * cw:(ci + 1) * cw],
                    in_=w1_d[:, ci * cw:(ci + 1) * cw]))
            pool(g, lambda: g.iota(iota_col[:, :], pattern=[[0, 1]], base=0,
                                   channel_multiplier=1,
                                   allow_small_or_imprecise_dtypes=True))
            pool(g, lambda: g.iota(iota_row[:, :], pattern=[[1, 128]], base=0,
                                   channel_multiplier=0,
                                   allow_small_or_imprecise_dtypes=True))
            pool(g, lambda: g.iota(iota_j[:, :], pattern=[[1, K]], base=0,
                                   channel_multiplier=0,
                                   allow_small_or_imprecise_dtypes=True))
            pool(g, lambda: g.iota(iota_colb[:, :], pattern=[[0, 1]], base=0,
                                   channel_multiplier=1,
                                   allow_small_or_imprecise_dtypes=True))
            pool(g, lambda: g.iota(chunkv_b[:, :], pattern=[[1, TC]], base=0,
                                   channel_multiplier=0,
                                   allow_small_or_imprecise_dtypes=True))
            pool(g, lambda: g.iota(iota_e[:, :], pattern=[[0, TC], [1, NE]],
                                   base=0, channel_multiplier=0,
                                   allow_small_or_imprecise_dtypes=True))


        @block.sync
        def _(s):
            dmas(s, lambda: s.dma_start(out=lam_sb[:, :], in_=lam_d))
            dmas(s, lambda: s.dma_start(out=rmsrow_sb[:, :], in_=rms_d))

        @block.vector
        def _(v):
            pool.wait(v)
            dmas.wait(v)
            dve(v, lambda: v.memset(ones8_f[:, :], 1.0))
            dve(v, lambda: v.memset(onesrow_f[:, :], 1.0))
            dve(v, lambda: v.memset(onescol_f[:, :], 1.0))
            dve(v, lambda: v.memset(biasc[:, :], 0.0))
            dve(v, lambda: v.memset(biasc[:, 0:1], -8.0))
            dve(v, lambda: v.memset(biasc[:, 1:2], -0.5))
            dve(v, lambda: v.memset(biasc[:, 2:3], 1.0 / math.sqrt(32.0)))
            dve(v, lambda: v.memset(biasc[:, 4:5], 1e-6))
            dve(v, lambda: v.memset(biasc[:, 5:6], 1.0 / 64))
            dve(v, lambda: v.memset(qblk_sb[:, :], 0.0))
            va3 = vaug_sb[:, :].rearrange("p (t c) -> p t c", c=130)
            dve(v, lambda: v.memset(va3[:, :, 64:65], 1.0))
            dve(v, lambda: v.memset(va3[:, :, 129:130], 1.0))
            dve(v, lambda: v.tensor_scalar(
                out=ep_f[0:1, 0:64], in0=rmsrow_sb[0:1, 0:64],
                scalar1=(1.0 - LAMBDA_INIT), scalar2=None, op0=ALU.mult))
            dve(v, lambda: v.tensor_scalar(
                out=ident_b[:, :], in0=iota_row[:, :], scalar1=iota_col[:, :1],
                scalar2=None, op0=ALU.is_equal))
            dve(v, lambda: v.tensor_scalar(
                out=ident_f[:, :], in0=iota_row[:, :], scalar1=iota_col[:, :1],
                scalar2=None, op0=ALU.is_equal))
            dve(v, lambda: v.tensor_scalar(
                out=ind_em[:, :], in0=iota_e[:, :], scalar1=iota_col[:NE, :1],
                scalar2=None, op0=ALU.is_equal))
            dve(v, lambda: v.tensor_mul(lamp[:, :], lam_sb[:, 0:64],
                                        lam_sb[:, 64:128]))
            dve(v, lambda: v.reduce_sum(lamsc[0:1, 0:1], lamp[:, 0:32],
                                        axis=AX.X))
            dve(v, lambda: v.reduce_sum(lamsc[0:1, 1:2], lamp[:, 32:64],
                                        axis=AX.X))

        @block.scalar
        def _(s):
            dve.wait(s)
            act(s, lambda: s.activation(out=lamsc[0:1, 2:4], in_=lamsc[0:1, 0:2],
                                        func=AF.Exp))
        lam_act = act.mark()

        @block.vector
        def _(v):
            act.wait(v)
            dve(v, lambda: v.tensor_sub(lamsc[0:1, 4:5], lamsc[0:1, 3:4],
                                        lamsc[0:1, 2:3]))
        lam_dve2 = dve.mark()

        @block.tensor
        def _(t):
            pe(t, lambda: t.matmul(chA[:, 0:1], onesrow_f[0:1, :],
                                   lamsc[0:1, 4:5], start=True, stop=True),
               waits=[(dve, lam_dve2)])
            pe(t, lambda: t.matmul(chB[:, 0:64], onesrow_f[0:1, :],
                                   ep_f[0:1, 0:64], start=True, stop=True))
        bc_mm = pe.mark()

        @block.vector
        def _(v):
            dve(v, lambda: v.tensor_scalar(
                out=neglam[:, :], in0=chA[:, 0:1], scalar1=-LAMBDA_INIT,
                scalar2=None, op0=ALU.add), waits=[(pe, bc_mm)])
            dve(v, lambda: v.tensor_copy(out=scaletile[:, :], in_=chB[:, 0:64]))
        chfree["A"] = (dve, dve.mark())
        chfree["B"] = (dve, dve.mark())
        setup_dmag = dmag.mark()

        def ln2_body(beta, rs_done):
            nln = (OUTR + 127) // 128
            l2stats = [0] * nln
            l2_base = act.count

            def ln2_scale(oc):
                par = oc % 2
                slot = lnslots[par]
                so = par * 8
                r0 = oc * 128

                @block.vector
                def _(v, slot=slot, so=so, oc=oc):
                    dve(v, lambda: v.reciprocal(
                        stat_f[:, so + 5:so + 6], stat_f[:, so + 5:so + 6]),
                        waits=[(act, l2_base + oc + 1)])
                    dve(v, lambda: v.tensor_scalar(
                        out=slot[:, :], in0=slot[:, :],
                        scalar1=stat_f[:, so + 2:so + 3],
                        scalar2=stat_f[:, so + 5:so + 6],
                        op0=ALU.subtract, op1=ALU.mult))
                l2o = dve.mark()

                @block.gpsimd
                def _(g, r0=r0, slot=slot, l2o=l2o):
                    ln2st(g, lambda: g.dma_start(
                        out=out_d[beta * OUTR + r0:beta * OUTR + r0 + 128, :],
                        in_=slot[:, :]), waits=[(dve, l2o)])

            for oc in range(nln):
                par = oc % 2
                slot = lnslots[par]
                so = par * 8

                @block.gpsimd
                def _(g, oc=oc, slot=slot, rs_done=rs_done):
                    w = [(ccx, rs_done)]
                    if oc >= 2:
                        w.append((ln2st, ln2st.count))
                    dmag(g, lambda oc=oc: g.dma_start(
                        out=slot[:, :],
                        in_=ored_d[beta * OUTR + oc * 128:
                                   beta * OUTR + (oc + 1) * 128, :]),
                        waits=w)
                lnl = dmag.mark()

                @block.vector
                def _(v, slot=slot, so=so, lnl=lnl):
                    Yl = slot[:, :]
                    dve(v, lambda: v.tensor_scalar(
                        out=hT_f32[:, 0:1024], in0=Yl, scalar1=1.0,
                        scalar2=None, op0=ALU.mult, op1=ALU.add,
                        accum_out=stat_f[:, so + 0:so + 1]),
                        waits=[lnl])
                    dve(v, lambda: v.scalar_tensor_tensor(
                        out=hT_f32[:, 0:1024], in0=Yl, scalar=1.0, in1=Yl,
                        op0=ALU.mult, op1=ALU.mult,
                        accum_out=stat_f[:, so + 1:so + 2]))
                    dve(v, lambda: v.tensor_scalar(
                        out=stat_f[:, so + 2:so + 3],
                        in0=stat_f[:, so + 0:so + 1],
                        scalar1=1.0 / D, scalar2=None, op0=ALU.mult))
                    dve(v, lambda: v.tensor_mul(
                        stat_f[:, so + 3:so + 4], stat_f[:, so + 2:so + 3],
                        stat_f[:, so + 2:so + 3]))
                    dve(v, lambda: v.scalar_tensor_tensor(
                        out=stat_f[:, so + 4:so + 5],
                        in0=stat_f[:, so + 1:so + 2], scalar=1.0 / D,
                        in1=stat_f[:, so + 3:so + 4],
                        op0=ALU.mult, op1=ALU.subtract))
                    dve(v, lambda: v.tensor_scalar(
                        out=stat_f[:, so + 4:so + 5],
                        in0=stat_f[:, so + 4:so + 5],
                        scalar1=1e-5, scalar2=None, op0=ALU.add))
                l2stats[oc] = dve.mark()
                if oc >= 1:
                    ln2_scale(oc - 1)
            ln2_scale(nln - 1)

            @block.scalar
            def _(s):
                for oc in range(nln):
                    so = (oc % 2) * 8
                    act(s, lambda so=so, oc=oc: s.activation(
                        out=stat_f[:, so + 5:so + 6],
                        in_=stat_f[:, so + 4:so + 5], func=AF.Sqrt),
                        waits=[(dve, l2stats[oc])], noser=True)
            assert act.count == l2_base + nln
            out_st_h[0] = [(ln2st, ln2st.count)]


        out_st_h = [[]]
        pending_ln2 = [None]
        prev_ffn_dma = []
        prev_gate_pe = 0
        prev_x1_dma = []
        at_exp_war = {0: 0, 1: 0, 2: 0, 3: 0}
        at_chain_war = [[]]
        at_trans_m = [0]
        at_evac_prev = [0]
        at_qblk_war = [0]
        prev_ffn_pe = [0]
        prev_gelu_m = [0]
        ffn_acc_last = [0]
        for beta in range(NB):
            xr = beta * T

            qk_evac = dve.mark()
            xfm_war = {0: 0, 1: 0}
            for tb in range(NTB):
                par = tb % 2
                xo = par * DC * TB

                @block.sync
                def _(s, tb=tb, par=par, xo=xo):
                    c0 = (beta * NTB + tb) * DC * TB
                    w = [xconv_m[c0 // (NB * T * DC // 8)]]
                    if xfm_war[par]:
                        w.append((pe, xfm_war[par]))
                    dmas(s, lambda c0=c0: s.dma_start(
                        out=xfm_buf[:, xo:xo + DC * TB],
                        in_=xtsbf_d[:, c0:c0 + DC * TB]), waits=w)
                xfm_ld = dmas.mark()

                @block.tensor
                def _(t, tb=tb, xo=xo):
                    outs = [sc0[:, 0:TB], sc0[:, TB:2 * TB], sc1[:, 0:TB]]
                    w = [xfm_ld,
                         (dve, qk_evac if tb > 0 else ffn_acc_last[0])]
                    if beta == 0 and tb == 0:
                        w.append(setup_dmag)
                    first = True
                    for ci in range(3):
                        for dc in range(DC):
                            pe(t, lambda ci=ci, dc=dc: t.matmul(
                                outs[ci][:, 0:TB],
                                wqkv_sb[:, dc * 384 + ci * 128:
                                        dc * 384 + (ci + 1) * 128],
                                xfm_buf[:, xo + dc * TB:xo + (dc + 1) * TB],
                                start=(dc == 0), stop=(dc == DC - 1)),
                               waits=w if first else ())
                            first = False
                xfm_war[par] = pe.mark()
                qkv_mm = pe.mark()

                @block.vector
                def _(v, tb=tb):
                    dve(v, lambda: v.tensor_copy(
                        out=qT_sb[:, tb * TB:(tb + 1) * TB], in_=sc0[:, 0:TB]),
                        waits=[(pe, qkv_mm)])
                    dve(v, lambda: v.tensor_copy(
                        out=kT_sb[:, tb * TB:(tb + 1) * TB], in_=sc0[:, TB:2 * TB]))
                    wv = []
                    if prev_gate_pe:
                        wv.append((pe, prev_gate_pe))
                    dve(v, lambda: v.tensor_copy(
                        out=vT_sb[:, tb * TB:(tb + 1) * TB], in_=sc1[:, 0:TB]),
                        waits=wv)
                qk_evac = dve.mark()

            if not ph("vtr"):
                continue
            for tg in range(0, TC, 8):
                n = min(8, TC - tg)
                bk = "A" if (tg // 8) % 2 == 0 else "B"
                slot = chAb if bk == "A" else chBb

                @block.tensor
                def _(t, tg=tg, n=n, slot=slot, bk=bk):
                    w = chw(bk) + [(dve, qk_evac)]
                    for i in range(n):
                        pe(t, lambda i=i: t.transpose(
                            slot(i), vT_sb[:, (tg + i) * 128:(tg + i + 1) * 128],
                            ident_b[:, :]), waits=w if i == 0 else ())
                vtp = pe.mark()

                @block.vector
                def _(v, tg=tg, n=n, slot=slot):
                    for i in range(n):
                        tc = tg + i
                        dve(v, lambda i=i, tc=tc: v.tensor_copy(
                            out=vaug_sb[:, tc * 130:tc * 130 + 64],
                            in_=slot(i)[:, 0:64]),
                            waits=[(pe, vtp)] if i == 0 else ())
                        dve(v, lambda i=i, tc=tc: v.tensor_copy(
                            out=vaug_sb[:, tc * 130 + 65:tc * 130 + 129],
                            in_=slot(i)[:, 64:128]))
                chfree[bk] = (dve, dve.mark())
            vtok_ready = dve.mark()
            at_chain_war[0] = chw("A") + chw("B")

            if not ph("attn"):
                continue
            accA = accv[:, 0:512]
            accB = accv[:, 512:1024]
            E = ep_f
            A0 = 16
            Z0 = A0 + 512
            SQ0 = Z0 + 256
            pstage_war = {0: [], 1: []}
            xq_war = {0: [], 1: []}
            xq_lanes = {0: [], 1: []}
            p_dma = dmas.mark()
            p_dma_q = [0, 0, 0, 0]
            epi = None
            NEARLY = 6
            for qb in range(NQB + 1):
                last = qb == NQB
                if not last:
                    q0 = qb * QB

                    @block.vector
                    def _(v, q0=q0, qb=qb):
                        w = []
                        if at_qblk_war[0]:
                            w.append((pe, at_qblk_war[0]))
                        if qb == 0:
                            w += list(prev_x1_dma)
                        for X in range(4):
                            dve(v, lambda X=X: v.tensor_copy(
                                out=qblk_sb[32 * X:32 * (X + 1),
                                            X * QB:(X + 1) * QB],
                                in_=qT_sb[32 * X:32 * (X + 1), q0:q0 + QB]),
                                waits=w if X == 0 else ())
                    qblk_done = dve.mark()

                if epi is not None:
                    @block.sync
                    def _(s, epi=epi):
                        for qt in range(2):
                            w = list(xq_war[qt])
                            if epi["q0"] == 0:
                                if prev_gelu_m[0]:
                                    w.append((act, prev_gelu_m[0]))
                                if prev_ffn_pe[0]:
                                    w.append((pe, prev_ffn_pe[0]))
                            r = xr + epi["q0"] + qt * 128
                            h = dmas(s, lambda r=r, qt=qt: s.dma_start(
                                out=xq_sb[qt][:, :], in_=x_d[r:r + 128, :]),
                                waits=w)
                            xq_lanes[qt] = h

                    @block.tensor
                    def _(t, epi=epi):
                        w = [(dve, epi["evac"])] + list(at_chain_war[0])
                        for s_ in range(8):
                            X, qt = s_ % 4, s_ // 4
                            co = X * 256 + qt * 128
                            cb = s_ * 65 if s_ < 4 else 512 + (s_ - 4) * 65
                            pe(t, lambda co=co, cb=cb: t.transpose(
                                chain[:, cb:cb + 65], fm_ep[0:65, co:co + 128],
                                ident_f[0:65, 0:65]),
                               waits=w if s_ == 0 else ())
                    at_trans_m[0] = pe.mark()

                sc_done = {}
                if not last:
                    act_base = act.count
                    nmid = 5 if epi is not None else 0

                    def _ep(kc):
                        return act_base + kc + 1 + (nmid if kc >= NEARLY
                                                    else 0)
                    exp_pred = [_ep(kc) for kc in range(KCN)]
                    sqrt_pred = (act_base + NEARLY + 5) if epi is not None \
                        else 0

                    @block.tensor
                    def _(t, qb=qb, qblk_done=qblk_done):
                        for kc in range(KCN):
                            par = kc % 2
                            scs = sc0 if par == 0 else sc1
                            po = par * 4 * QB
                            w = []
                            if at_exp_war[par]:
                                w.append((act, at_exp_war[par]))
                            if kc == 0:
                                w.append((dve, qblk_done))
                                if qb == 0:
                                    w.append((dve, vtok_ready))
                                    w.append(prev_ffn_dma)
                            for half in range(2):
                                pe(t, lambda kc=kc, scs=scs, half=half:
                                   t.matmul(
                                    scs[:, half * 512:(half + 1) * 512],
                                    kT_sb[:, kc * 128:(kc + 1) * 128],
                                    qblk_sb[:, half * 512:(half + 1) * 512],
                                    start=True, stop=True),
                                   waits=w if half == 0 else ())
                            at_exp_war[par] = exp_pred[kc]
                            sc_done[kc] = pe.mark()
                            wv = [(act, exp_pred[kc])]
                            if kc == 0:
                                if at_evac_prev[0]:
                                    wv.append((dve, at_evac_prev[0]))
                                if qb == 0:
                                    wv.append((dve, vtok_ready))
                            pe(t, lambda kc=kc, po=po: t.matmul(
                                accA[0:65, :],
                                vaug_sb[:, kc * 130:kc * 130 + 65],
                                probs_buf[:, po:po + 512],
                                start=(kc == 0), stop=(kc == KCN - 1)),
                               waits=wv)
                            pe(t, lambda kc=kc, po=po: t.matmul(
                                accB[0:65, :],
                                vaug_sb[:, kc * 130 + 65:kc * 130 + 130],
                                probs_buf[:, po + 512:po + 1024],
                                start=(kc == 0), stop=(kc == KCN - 1)))
                    pv_all = pe.mark()

                if epi is not None:
                    zm = [0] * 4

                    @block.vector
                    def _(v, epi=epi, zm=zm):
                        w = [(pe, at_trans_m[0])]
                        ch3a = chain[:, 0:260].rearrange(
                            "p (s c) -> p s c", c=65)
                        ch3b = chain[:, 512:772].rearrange(
                            "p (s c) -> p s c", c=65)
                        dve(v, lambda: v.reciprocal(
                            E[:, 0:4].rearrange("p (s c) -> p s c", c=1),
                            ch3a[:, :, 64:65]), waits=w)
                        dve(v, lambda: v.reciprocal(
                            E[:, 4:8].rearrange("p (s c) -> p s c", c=1),
                            ch3b[:, :, 64:65]))
                        eodd = E[:, 0:8].rearrange(
                            "p (s c) -> p s c", c=2)[:, :, 1:2]
                        dve(v, lambda: v.tensor_scalar(
                            out=eodd, in0=eodd, scalar1=neglam[:, :1],
                            scalar2=None, op0=ALU.mult))
                        for qt in range(2):
                            for head in range(2):
                                s1 = qt * 4 + 2 * head
                                s2 = s1 + 1
                                cb1 = s1 * 65 if s1 < 4 else 512 + (s1 - 4) * 65
                                cb2 = s2 * 65 if s2 < 4 else 512 + (s2 - 4) * 65
                                zi = qt * 2 + head
                                z = E[:, Z0 + zi * 64:Z0 + (zi + 1) * 64]
                                dve(v, lambda cb1=cb1, s1=s1, z=z:
                                    v.tensor_scalar(
                                        out=z, in0=chain[:, cb1:cb1 + 64],
                                        scalar1=E[:, s1:s1 + 1], scalar2=None,
                                        op0=ALU.mult))
                                dve(v, lambda cb2=cb2, s2=s2, z=z:
                                    v.scalar_tensor_tensor(
                                        out=z, in0=chain[:, cb2:cb2 + 64],
                                        scalar=E[:, s2:s2 + 1], in1=z,
                                        op0=ALU.mult, op1=ALU.add))
                                zm[zi] = dve.mark()
                    epi["zm"] = zm
                    epi["rsq"] = dve.mark()

                @block.scalar
                def _(s, epi=epi, sc_done=sc_done, last=last):
                    def emit_exp(kc):
                        par = kc % 2
                        scs = sc0 if par == 0 else sc1
                        po = par * 4 * QB
                        act(s, lambda scs=scs, po=po: s.activation(
                            out=probs_buf[:, po:po + 4 * QB],
                            in_=scs[:, 0:4 * QB], func=AF.Exp,
                            bias=biasc[:, 0:1], scale=biasc[:, 2:3]),
                            waits=[(pe, sc_done[kc])], noser=True)

                    if not last:
                        for kc in range(NEARLY):
                            emit_exp(kc)
                    if epi is not None:
                        for zi in range(4):
                            act(s, lambda zi=zi: s.activation(
                                out=E[:, A0 + zi * 64:A0 + (zi + 1) * 64],
                                in_=E[:, Z0 + zi * 64:Z0 + (zi + 1) * 64],
                                func=AF.Square,
                                accum_out=E[:, 8 + zi:9 + zi]),
                                waits=[(dve, epi["zm"][zi])], noser=True)
                        act(s, lambda: s.activation(
                            out=E[:, 12:16], in_=E[:, 8:12], func=AF.Sqrt,
                            bias=biasc[:, 4:5], scale=biasc[:, 5:6]))
                    if not last:
                        for kc in range(NEARLY, KCN):
                            emit_exp(kc)
                if not last:
                    assert act.count == act_base + KCN + nmid
                    if epi is not None:
                        assert act.count >= sqrt_pred
                else:
                    sqrt_pred = act.count

                if qb == 2 and pending_ln2[0] is not None:
                    pending_ln2[0]()
                    pending_ln2[0] = None

                if epi is not None:
                    @block.vector
                    def _(v, sqrt_pred=sqrt_pred):
                        dve(v, lambda: v.reciprocal(E[:, 12:16], E[:, 12:16]),
                            waits=[(act, sqrt_pred)])
                        for qt in range(2):
                            for head in range(2):
                                zi = qt * 2 + head
                                dve(v, lambda zi=zi, qt=qt, head=head:
                                    v.scalar_tensor_tensor(
                                        out=zct_sb[:, qt * 128 + head * 64:
                                                   qt * 128 + (head + 1) * 64],
                                        in0=E[:, Z0 + zi * 64:
                                              Z0 + (zi + 1) * 64],
                                        scalar=E[:, 12 + zi:13 + zi],
                                        in1=scaletile[:, :],
                                        op0=ALU.mult, op1=ALU.mult))
                    zct_done = dve.mark()

                    @block.tensor
                    def _(t, zct_done=zct_done):
                        for qt in range(2):
                            pe(t, lambda qt=qt: t.transpose(
                                chain_bf[:, qt * 128:(qt + 1) * 128],
                                zct_sb[:, qt * 128:(qt + 1) * 128],
                                ident_b[:, :]),
                               waits=[(dve, zct_done)] if qt == 0 else ())
                    zT_mm = pe.mark()

                    @block.vector
                    def _(v, zT_mm=zT_mm):
                        for qt in range(2):
                            dve(v, lambda qt=qt: v.tensor_copy(
                                out=zcatT_sb[:, qt * 128:(qt + 1) * 128],
                                in_=chain_bf[:, qt * 128:(qt + 1) * 128]),
                                waits=[(pe, zT_mm)] if qt == 0 else ())
                    zcat_done = dve.mark()

                    pstage_evac = 0
                    for qt in range(2):
                        @block.tensor
                        def _(t, qt=qt, zcat_done=zcat_done,
                              pstage_evac=pstage_evac, epi=epi):
                            w = []
                            if qt == 0:
                                w.append((dve, zcat_done))
                                w.append((dve, epi["rsq"]))
                            else:
                                w.append((dve, pstage_evac))
                            pe(t, lambda qt=qt: t.matmul(
                                chA[:, 0:512],
                                zcatT_sb[:, qt * 128:(qt + 1) * 128],
                                wproj_sb[:, 0:512], start=True, stop=True),
                               waits=w)
                            pe(t, lambda qt=qt: t.matmul(
                                chB[:, 0:512],
                                zcatT_sb[:, qt * 128:(qt + 1) * 128],
                                wproj_sb[:, 512:1024], start=True, stop=True))
                        pmm = pe.mark()

                        @block.vector
                        def _(v, qt=qt, pmm=pmm):
                            w = ([(pe, pmm)] + list(pstage_war[qt])
                                 + list(xq_lanes[qt]))
                            dve(v, lambda qt=qt: v.scalar_tensor_tensor(
                                out=pstage2[qt][:, 0:512],
                                in0=xq_sb[qt][:, 0:512],
                                scalar=1.0 / cfg.cores,
                                in1=chA[:, 0:512],
                                op0=ALU.mult, op1=ALU.add), waits=w)
                            dve(v, lambda qt=qt: v.scalar_tensor_tensor(
                                out=pstage2[qt][:, 512:1024],
                                in0=xq_sb[qt][:, 512:1024],
                                scalar=1.0 / cfg.cores,
                                in1=chB[:, 0:512],
                                op0=ALU.mult, op1=ALU.add))
                        pstage_evac = dve.mark()
                        xq_war[qt] = [(dve, pstage_evac)]

                        @block.sync
                        def _(s, qt=qt, epi=epi, pstage_evac=pstage_evac):
                            r = xr + epi["q0"] + qt * 128
                            h = dmas(s, lambda r=r, qt=qt: s.dma_start(
                                out=ppart_d[r:r + 128, :],
                                in_=pstage2[qt][:, :]),
                                waits=[(dve, pstage_evac)])
                            pstage_war[qt] = h
                        p_dma = dmas.mark()
                    pq = epi["q0"] // QB
                    if pq % 4 == 3:
                        p_dma_q[pq // 4] = p_dma
                    at_chain_war[0] = [(dve, pstage_evac)]
                    chfree["A"] = (dve, pstage_evac)
                    chfree["B"] = (dve, pstage_evac)

                if not last:
                    @block.vector
                    def _(v, pv_all=pv_all):
                        w = [(pe, pv_all)]
                        if at_trans_m[0]:
                            w.append((pe, at_trans_m[0]))
                        dve(v, lambda: v.tensor_copy(
                            out=fm_ep[0:65, 0:512], in_=accA[0:65, :]),
                            waits=w)
                        dve(v, lambda: v.tensor_copy(
                            out=fm_ep[0:65, 512:1024], in_=accB[0:65, :]))
                    at_evac_prev[0] = dve.mark()
                    epi = {"evac": at_evac_prev[0], "q0": q0}
                    at_qblk_war[0] = pe.mark()
                else:
                    epi = None

            if not ph("ar"):
                continue
            arq = []

            @block.gpsimd
            def _(g):
                QR = T // 4
                for q in range(4):
                    dmas.wait(g, p_dma_q[q])
                    ccx(g, lambda q=q, QR=QR: g.collective_compute(
                        "AllReduce", ALU.add, replica_groups=groups,
                        ins=[ppart_d[xr + q * QR:xr + (q + 1) * QR, :]],
                        outs=[psum_d[xr + q * QR:xr + (q + 1) * QR, :]]))
                    arq.append(ccx.mark())

            if not ph("ln1"):
                continue
            scale_war = {0: 0, 1: 0}
            x1war = {0: [], 1: []}
            stats_m = [0] * TC
            ln_base = act.count
            x18slot = [ep_f[:, 0:D], fm_ep[:, 0:D]]

            def ln1_scale(tc):
                par = tc % 2
                so = par * 8
                Y = ybuf[:, par * D:(par + 1) * D]

                @block.vector
                def _(v, tc=tc, par=par, so=so, Y=Y):
                    dve(v, lambda: v.reciprocal(
                        stat_f[:, so + 5:so + 6], stat_f[:, so + 5:so + 6]),
                        waits=[(act, ln_base + tc + 1),
                               (pe, at_trans_m[0])])
                    dve(v, lambda: v.tensor_scalar(
                        out=x1buf[:, par * D:(par + 1) * D], in0=Y,
                        scalar1=stat_f[:, so + 2:so + 3],
                        scalar2=stat_f[:, so + 5:so + 6],
                        op0=ALU.subtract, op1=ALU.mult))
                    dve(v, lambda: v.tensor_scalar(
                        out=x18slot[par], in0=x1buf[:, par * D:(par + 1) * D],
                        scalar1=1.0 / cfg.cores, scalar2=None, op0=ALU.mult))
                scale_war[par] = dve.mark()

                @block.sync
                def _(s, tc=tc, par=par):
                    r = tc * 128
                    h1 = dmas(s, lambda r=r: s.dma_start(
                        out=x1_ds[beta][r:r + 128, :],
                        in_=x1buf[:, par * D:(par + 1) * D]),
                        waits=[(dve, scale_war[par])])
                    h2 = dmas(s, lambda r=r, par=par: s.dma_start(
                        out=opart_ds[beta][r:r + 128, :], in_=x18slot[par]))
                    x1war[par] = h1 + h2

            for tc in range(TC):
                par = tc % 2
                yo = par * D
                so = par * 8

                @block.sync
                def _(s, tc=tc, yo=yo, par=par):
                    w = [(ccx, arq[tc // 8])]
                    if scale_war[par]:
                        w.append((dve, scale_war[par]))
                    if tc < 2:
                        w += list(out_st_h[0])
                    dmas(s, lambda: s.dma_start(
                        out=ybuf[:, yo:yo + D],
                        in_=psum_d[xr + tc * 128:xr + (tc + 1) * 128, :]),
                        waits=w)
                xl = dmas.mark()

                @block.vector
                def _(v, yo=yo, so=so, par=par, xl=xl):
                    Y = ybuf[:, yo:yo + D]
                    xb = x1buf[:, par * D:(par + 1) * D]
                    dve(v, lambda: v.tensor_scalar(
                        out=xb, in0=Y, scalar1=1.0, scalar2=None,
                        op0=ALU.mult, op1=ALU.add,
                        accum_out=stat_f[:, so + 0:so + 1]),
                        waits=[xl] + list(x1war[par]))
                    dve(v, lambda: v.scalar_tensor_tensor(
                        out=xb, in0=Y, scalar=1.0, in1=Y,
                        op0=ALU.mult, op1=ALU.mult,
                        accum_out=stat_f[:, so + 1:so + 2]))
                    dve(v, lambda: v.tensor_scalar(
                        out=stat_f[:, so + 2:so + 3],
                        in0=stat_f[:, so + 0:so + 1],
                        scalar1=1.0 / D, scalar2=None, op0=ALU.mult))
                    dve(v, lambda: v.tensor_mul(
                        stat_f[:, so + 3:so + 4], stat_f[:, so + 2:so + 3],
                        stat_f[:, so + 2:so + 3]))
                    dve(v, lambda: v.scalar_tensor_tensor(
                        out=stat_f[:, so + 4:so + 5],
                        in0=stat_f[:, so + 1:so + 2], scalar=1.0 / D,
                        in1=stat_f[:, so + 3:so + 4],
                        op0=ALU.mult, op1=ALU.subtract))
                    dve(v, lambda: v.tensor_scalar(
                        out=stat_f[:, so + 4:so + 5],
                        in0=stat_f[:, so + 4:so + 5],
                        scalar1=1e-5, scalar2=None, op0=ALU.add))
                stats_m[tc] = dve.mark()
                if tc >= 1:
                    ln1_scale(tc - 1)
            ln1_scale(TC - 1)

            @block.scalar
            def _(s):
                for tc in range(TC):
                    so = (tc % 2) * 8
                    act(s, lambda so=so, tc=tc: s.activation(
                        out=stat_f[:, so + 5:so + 6],
                        in_=stat_f[:, so + 4:so + 5], func=AF.Sqrt),
                        waits=[(dve, stats_m[tc])], noser=True)
            assert act.count == ln_base + TC
            x1_dma = dmas.mark()
            prev_x1_dma = x1_dma

            if not ph("x1bf"):
                continue

            @block.gpsimd
            def _(g):
                dmag(g, lambda: g.dma_start(
                    out=x1bf_d[xr:xr + T, :], in_=x1_ds[beta][:, :]),
                    waits=[x1_dma])
            x1bf_done = dmag.mark()

            if not ph("gate"):
                continue
            x1fm_war = {0: 0}
            for dc in range(DC):
                par = 0
                fo = 0

                @block.sync
                def _(s, dc=dc, fo=fo, par=par):
                    w = [x1bf_done]
                    if x1fm_war[par]:
                        w.append((pe, x1fm_war[par]))
                    dmas(s, lambda: s.dma_start(
                        out=x1fm_buf[:, fo:fo + T],
                        in_=x1bf_d[xr:xr + T, dc * 128:(dc + 1) * 128],
                        transpose=True), waits=w)
                x1fm_ld = dmas.mark()

                for ft in range(T // 512):
                    bk = "A" if ft % 2 == 0 else "B"
                    dst = chA if ft % 2 == 0 else chB

                    @block.tensor
                    def _(t, dc=dc, ft=ft, fo=fo, dst=dst, bk=bk,
                          x1fm_ld=x1fm_ld):
                        w = [x1fm_ld] + chw(bk)
                        pe(t, lambda: t.matmul(
                            dst[0:NE, 0:512],
                            gatew_sb[:, dc * NE:(dc + 1) * NE],
                            x1fm_buf[:, fo + ft * 512:fo + (ft + 1) * 512],
                            start=True, stop=True), waits=w)
                    gmm = pe.mark()

                    @block.vector
                    def _(v, dc=dc, ft=ft, dst=dst, gmm=gmm):
                        if dc == 0:
                            dve(v, lambda: v.tensor_copy(
                                out=gacc[:, ft * 512:(ft + 1) * 512],
                                in_=dst[0:NE, 0:512]), waits=[(pe, gmm)])
                        else:
                            dve(v, lambda: v.tensor_add(
                                gacc[:, ft * 512:(ft + 1) * 512],
                                gacc[:, ft * 512:(ft + 1) * 512],
                                dst[0:NE, 0:512]), waits=[(pe, gmm)])
                    chfree[bk] = (dve, dve.mark())
                x1fm_war[par] = pe.mark()
            prev_gate_pe = pe.mark()
            gate_done = dve.mark()

            for tg in range(0, TC, 8):
                n = min(8, TC - tg)
                bk = "A" if (tg // 8) % 2 == 0 else "B"
                base = 0 if bk == "A" else 512

                @block.tensor
                def _(t, tg=tg, n=n, base=base, bk=bk):
                    w = [(dve, gate_done)] + chw(bk)
                    for i in range(n):
                        pe(t, lambda i=i: t.transpose(
                            chain[:, base + i * 8:base + i * 8 + NE],
                            gacc[:, (tg + i) * 128:(tg + i + 1) * 128],
                            ident_f[0:NE, 0:NE]), waits=w if i == 0 else ())
                gt = pe.mark()

                @block.vector
                def _(v, tg=tg, n=n, base=base, gt=gt):
                    for i in range(n):
                        dve(v, lambda i=i: v.tensor_copy(
                            out=gtok[:, (tg + i) * NE:(tg + i + 1) * NE],
                            in_=chain[:, base + i * 8:base + i * 8 + NE]),
                            waits=[(pe, gt)] if i == 0 else ())
                chfree[bk] = (dve, dve.mark())
            gtok_done = dve.mark()

            if not ph("sm"):
                continue
            @block.scalar
            def _(s):
                act(s, lambda: s.activation(out=probs_tok[:, :], in_=gtok[:, :],
                                            func=AF.Exp),
                    waits=[(dve, gtok_done)], noser=True)
            sm_exp = act.mark()

            @block.vector
            def _(v):
                pt3 = probs_tok[:, :].rearrange("p (tc e) -> p tc e", e=NE)
                dve(v, lambda: v.reduce_sum(ssum[:, :], pt3, axis=AX.X),
                    waits=[(act, sm_exp)])
                dve(v, lambda: v.reciprocal(ssum[:, :], ssum[:, :]))
                for e in range(NE):
                    dve(v, lambda e=e: v.tensor_mul(
                        pt3[:, :, e], pt3[:, :, e], ssum[:, :]))
                dve(v, lambda: v.memset(tau_lo[:, :], 0.0))
                dve(v, lambda: v.memset(tau_hi[:, :], 1.0))
            probs_done = dve.mark()

            def count_pass(tau_src, out_mask):
                @block.vector
                def _(v):
                    dve(v, lambda: v.tensor_scalar(
                        out=tau_m8[:, :], in0=ind_em[:, :],
                        scalar1=tau_src[0:NE, 0:1], scalar2=None, op0=ALU.mult))
                tm = dve.mark()

                @block.tensor
                def _(t, tm=tm):
                    w = [(dve, tm)] + chw("A")
                    pe(t, lambda: t.matmul(chA[:, 0:TC * NE], ones8_f[:, :],
                                           tau_m8[:, :], start=True, stop=True),
                       waits=w)
                taumm = pe.mark()

                @block.vector
                def _(v, taumm=taumm):
                    dve(v, lambda: v.tensor_tensor(
                        out=out_mask[:, :], in0=probs_tok[:, :],
                        in1=chA[:, 0:TC * NE], op=ALU.is_gt),
                        waits=[(pe, taumm)])
                    om3 = out_mask[:, :].rearrange("p (tc e) -> p e tc", e=NE)
                    dve(v, lambda: v.reduce_sum(part8[:, :], om3, axis=AX.X))
                chfree["A"] = (dve, dve.mark())
                pm = dve.mark()

                @block.tensor
                def _(t, pm=pm):
                    w = [(dve, pm)] + chw("B")
                    pe(t, lambda: t.matmul(chB[0:NE, 0:1], part8[:, :],
                                           onescol_f[:, 0:1],
                                           start=True, stop=True), waits=w)
                cmm = pe.mark()

                @block.vector
                def _(v, cmm=cmm):
                    dve(v, lambda: v.tensor_copy(out=cnt8[:, :],
                                                 in_=chB[0:NE, 0:1]),
                        waits=[(pe, cmm)])
                chfree["B"] = (dve, dve.mark())

            if not ph("bis"):
                continue
            TN = TC * NE
            cmp_x = x1buf[:, 0:3 * TN]
            tau_m8x = x1buf[0:NE, 4 * TN:7 * TN]
            part8x = x1buf[:, 7 * TN:7 * TN + 3 * NE]
            delta = tau_hi
            for it in range(18):
                @block.vector
                def _(v):
                    dve(v, lambda: v.tensor_scalar(
                        out=delta[:, :], in0=delta[:, :], scalar1=0.25,
                        scalar2=None, op0=ALU.mult))
                    dve(v, lambda: v.tensor_add(bs_sc[:, 0:1], tau_lo[:, :],
                                                delta[:, :]))
                    dve(v, lambda: v.tensor_add(bs_sc[:, 1:2], bs_sc[:, 0:1],
                                                delta[:, :]))
                    dve(v, lambda: v.tensor_add(bs_sc[:, 2:3], bs_sc[:, 1:2],
                                                delta[:, :]))
                    for j in range(3):
                        dve(v, lambda j=j: v.tensor_scalar(
                            out=tau_m8x[:, j * TN:(j + 1) * TN],
                            in0=ind_em[:, :], scalar1=bs_sc[:, j:j + 1],
                            scalar2=None, op0=ALU.mult))
                tm = dve.mark()

                @block.tensor
                def _(t, tm=tm):
                    w = [(dve, tm)] + chw("A") + chw("B")
                    pe(t, lambda: t.matmul(chA[:, 0:512], ones8_f[:, :],
                                           tau_m8x[:, 0:512],
                                           start=True, stop=True), waits=w)
                    pe(t, lambda: t.matmul(chB[:, 0:256], ones8_f[:, :],
                                           tau_m8x[:, 512:768],
                                           start=True, stop=True))
                taumm = pe.mark()

                @block.vector
                def _(v, taumm=taumm):
                    for j in range(3):
                        src = (chA[:, j * TN:(j + 1) * TN] if j < 2
                               else chB[:, 0:TN])
                        dve(v, lambda j=j, src=src: v.tensor_tensor(
                            out=cmp_x[:, j * TN:(j + 1) * TN],
                            in0=probs_tok[:, :], in1=src, op=ALU.is_gt),
                            waits=[(pe, taumm)] if j == 0 else ())
                    cm4 = cmp_x.rearrange("p (j tc e) -> p j e tc", e=NE,
                                          tc=TC)
                    dve(v, lambda: v.reduce_sum(part8x[:, :], cm4, axis=AX.X))
                pm = dve.mark()

                @block.tensor
                def _(t, pm=pm):
                    for j in range(3):
                        pe(t, lambda j=j: t.matmul(
                            chB[0:NE, 300 + j:301 + j],
                            part8x[:, j * NE:(j + 1) * NE],
                            onescol_f[:, 0:1], start=True, stop=True),
                           waits=[(dve, pm)] if j == 0 else ())
                cmm = pe.mark()

                @block.vector
                def _(v, cmm=cmm):
                    dve(v, lambda: v.tensor_scalar(
                        out=bs_sc[:, 4:7], in0=chB[0:NE, 300:303],
                        scalar1=float(K), scalar2=None, op0=ALU.is_ge),
                        waits=[(pe, cmm)])
                    dve(v, lambda: v.reduce_sum(
                        bs_sc[:, 7:8], bs_sc[:, 4:7], axis=AX.X))
                    dve(v, lambda: v.scalar_tensor_tensor(
                        out=tau_lo[:, :], in0=bs_sc[:, 7:8],
                        scalar=delta[:, 0:1], in1=tau_lo[:, :],
                        op0=ALU.mult, op1=ALU.add))
            @block.vector
            def _(v):
                dve(v, lambda: v.tensor_add(tau_hi[:, :], tau_lo[:, :],
                                            delta[:, :]))

            if not ph("mask"):
                continue
            count_pass(tau_hi, hi_tok)

            @block.vector
            def _(v):
                dve(v, lambda: v.tensor_scalar(
                    out=fill8[:, :], in0=cnt8[:, :], scalar1=-1.0,
                    scalar2=float(K), op0=ALU.mult, op1=ALU.add))
            count_pass(tau_lo, tie_tok)

            @block.vector
            def _(v):
                dve(v, lambda: v.tensor_sub(tie_tok[:, :], tie_tok[:, :],
                                            hi_tok[:, :]))
            masks_done = dve.mark()

            for tg in range(0, TC, 2):
                n = min(2, TC - tg)
                bk = "A" if (tg // 2) % 2 == 0 else "B"
                base = 0 if bk == "A" else 512

                @block.tensor
                def _(t, tg=tg, n=n, base=base, bk=bk):
                    w = [(dve, masks_done)] + chw(bk)
                    for i in range(n):
                        pe(t, lambda i=i: t.transpose(
                            chain[0:NE, base + (2 * i) * 128:
                                  base + (2 * i) * 128 + 128],
                            hi_tok[:, (tg + i) * NE:(tg + i + 1) * NE],
                            ident_f[:, :]), waits=w if i == 0 else ())
                        pe(t, lambda i=i: t.transpose(
                            chain[0:NE, base + (2 * i + 1) * 128:
                                  base + (2 * i + 1) * 128 + 128],
                            tie_tok[:, (tg + i) * NE:(tg + i + 1) * NE],
                            ident_f[:, :]))
                ht = pe.mark()

                @block.vector
                def _(v, tg=tg, n=n, base=base, ht=ht):
                    for i in range(n):
                        w = [(pe, ht)] + list(out_st_h[0])
                        dve(v, lambda i=i: v.tensor_copy(
                            out=hi_b[0:NE, (tg + i) * 128:(tg + i + 1) * 128],
                            in_=chain[0:NE, base + (2 * i) * 128:
                                      base + (2 * i) * 128 + 128]),
                            waits=w if i == 0 else ())
                        dve(v, lambda i=i: v.tensor_copy(
                            out=tie_b[0:NE, (tg + i) * 128:(tg + i + 1) * 128],
                            in_=chain[0:NE, base + (2 * i + 1) * 128:
                                      base + (2 * i + 1) * 128 + 128]))
                chfree[bk] = (dve, dve.mark())
            em_done = dve.mark()

            if not ph("sel"):
                continue
            @block.vector
            def _(v):
                HIe = hi_b[0:NE, :]
                TIe = tie_b[0:NE, :]
                RNK = rank_f[0:NE, :]
                dve(v, lambda: v.tensor_tensor_scan(
                    out=RNK, data0=TIe, data1=TIe, initial=0.0,
                    op0=ALU.add, op1=ALU.bypass), waits=[(dve, em_done)])
                dve(v, lambda: v.tensor_scalar(
                    out=RNK, in0=RNK,
                    scalar1=fill8[0:NE, 0:1], scalar2=None, op0=ALU.is_le))
                dve(v, lambda: v.tensor_mul(TIe, TIe, RNK))
                dve(v, lambda: v.tensor_add(TIe, TIe, HIe))
                dve(v, lambda: v.tensor_tensor_scan(
                    out=RNK, data0=TIe, data1=TIe, initial=0.0,
                    op0=ALU.add, op1=ALU.bypass))
                dve(v, lambda: v.tensor_scalar(
                    out=RNK, in0=RNK, scalar1=-1.0,
                    scalar2=None, op0=ALU.add))
            gs_done = dve.mark()

            for tg in range(0, TC, 8):
                n = min(8, TC - tg)
                bk = "A" if (tg // 8) % 2 == 0 else "B"
                base = 0 if bk == "A" else 512

                @block.tensor
                def _(t, tg=tg, n=n, base=base, bk=bk):
                    w = [(dve, gs_done)] + chw(bk)
                    for i in range(n):
                        pe(t, lambda i=i: t.transpose(
                            chain[:, base + i * 2:base + i * 2 + 1],
                            rank_f[0:1, (tg + i) * 128:(tg + i + 1) * 128],
                            ident_f[0:1, 0:1]), waits=w if i == 0 else ())
                        pe(t, lambda i=i: t.transpose(
                            chain_bf[:, 2 * (base + i * 2 + 1):
                                     2 * (base + i * 2 + 1) + 1],
                            tie_b[0:1, (tg + i) * 128:(tg + i + 1) * 128],
                            ident_b[0:1, 0:1]))
                gst = pe.mark()

                @block.vector
                def _(v, tg=tg, n=n, base=base, gst=gst):
                    for i in range(n):
                        dve(v, lambda i=i: v.tensor_copy(
                            out=gs_tok[:, (tg + i) * 2:(tg + i) * 2 + 1],
                            in_=chain[:, base + i * 2:base + i * 2 + 1]),
                            waits=[(pe, gst)] if i == 0 else ())
                        dve(v, lambda i=i: v.tensor_copy(
                            out=gs_tok[:, (tg + i) * 2 + 1:(tg + i) * 2 + 2],
                            in_=chain_bf[:, 2 * (base + i * 2 + 1):
                                         2 * (base + i * 2 + 1) + 1]))
                chfree[bk] = (dve, dve.mark())
            gstok_done = dve.mark()

            if not ph("ex"):
                continue
            if beta > 0:
                @block.gpsimd
                def _(g):
                    pool(g, lambda: g.iota(
                        iota_j[:, :], pattern=[[1, K]], base=0,
                        channel_multiplier=0,
                        allow_small_or_imprecise_dtypes=True),
                        waits=[(pe, prev_ffn_pe[0])])
            iota_m = pool.mark()
            ex_war = {0: 0, 1: 0}
            for tc in range(TC):
                epar = tc % 2
                po3 = epar * 3
                pcs = pc_b2[epar]

                @block.vector
                def _(v, tc=tc, epar=epar, po3=po3, pcs=pcs):
                    w = []
                    if ex_war[epar]:
                        w.append((pe, ex_war[epar]))
                    if tc == 0:
                        w.append((dve, gstok_done))
                        w.append((pool, iota_m))
                    dve(v, lambda: v.tensor_copy(out=payload[:, po3:po3 + 1],
                                                 in_=iota_colb[:, :]), waits=w)
                    dve(v, lambda: v.tensor_copy(out=payload[:, po3 + 1:po3 + 2],
                                                 in_=chunkv_b[:, tc:tc + 1]))
                    dve(v, lambda: v.tensor_copy(
                        out=payload[:, po3 + 2:po3 + 3],
                        in_=probs_tok[:, tc * NE:tc * NE + 1]))
                    dve(v, lambda: v.tensor_scalar(
                        out=pcs[:, :], in0=iota_j[:, :],
                        scalar1=gs_tok[:, 2 * tc:2 * tc + 1],
                        scalar2=gs_tok[:, 2 * tc + 1:2 * tc + 2],
                        op0=ALU.is_equal, op1=ALU.mult))
                pcm = dve.mark()

                @block.tensor
                def _(t, tc=tc, pcm=pcm, po3=po3, pcs=pcs):
                    w = [(dve, pcm)]
                    if tc == 0:
                        w += chw("A") + chw("B")
                    for kf in range((K + 511) // 512):
                        f0, f1 = kf * 512, min(K, kf * 512 + 512)
                        pe(t, lambda f0=f0, f1=f1: t.matmul(
                            chain[0:3, f0:f1], payload[:, po3:po3 + 3],
                            pcs[:, f0:f1],
                            start=(tc == 0), stop=(tc == TC - 1)),
                           waits=w if kf == 0 else ())
                ex_war[epar] = pe.mark()
            ex_mm = pe.mark()

            @block.vector
            def _(v):
                dve(v, lambda: v.tensor_copy(out=mg_f[:, :],
                                             in_=chain[0:3, 0:K]),
                    waits=[(pe, ex_mm)])
            chfree["A"] = (dve, dve.mark())
            chfree["B"] = (dve, dve.mark())
            mg_done = dve.mark()

            if not ph("mg"):
                continue
            prev_ffn = prev_ffn_dma if beta > 0 else []
            for ks in range(KC):
                bk = "A" if ks % 2 == 0 else "B"
                base = 0 if bk == "A" else 512

                @block.tensor
                def _(t, ks=ks, base=base, bk=bk):
                    w = [(dve, mg_done)] + chw(bk)
                    pe(t, lambda: t.transpose(
                        chain_bf[:, 2 * base:2 * base + 3],
                        mg_f[0:3, ks * 128:(ks + 1) * 128],
                        ident_b[0:3, 0:3]), waits=w)
                mt = pe.mark()

                @block.vector
                def _(v, ks=ks, base=base, mt=mt):
                    dve(v, lambda: v.tensor_copy(
                        out=mg3[:, 0:3], in_=chain_bf[:, 2 * base:2 * base + 3]),
                        waits=[(pe, mt)] + list(prev_ffn))
                    dve(v, lambda: v.scalar_tensor_tensor(
                        out=mcols[:, ks:ks + 1], in0=mg3[:, 1:2],
                        scalar=128.0, in1=mg3[:, 0:1],
                        op0=ALU.mult, op1=ALU.add))
                    dve(v, lambda: v.tensor_copy(out=gcol[:, ks:ks + 1],
                                                 in_=mg3[:, 2:3]))
                chfree[bk] = (dve, dve.mark())

            @block.vector
            def _(v):
                dve(v, lambda: v.tensor_copy(out=midx[:, :], in_=mcols[:, :]))
            midx_done = dve.mark()

            if not ph("ffn"):
                continue
            NG = K // 128
            NGH = NG // 2
            accpair = [sc1, accv]
            acc_war = {0: 0, 1: 0}
            scat_h = {0: [], 1: []}
            scat_mark = dmag.mark()
            xin_war = [0]
            mm2_last = 0
            for hf in range(2):
                xinT_done = 0
                for g in range(NGH):
                    par = g % 2
                    col = hf * NGH + g

                    @block.gpsimd
                    def _(gg, g=g, par=par, col=col):
                        w = [(dve, midx_done), x1_dma]
                        if xin_war[0]:
                            w.append((pe, xin_war[0]))
                        dmag(gg, lambda: gg.indirect_dma_start(
                            out=xin_sb[:, par * D:(par + 1) * D],
                            out_offset=None, in_=x1_ds[beta],
                            in_offset=IndirectOffsetOnAxis(
                                ap=midx[:, col:col + 1], axis=0)), waits=w)
                    gth = dmag.mark()

                    bk = "A" if par == 0 else "B"
                    slot = chAb if par == 0 else chBb

                    @block.tensor
                    def _(t, par=par, slot=slot, bk=bk, gth=gth):
                        w = [gth] + chw(bk)
                        if mm2_last:
                            w.append((pe, mm2_last))
                        for dc in range(DC):
                            pe(t, lambda dc=dc, par=par: t.transpose(
                                slot(dc),
                                xin_sb[:, par * D + dc * 128:
                                       par * D + (dc + 1) * 128],
                                ident_b[:, :]), waits=w if dc == 0 else ())
                    xt = pe.mark()
                    if g >= 1:
                        xin_war[0] = xt

                    @block.vector
                    def _(v, g=g, slot=slot, xt=xt):
                        for dc in range(DC):
                            dve(v, lambda dc=dc: v.tensor_copy(
                                out=xinT_sb[:, dc * 512 + g * 128:
                                            dc * 512 + (g + 1) * 128],
                                in_=slot(dc)),
                                waits=[(pe, xt)] if dc == 0 else ())
                    chfree[bk] = (dve, dve.mark())
                xinT_done = dve.mark()
                xin_war[0] = pe.mark()

                pe_base = pe.count
                w1_lds = []
                for hc in range(HC):
                    @block.gpsimd
                    def _(gg, hc=hc):
                        w = [setup_dmag] if (beta == 0 and hf == 0) else []
                        if hc >= 2:
                            w.append((pe, pe_base + (hc - 1) * DC))
                        dmag(gg, lambda hc=hc: gg.dma_start(
                            out=w1_buf[:, (hc % 2) * D:(hc % 2) * D + D],
                            in_=w1bf_d[:, hc * D:(hc + 1) * D]), waits=w)
                    w1_lds.append(dmag.mark())

                gelu_base = act.count
                mm1_m = []
                @block.tensor
                def _(t, xinT_done=xinT_done, gelu_base=gelu_base,
                      w1_lds=w1_lds):
                    for hc in range(HC):
                        par = hc % 2
                        w = [w1_lds[hc]]
                        if hc == 0:
                            w.append((dve, xinT_done))
                        if hc >= 2:
                            w.append((act, gelu_base + hc - 1))
                        for dc in range(DC):
                            pe(t, lambda hc=hc, par=par, dc=dc: t.matmul(
                                sc0[:, par * 512:(par + 1) * 512],
                                w1_buf[:, (hc % 2) * D + dc * 128:
                                       (hc % 2) * D + (dc + 1) * 128],
                                xinT_sb[:, dc * 512:(dc + 1) * 512],
                                start=(dc == 0), stop=(dc == DC - 1)),
                               waits=w if dc == 0 else ())
                        mm1_m.append(pe.mark())
                assert pe.count == pe_base + HC * DC

                @block.scalar
                def _(s, mm1_m=mm1_m):
                    for hc in range(HC):
                        par = hc % 2
                        act(s, lambda hc=hc, par=par: s.activation(
                            out=hT_sb[:, hc * 512:(hc + 1) * 512],
                            in_=sc0[:, par * 512:(par + 1) * 512],
                            func=AF.Gelu_apprx_tanh),
                            waits=[(pe, mm1_m[hc])], noser=True)
                assert act.count == gelu_base + HC
                gelu_mk = [gelu_base + hc + 1 for hc in range(HC)]

                for g in range(NGH):
                    col = hf * NGH + g
                    par = g % 2
                    acc = accpair[par]

                    @block.tensor
                    def _(t, g=g, par=par, acc=acc, gelu_mk=gelu_mk):
                        for hc in range(HC):
                            w = [(act, gelu_mk[HC - 1])]
                            if hc == 0 and acc_war[par]:
                                w.append((dve, acc_war[par]))
                            if hc == 0 and hf == 0 and g < 2:
                                w.append((dve, at_evac_prev[0]))
                            for dh in range(2):
                                pe(t, lambda hc=hc, g=g, dh=dh, acc=acc:
                                   t.matmul(
                                    acc[:, dh * 512:(dh + 1) * 512],
                                    hT_sb[:, hc * 512 + g * 128:
                                          hc * 512 + (g + 1) * 128],
                                    w2_sb[:, hc * D + dh * 512:
                                          hc * D + dh * 512 + 512],
                                    start=(hc == 0), stop=(hc == HC - 1)),
                                   waits=w if (hc == 0 and dh == 0) else ())
                    fmm = pe.mark()
                    mm2_last = fmm

                    @block.vector
                    def _(v, col=col, acc=acc, fmm=fmm, g=g):
                        dve(v, lambda: v.tensor_scalar(
                            out=fw2[g % 2][:, :], in0=acc[:, :],
                            scalar1=gcol[:, col:col + 1],
                            scalar2=None, op0=ALU.mult),
                            waits=[(pe, fmm)] + list(scat_h[g % 2]))
                    acc_war[par] = dve.mark()
                    fw_done = dve.mark()

                    @block.gpsimd
                    def _(gg, col=col, fw_done=fw_done, scat_mark=scat_mark,
                          g=g):
                        h = dmag(gg, lambda: gg.indirect_dma_start(
                            out=opart_ds[beta],
                            out_offset=IndirectOffsetOnAxis(
                                ap=midx[:, col:col + 1], axis=0),
                            in_=fw2[g % 2][:, :], in_offset=None,
                            compute_op=ALU.add),
                            waits=[(dve, fw_done), x1_dma, scat_mark])
                        scat_h[g % 2] = h
                    scat_mark = dmag.mark()

            at_evac_prev[0] = acc_war[1]
            ffn_acc_last[0] = max(acc_war[0], acc_war[1])
            prev_gelu_m[0] = act.count
            prev_ffn_pe[0] = pe.mark()
            prev_ffn_dma = dmag.mark()
            if not ph("rs"):
                continue

            @block.gpsimd
            def _(g, scat_mark=scat_mark):
                dmag.wait(g, scat_mark)
                ccx(g, lambda: g.collective_compute(
                    "ReduceScatter", ALU.add, replica_groups=groups,
                    ins=[opart_ds[beta][:, :]],
                    outs=[ored_d[beta * OUTR:(beta + 1) * OUTR, :]]))
            rs_done = ccx.mark()

            if not ph("ln2"):
                continue

            def emit_ln2(beta=beta, rs_done=rs_done):
                ln2_body(beta, rs_done)

            if beta == NB - 1:
                emit_ln2()
            else:
                pending_ln2[0] = emit_ln2

        if dbg_tensors:
            dbg_map = {"ppart": ppart_d, "psumd": psum_d, "xbf": xbf_d,
                       "x1bf": x1bf_d, "ored": ored_d}
            for b in range(NB):
                dbg_map[f"x1tok{b}"] = x1_ds[b]
                dbg_map[f"opart{b}"] = opart_ds[b]

            @block.gpsimd
            def _(g):
                dmas.wait(g)
                dmag.wait(g)
                ccx.wait(g)
                for nm in dbg_tensors:
                    src = dbg_map[nm]
                    dst = nc.dram_tensor(
                        "dbg_" + nm, list(src.shape),
                        src.dtype, kind="ExternalOutput").ap()
                    dmag(g, lambda src=src, dst=dst: g.dma_start(
                        out=dst, in_=src))
                dmag.wait(g)

        @block.sync
        def _(s):
            dmas.wait(s)

        @block.gpsimd
        def _(g):
            dmag.wait(g)
            ccx.wait(g)
            ln2st.wait(g)

    return nc


def make_inmaps(cfg: Cfg, x, Wqkv, Wproj, gate_w, w1, w2, lq1, lk1, lq2, lk2,
                rms_scale):
    T, D, NB, NE, HC = cfg.T, cfg.D, cfg.NB, cfg.NE, cfg.HC
    xf = np.ascontiguousarray(np.asarray(x).reshape(NB * T, D), dtype=np.float32)
    xts = np.ascontiguousarray(
        xf.reshape(NB * T // cfg.TB, cfg.TB, cfg.DC, 128)
        .transpose(3, 0, 2, 1).reshape(128, NB * T * cfg.DC),
        dtype=np.float32)
    lamv = np.ascontiguousarray(
        np.concatenate([lq1, lq2, lk1, lk2])[None, :], dtype=np.float32)
    rmsrow = np.ascontiguousarray(
        np.concatenate([rms_scale, rms_scale])[None, :], dtype=np.float32)
    in_maps = []
    for c in range(cfg.cores):
        A, B = 2 * c, 2 * c + 1
        qcols = []
        for hh in (A, B):
            qcols += list(range(32 * hh, 32 * hh + 32))
            qcols += list(range(D // 2 + 32 * hh, D // 2 + 32 * hh + 32))
        kcols = [D + cc for cc in qcols]
        vcols = []
        for hh in (A, B):
            vcols += list(range(2 * D + 64 * hh, 2 * D + 64 * hh + 64))
        wq = np.ascontiguousarray(Wqkv[:, qcols + kcols + vcols], dtype=np.float32)
        wp = np.ascontiguousarray(Wproj[128 * c:128 * (c + 1), :], dtype=np.float32)
        perm = [c] + [e for e in range(NE) if e != c]
        gw = np.ascontiguousarray(gate_w[:, perm], dtype=np.float32)
        w1c = np.ascontiguousarray(
            np.asarray(w1)[c].reshape(cfg.DC, 128, HC, 128)
            .transpose(1, 2, 0, 3).reshape(128, HC * cfg.DC * 128),
            dtype=np.float32)
        w2c = np.ascontiguousarray(np.asarray(w2)[c], dtype=np.float32)
        in_maps.append({
            "x": xf, "xts": xts, "wqkv": wq, "wproj": wp, "gatew": gw,
            "w1": w1c, "w2": w2c, "lamv": lamv, "rmsrow": rmsrow,
        })
    return in_maps


def assemble(cfg: Cfg, results):
    out = np.empty((cfg.NB, cfg.T, cfg.D), dtype=np.float32)
    for c in range(cfg.cores):
        r = np.asarray(results[c]["out"]).reshape(cfg.NB, cfg.OUTR, cfg.D)
        out[:, c * cfg.OUTR:(c + 1) * cfg.OUTR, :] = r
    return out


def kernel(L0_attn_x, L0_attn_Wqkv, L0_attn_Wproj, L0_attn_lambda_q1,
           L0_attn_lambda_k1, L0_attn_lambda_q2, L0_attn_lambda_k2,
           L0_attn_rms_scale, L0_ffn_gate_w, L0_ffn_w1, L0_ffn_w2):
    cfg = FULL
    in_maps = make_inmaps(
        cfg, np.asarray(L0_attn_x), np.asarray(L0_attn_Wqkv),
        np.asarray(L0_attn_Wproj), np.asarray(L0_ffn_gate_w),
        np.asarray(L0_ffn_w1), np.asarray(L0_ffn_w2),
        np.asarray(L0_attn_lambda_q1), np.asarray(L0_attn_lambda_k1),
        np.asarray(L0_attn_lambda_q2), np.asarray(L0_attn_lambda_k2),
        np.asarray(L0_attn_rms_scale))
    nc = build_nc(cfg)
    res = run_bass_kernel_spmd(nc, in_maps, core_ids=list(range(cfg.cores)))
    return assemble(cfg, res.results)
